# version 08
# baseline (speedup 1.0000x reference)
import numpy as np
import ml_dtypes
import concourse.bass as bass
import concourse.mybir as mybir
from concourse.bass_utils import run_bass_kernel_spmd

F32 = mybir.dt.float32
BF16 = mybir.dt.bfloat16
AF = mybir.ActivationFunctionType
ALU = mybir.AluOpType
AX = mybir.AxisListType

B, S, D = 4, 4096, 1024
NCH = D // 128
TOK = 2048
NEG = -1e30
LN_EPS = 1e-5


def fm(x):
    r, d = x.shape
    return np.ascontiguousarray(x.T.reshape(d // 128, 128, r))

def unfm(x):
    n, _, r = x.shape
    return np.ascontiguousarray(x.reshape(n * 128, r).T)

def wchunks(w):
    din, dout = w.shape
    return np.ascontiguousarray(w.reshape(din // 128, 128, dout))


class ArView:
    def __init__(self, ap):
        self._ap = ap

    def ap(self):
        return self._ap

    def __getitem__(self, idx):
        return self._ap[idx]


class Arena:
    def __init__(self, p, name, nbytes):
        self.base = p.nc.alloc_sbuf_tensor(name, [128, nbytes // 4], mybir.dt.float32)
        self.nbytes = nbytes
        self.off = 0

    def reset(self):
        self.off = 0

    def take(self, shape, dtype):
        rows, cols = shape
        es = mybir.dt.size(dtype)
        off = (self.off + 31) & ~31
        need = cols * es
        assert off + need <= self.nbytes, (
            "arena overflow: %d + %d > %d" % (off, need, self.nbytes))
        self.off = off + need
        ap = self.base.ap()
        if dtype != mybir.dt.float32:
            ap = ap.bitcast(dtype)
        return ArView(ap[0:rows, off // es: off // es + cols])


GLOBAL_SEMS = {"ser", "scser", "xld", "xws", "xpl", "xpl0", "xpl0v", "xrc",
               "hal", "hmk", "phg", "phsy", "sto", "oat", "st", "sth", "stf",
               "stx", "stu", "std", "stz", "stb", "syst", "sout"}
SEM_POOL_N = 70


class CntMap:
    def __init__(self, prog):
        self.p = prog
        self.d = {}

    def get(self, k, dflt=0):
        return self.d.get(self.p._csem(k), dflt)

    def __getitem__(self, k):
        return self.d[self.p._csem(k)]

    def __setitem__(self, k, v):
        self.d[self.p._csem(k)] = v

    def __contains__(self, k):
        return self.p._csem(k) in self.d


class Prog:
    def __init__(self, name="prog"):
        self.nc = bass.Bass("TRN2", target_bir_lowering=False, name=name)
        self.cnt = CntMap(self)
        self.sems = {}
        self.canon = {}
        self.slot_rr = 0
        self.arena = None

    def _csem(self, name):
        if name in GLOBAL_SEMS or name.startswith("zp"):
            return name
        s = self.canon.get(name)
        if s is None:
            s = "zp%d" % (self.slot_rr % SEM_POOL_N)
            self.slot_rr += 1
            self.canon[name] = s
        return s

    def new_phase(self):
        self.canon = {}

    def sem(self, name):
        cn = self._csem(name)
        if cn not in self.sems:
            self.sems[cn] = self.nc.alloc_semaphore(cn)
            if cn not in self.cnt:
                self.cnt[name] = 0
        return self.sems[cn]

    def inc(self, instr, name, k=1):
        s = self.sem(name)
        instr.then_inc(s, k)
        self.cnt[name] += k
        return self.cnt[name]

    def wait(self, eng, name, val=None):
        eng.wait_ge(self.sem(name), self.cnt[name] if val is None else val)

    def semtick(self, eng, name, k=1):
        eng.sem_inc(self.sem(name), k)
        self.cnt[name] = self.cnt.get(name, 0) + k
        return self.cnt[name]

    def serial(self, eng, instr, tag="ser"):
        n = self.inc(instr, tag)
        eng.wait_ge(self.sem(tag), n)

    def sb(self, name, shape, dtype):
        ar = getattr(self, "arena", None)
        if ar is not None:
            return ar.take(shape, dtype)
        return self.nc.alloc_sbuf_tensor(name, list(shape), dtype)

    def set_arena(self, ar):
        self.arena = ar

    def ps(self, name, shape, dtype=F32):
        return self.nc.alloc_psum_tensor(name, list(shape), dtype)

    def inp(self, name, shape, dtype=F32):
        return self.nc.dram_tensor(name, list(shape), dtype, kind="ExternalInput")

    def outp(self, name, shape, dtype=F32):
        return self.nc.dram_tensor(name, list(shape), dtype, kind="ExternalOutput")

    def scratch(self, name, shape, dtype=F32):
        return self.nc.dram_tensor(name, list(shape), dtype, kind="Internal")

    def setup_consts(self, vals=()):
        self.cvals = list(vals)
        self.ccol = {}
        for k, v in enumerate(self.cvals):
            self.ccol[v] = self.sb("c_cc%d" % k, (128, 1), F32)
        self.ones_col = self.sb("c_ones_col", (128, 1), F32)
        self.ones_colb = self.sb("c_ones_colb", (128, 1), BF16)
        self.ones_row = self.sb("c_ones_row", (1, 128), F32)
        self.ones_rowb = self.sb("c_ones_rowb", (1, 128), BF16)
        self.identb = self.sb("c_identb", (128, 128), BF16)
        self.identf = self.sb("c_identf", (128, 128), F32)

    def emit_consts(self, g):
        for v, t in self.ccol.items():
            g.memset(t[:, :], float(v))
        g.memset(self.ones_col[:, :], 1.0)
        g.memset(self.ones_colb[:, :], 1.0)
        g.memset(self.ones_row[:, :], 1.0)
        g.memset(self.ones_rowb[:, :], 1.0)
        g.memset(self.identb[:, :], 1.0)
        i = g.memset(self.identf[:, :], 1.0)
        self.serial(g, i)
        g.affine_select(self.identb[:, :], self.identb[:, :], [[-1, 128]], ALU.is_equal,
                        0.0, base=0, channel_multiplier=1)
        i = g.affine_select(self.identf[:, :], self.identf[:, :], [[-1, 128]], ALU.is_equal,
                        0.0, base=0, channel_multiplier=1)
        self.serial(g, i)


def emit_ln(p, env, src, dst, TT, gcol=None, bcol=None, dstb=None):
    nc = p.nc
    ntc = TT // 128
    b = {nm: p.cnt.get(nm, 0) for nm in
         ("lnsq", "lnst", "lnv", "lns", "lnr", "lnt", "lnrw", "lnbc", "lnap")}
    with nc.Block() as blk:
        @blk.scalar
        def _(sc):
            if b["lnap"]:
                sc.wait_ge(p.sem("lnap"), b["lnap"])
            for c in range(NCH):
                i = sc.activation(env["xsq"][:, c * TT:(c + 1) * TT], src(c), AF.Square)
            p.inc(i, "lnsq")
            sc.wait_ge(p.sem("lnv"), b["lnv"] + 1)
            i = sc.activation(env["scol"][:, 0:ntc], env["vcol"][:, 0:ntc], AF.Sqrt,
                              bias=p.ccol[LN_EPS][:, 0:1])
            p.inc(i, "lns")
        @blk.tensor
        def _(t):
            t.wait_ge(p.sem("lnsq"), b["lnsq"] + 1)
            for tc in range(ntc):
                for c in range(NCH):
                    t.matmul(env["ps_stat"][:, 2 * tc:2 * tc + 1],
                             src(c)[:, tc * 128:(tc + 1) * 128],
                             p.ones_col[:, :], start=(c == 0), stop=(c == NCH - 1))
                for c in range(NCH):
                    i = t.matmul(env["ps_stat"][:, 2 * tc + 1:2 * tc + 2],
                                 env["xsq"][:, c * TT + tc * 128: c * TT + (tc + 1) * 128],
                                 p.ones_col[:, :], start=(c == 0), stop=(c == NCH - 1))
            p.inc(i, "lnst")
            t.wait_ge(p.sem("lnr"), b["lnr"] + 1)
            for tc in range(ntc):
                t.transpose(env["ps_tra"][:, tc * 128:(tc + 1) * 128],
                            env["rcol"][:, tc:tc + 1], p.identf[:, :])
                i = t.transpose(env["ps_trb"][:, tc * 128:(tc + 1) * 128],
                                env["ncol"][:, tc:tc + 1], p.identf[:, :])
            p.inc(i, "lnt")
            t.wait_ge(p.sem("lnrw"), b["lnrw"] + 1)
            t.matmul(env["ps_bca"][:, 0:TT], p.ones_row[:, :], env["rows_a"][:, 0:TT],
                     start=True, stop=True)
            i = t.matmul(env["ps_bcb"][:, 0:TT], p.ones_row[:, :], env["rows_b"][:, 0:TT],
                         start=True, stop=True)
            p.inc(i, "lnbc")
        @blk.vector
        def _(v):
            v.wait_ge(p.sem("lnst"), b["lnst"] + 1)
            stp = env["ps_stat"]
            v.tensor_scalar(env["mcol"][:, 0:ntc], stp[:, 0:2 * ntc:2], 1.0 / D, None, ALU.mult)
            i = v.tensor_scalar(env["qcol"][:, 0:ntc], stp[:, 1:2 * ntc:2], 1.0 / D, None, ALU.mult)
            p.serial(v, i)
            i = v.tensor_mul(env["tcol"][:, 0:ntc], env["mcol"][:, 0:ntc], env["mcol"][:, 0:ntc])
            p.serial(v, i)
            i = v.tensor_sub(env["vcol"][:, 0:ntc], env["qcol"][:, 0:ntc], env["tcol"][:, 0:ntc])
            p.inc(i, "lnv")
            v.wait_ge(p.sem("lns"), b["lns"] + 1)
            i = v.reciprocal(env["rcol"][:, 0:ntc], env["scol"][:, 0:ntc])
            p.serial(v, i)
            i = v.tensor_mul(env["tcol"][:, 0:ntc], env["mcol"][:, 0:ntc], env["rcol"][:, 0:ntc])
            p.serial(v, i)
            i = v.tensor_scalar(env["ncol"][:, 0:ntc], env["tcol"][:, 0:ntc], -1.0, None, ALU.mult)
            p.inc(i, "lnr")
            v.wait_ge(p.sem("lnt"), b["lnt"] + 1)
            v.tensor_copy(env["rows_a"][:, 0:TT], env["ps_tra"][0:1, 0:TT])
            i = v.tensor_copy(env["rows_b"][:, 0:TT], env["ps_trb"][0:1, 0:TT])
            p.inc(i, "lnrw")
            v.wait_ge(p.sem("lnbc"), b["lnbc"] + 1)
            for c in range(NCH):
                i = v.tensor_mul(env["xsq"][:, c * TT:(c + 1) * TT], src(c), env["ps_bca"][:, 0:TT])
            p.serial(v, i)
            if gcol is None:
                for c in range(NCH):
                    i = v.tensor_add(dst(c), env["xsq"][:, c * TT:(c + 1) * TT], env["ps_bcb"][:, 0:TT])
            else:
                for c in range(NCH):
                    i = v.tensor_add(env["xsq"][:, c * TT:(c + 1) * TT],
                                     env["xsq"][:, c * TT:(c + 1) * TT], env["ps_bcb"][:, 0:TT])
                p.serial(v, i)
                for c in range(NCH):
                    i = v.tensor_scalar(dst(c), env["xsq"][:, c * TT:(c + 1) * TT],
                                        gcol(c), bcol(c), ALU.mult, ALU.add)
            if dstb is not None:
                p.serial(v, i)
                for c in range(NCH):
                    i = v.tensor_copy(dstb(c), dst(c))
            p.inc(i, "lnap")


def make_ln_env(p, TT=512, alloc_psum=True, pref=""):
    env = {}
    ntc = TT // 128
    env["xsq"] = p.sb(pref + "ln_xsq", (128, NCH * TT), F32)
    for nm in ("mcol", "qcol", "tcol", "vcol", "scol", "rcol", "ncol"):
        env[nm] = p.sb(pref + "ln_" + nm, (128, ntc), F32)
    env["rows_a"] = p.sb(pref + "ln_rows_a", (1, TT), F32)
    env["rows_b"] = p.sb(pref + "ln_rows_b", (1, TT), F32)
    if alloc_psum:
        env["ps_stat"] = p.ps("lnp_stat", (128, 2 * ntc))
        env["ps_tra"] = p.ps("lnp_tra", (1, TT))
        env["ps_trb"] = p.ps("lnp_trb", (1, TT))
        env["ps_bca"] = p.ps("lnp_bca", (128, TT))
        env["ps_bcb"] = p.ps("lnp_bcb", (128, TT))
    return env


def build_proj_res_ln(name, nin, abf):
    p = Prog(name)
    x = p.inp("x", (NCH, 128, TOK))
    a_in = p.inp("act", (nin, 128, TOK), BF16 if abf else F32)
    w = p.inp("w", (nin, 128, D))
    xo = p.outp("xo", (NCH, 128, TOK))
    z = p.outp("z", (NCH, 128, TOK))
    p.setup_consts(vals=(LN_EPS,))
    env = make_ln_env(p, 512)
    ps_o = p.ps("ps_o", (128, 512))
    emit_projresln(p, env, ps_o, x, a_in, w, xo, z, nin, abf, consts=True)
    return p


def emit_projresln(p, env, ps_o, x, a_in, w, xo, z, nin, abf, consts=False, pref="pr"):
    nc = p.nc
    TT = 512
    NT = TOK // TT
    x_sb = p.sb(pref + "x_sb", (128, NCH * TT), F32)
    a_sb = p.sb(pref + "a_sb", (128, nin * TT), BF16)
    af_sb = p.sb(pref + "af_sb", (128, nin * TT), F32) if not abf else None
    w_sb = p.sb(pref + "w_sb", (128, nin * D), BF16)
    xo_sb = p.sb(pref + "xo_sb", (128, NCH * TT), F32)
    z_sb = p.sb(pref + "z_sb", (128, NCH * TT), F32)

    with nc.Block() as b:
        @b.gpsimd
        def _(g):
            if consts:
                p.emit_consts(g)
            i = g.dma_start(w_sb.ap().rearrange("p (c n) -> p c n", c=nin),
                            w.ap().transpose([1, 0, 2]))
            n = p.inc(i, "ldw", 16)
            g.wait_ge(p.sem("ldw"), n)

    for it in range(NT):
        t0 = it * TT
        with nc.Block() as b:
            @b.sync
            def _(s, t0=t0):
                i = s.dma_start(x_sb.ap().rearrange("p (c n) -> p c n", c=NCH),
                                x.ap()[:, :, t0:t0 + TT].transpose([1, 0, 2]))
                p.inc(i, "ldx", 16)
                i = s.dma_start((a_sb if abf else af_sb).ap().rearrange(
                    "p (c n) -> p c n", c=nin),
                    a_in.ap()[:, :, t0:t0 + TT].transpose([1, 0, 2]))
                p.inc(i, "ldx", 16)
                p.wait(s, "ldx")
        if not abf:
            with nc.Block() as b:
                @b.vector
                def _(v):
                    v.tensor_copy(a_sb[:, 0:nin * TT], af_sb[:, 0:nin * TT])
        mm0 = p.cnt.get("g1mm", 0)
        f0 = p.cnt.get("g1f", 0)
        with nc.Block() as b:
            @b.tensor
            def _(t):
                for oc in range(NCH):
                    if oc >= 1:
                        t.wait_ge(p.sem("g1f"), f0 + oc)
                    for ic in range(nin):
                        i = t.matmul(ps_o[:, :],
                                     w_sb[:, ic * D + oc * 128: ic * D + oc * 128 + 128],
                                     a_sb[:, ic * TT:(ic + 1) * TT],
                                     start=(ic == 0), stop=(ic == nin - 1))
                    p.inc(i, "g1mm")
            @b.vector
            def _(v):
                for oc in range(NCH):
                    v.wait_ge(p.sem("g1mm"), mm0 + oc + 1)
                    i = v.tensor_add(xo_sb[:, oc * TT:(oc + 1) * TT],
                                     x_sb[:, oc * TT:(oc + 1) * TT], ps_o[:, :])
                    p.inc(i, "g1f")
        emit_ln(p, env, lambda c: xo_sb[:, c * TT:(c + 1) * TT],
                lambda c: z_sb[:, c * TT:(c + 1) * TT], TT)
        with nc.Block() as b:
            @b.sync
            def _(s, t0=t0):
                i = s.dma_start(xo.ap()[:, :, t0:t0 + TT].transpose([1, 0, 2]),
                                xo_sb.ap().rearrange("p (c n) -> p c n", c=NCH))
                p.inc(i, "st", 16)
                i = s.dma_start(z.ap()[:, :, t0:t0 + TT].transpose([1, 0, 2]),
                                z_sb.ap().rearrange("p (c n) -> p c n", c=NCH))
                p.inc(i, "st", 16)
                p.wait(s, "st")


HGN = 44
HP2 = 22
FFNR = 2176

def build_layer(name, W, H, dbg_max_win=None, dbg_ffn=True, dbg_stage=99, dbg_s4=""):
    HALO_W = 2 if W == 512 else 3
    R = HALO_W * W + 2048
    NWIN = R // W
    INNER = H * 64
    NI = INNER // 128
    QKW = 3 * INNER
    p = Prog(name)
    io = {}
    io["h"] = p.inp("h", (NCH, 128, R + 1))
    io["wqkv"] = p.inp("wqkv", (NCH, 128, QKW))
    io["wo"] = p.inp("wo", (NI, 128, D))
    io["lncol"] = p.inp("lncol", (128, 2 * NCH))
    io["qskcol"] = p.inp("qskcol", (64, 2))
    io["tabs"] = p.inp("tabs", (64, 4 * R))
    io["wkill"] = p.inp("wkill", (128, NWIN))
    io["w1p"] = p.inp("w1p", (NCH, 128, HGN * 128))
    io["b1acol"] = p.inp("b1acol", (128, HGN))
    io["b1gcol"] = p.inp("b1gcol", (128, HGN))
    io["w2p"] = p.inp("w2p", (HP2, 128, D))
    io["b2col"] = p.inp("b2col", (128, NCH))
    io["hout"] = p.outp("hout", (NCH, 128, 2048))
    io["hmid_dr"] = p.scratch("hmid_dr", (NCH, 128, R))
    io["hhat_dr"] = p.scratch("hhat_dr", (NCH, 128, R), BF16)
    p.setup_consts(vals=(LN_EPS, 1e-12))
    PS = p.ps("PS", (128, 2048))
    ps_a = p.ps("ps_a", (128, 512))
    ps_b = p.ps("ps_b", (128, 512))
    ps_c = p.ps("ps_c", (128, 512))
    ps_t = p.ps("ps_t", (128, 1024), BF16)
    pv = {"PS": PS, "ps_a": ps_a, "ps_b": ps_b, "ps_c": ps_c, "ps_t": ps_t}
    emit_layer(p, pv, W, H, io, consts=True, dbg_max_win=dbg_max_win,
               dbg_ffn=dbg_ffn, dbg_stage=dbg_stage, dbg_s4=dbg_s4)
    return p


def emit_layer(p, pv, W, H, io, consts=False, dbg_max_win=None, dbg_ffn=True,
               dbg_stage=99, dbg_s4=""):
    INNER = H * 64
    NI = INNER // 128
    HALO_W = 2 if W == 512 else 3
    R = HALO_W * W + 2048
    NWIN = R // W
    QW0 = HALO_W - 1
    F0 = R - FFNR
    NTC = W // 128
    BW = W + 128
    QKW = 3 * INNER
    nc = p.nc
    h_in = io["h"]
    wqkv = io["wqkv"]
    wo = io["wo"]
    lncol = io["lncol"]
    qskcol = io["qskcol"]
    tabs = io["tabs"]
    wkill = io["wkill"]
    w1p = io["w1p"]
    b1acol = io["b1acol"]
    b1gcol = io["b1gcol"]
    w2p = io["w2p"]
    b2col = io["b2col"]
    hout = io["hout"]
    hmid_dr = io["hmid_dr"]
    hhat_dr = io["hhat_dr"]
    PS = pv["PS"]
    ps_a, ps_b, ps_c, ps_t = pv["ps_a"], pv["ps_b"], pv["ps_c"], pv["ps_t"]
    env = make_ln_env(p, 512, alloc_psum=False, pref="ly")
    xsq_b = env["xsq"].ap().bitcast(BF16)

    PH2_F32 = NCH * HGN * 128 // 2 + HP2 * D // 2
    ph1 = [("wq", NCH * QKW // 2), ("wo", NI * D // 2), ("tabbf", 2 * R),
           ("qbuf", H * W // 2), ("kr0", H * W // 2), ("kr1", H * W // 2),
           ("vr0", NTC * INNER // 2), ("vr1", NTC * INNER // 2),
           ("rota", H * W // 2)]
    need = sum(n for _, n in ph1)
    base2 = max(PH2_F32, need)
    EXT = [("hmt", NCH * 256), ("hht", NCH * 256 // 2), ("gel", 512), ("act", HP2 * 128)]
    AR_F32 = base2 + sum(n for _, n in EXT)
    arena = p.sb("arena", (128, AR_F32), F32)
    arb = arena.ap().bitcast(BF16)
    offs = {}
    o = 0
    for nm, n in ph1:
        offs[nm] = o
        o += n
    def bfv(nm, ncols, rows=128):
        ofs = 2 * offs[nm]
        return arb[0:rows, ofs:ofs + ncols]
    wq_sb = bfv("wq", NCH * QKW)
    wo_sb = bfv("wo", NI * D)
    tab_bf = bfv("tabbf", 4 * R, 64)
    qbuf = bfv("qbuf", H * W, 64)
    kring = [bfv("kr0", H * W, 64), bfv("kr1", H * W, 64)]
    vring = [bfv("vr0", NTC * INNER), bfv("vr1", NTC * INNER)]
    rot_a = bfv("rota", H * W, 64)
    rot_b = xsq_b[0:64, H * W:2 * H * W]
    w1_sb = arb[:, 0:NCH * HGN * 128]
    w2_sb = arb[:, NCH * HGN * 128: NCH * HGN * 128 + HP2 * D]

    lng = p.sb("lng_sb", (128, 2 * NCH), F32)
    qsk = p.sb("qsk_sb", (64, 2), F32)
    wk_sb = p.sb("wk_sb", (128, NWIN), F32)
    b1a_sb = p.sb("b1a_sb", (128, HGN), F32)
    b1g_sb = p.sb("b1g_sb", (128, HGN), F32)
    b2_sb = p.sb("b2_sb", (128, NCH), F32)
    mask_lo = p.sb("mask_lo", (128, 128), F32)
    mask_hi = p.sb("mask_hi", (128, 128), F32)
    gap0 = sum(n for _, n in ph1)
    if PH2_F32 - gap0 >= 2 * (NCH * (W + 1)):
        hbuf = arena.ap()[:, gap0:gap0 + NCH * (W + 1)]
        hmid = arena.ap()[:, gap0 + NCH * (W + 1):gap0 + 2 * NCH * (W + 1)]
        gap0 += 2 * (NCH * (W + 1))
    else:
        hbuf = p.sb("hbuf", (128, NCH * (W + 1)), F32).ap()
        hmid = p.sb("hmid", (128, NCH * (W + 1)), F32).ap()
    o_sb = p.sb("o_sb", (128, INNER), BF16).ap()
    of_sb = p.sb("of_sb", (128, INNER), BF16).ap()
    if PH2_F32 - gap0 >= NCH * W // 2:
        sh_bf = arb[:, 2 * gap0:2 * gap0 + NCH * W]
    else:
        sh_bf = p.sb("sh_bf", (128, NCH * W), BF16).ap()
    hhat_b = p.sb("hhat_b", (128, NCH * W), BF16)
    rr = p.sb("rr", (128, 2 * H * NTC), F32)
    rrb = p.sb("rrb", (128, 2 * H * NTC), BF16)
    rbc_sb = p.sb("rbc_sb", (1, 256), BF16)
    p_sb = p.sb("p_sb", (128, BW), BF16)
    pt_sb = p.sb("pt_sb", (128, BW), BF16)
    l_col = p.sb("l_col", (128, 64), F32)
    eo = base2
    eoffs = {}
    for nm, n in EXT:
        eoffs[nm] = eo
        eo += n
    hmt = arena.ap()[:, eoffs["hmt"]:eoffs["hmt"] + NCH * 256]
    hht = arb[:, 2 * eoffs["hht"]:2 * eoffs["hht"] + NCH * 256]
    gel = arena.ap()[0:64, eoffs["gel"]:eoffs["gel"] + 512]
    act_sb = arb[0:128, 2 * eoffs["act"]:2 * eoffs["act"] + HP2 * 256]

    env["ps_stat"] = ps_c[:, 0:8]
    env["ps_tra"] = ps_a[0:1, 0:512]
    env["ps_trb"] = ps_b[0:1, 0:512]
    env["ps_bca"] = PS[:, 1024:1536]
    env["ps_bcb"] = PS[:, 1536:2048]

    with nc.Block() as blk:
        @blk.gpsimd
        def _(g):
            if consts:
                p.emit_consts(g)
            g.memset(hmid[:, 0:NCH * (W + 1)], 0.0)
            g.memset(mask_lo[:, :], 0.0)
            i = g.memset(mask_hi[:, :], 0.0)
            p.serial(g, i)
            g.affine_select(mask_lo[:, :], mask_lo[:, :], [[1, 128]], ALU.is_ge, NEG,
                            base=0, channel_multiplier=-1)
            i = g.affine_select(mask_hi[:, :], mask_hi[:, :], [[-1, 128]], ALU.is_ge, NEG,
                            base=0, channel_multiplier=1)
            p.serial(g, i)
            i = g.dma_start(wq_sb.rearrange("p (c n) -> p c n", c=NCH),
                            wqkv.ap().transpose([1, 0, 2]))
            p.inc(i, "ldg", 16)
            i = g.dma_start(wo_sb.rearrange("p (c n) -> p c n", c=NI),
                            wo.ap().transpose([1, 0, 2]))
            p.inc(i, "ldg", 16)
        @blk.sync
        def _(s):
            for ap_out, ap_in in ((lng.ap(), lncol.ap()),
                                  (qsk.ap(), qskcol.ap()), (wk_sb.ap(), wkill.ap()),
                                  (b1a_sb.ap(), b1acol.ap()), (b1g_sb.ap(), b1gcol.ap()), (b2_sb.ap(), b2col.ap())):
                i = s.dma_start(ap_out, ap_in)
                p.inc(i, "ld", 16)
            p.wait(s, "ld")
        @blk.gpsimd
        def _(g):
            p.wait(g, "ldg")
    CH_T = 2048
    for part in range(2):
        for c0 in range(0, 2 * R, CH_T):
            cw = min(CH_T, 2 * R - c0)
            with nc.Block() as blk:
                @blk.sync
                def _(s, part=part, c0=c0, cw=cw):
                    i = s.dma_start(env["xsq"][0:64, 0:cw],
                                    tabs.ap()[:, part * 2 * R + c0: part * 2 * R + c0 + cw])
                    p.inc(i, "ldt", 16)
                    p.wait(s, "ldt")
            with nc.Block() as blk:
                @blk.vector
                def _(v, part=part, c0=c0, cw=cw):
                    i = v.tensor_scalar(env["xsq"][0:64, 0:cw], env["xsq"][0:64, 0:cw],
                                        qsk[:, part:part + 1], None, ALU.mult)
                    p.serial(v, i)
                    v.tensor_copy(tab_bf[0:64, part * 2 * R + c0: part * 2 * R + c0 + cw],
                                  env["xsq"][0:64, 0:cw])

    for w in range(NWIN if dbg_max_win is None else min(NWIN, dbg_max_win)):
        cur, prev = w % 2, (w - 1) % 2
        r0 = w * W
        with nc.Block() as blk:
            @blk.sync
            def _(s, r0=r0):
                i = s.dma_start(hbuf.rearrange("p (c n) -> p c n", c=NCH),
                                h_in.ap()[:, :, r0:r0 + W + 1].transpose([1, 0, 2]))
                p.inc(i, "ldh", 16)
                p.wait(s, "ldh")
        with nc.Block() as blk:
            @blk.vector
            def _(v):
                for c in range(NCH):
                    off = 1 if c < NCH // 2 else 0
                    v.tensor_copy(sh_bf[:, c * W:(c + 1) * W],
                                  hbuf[:, c * (W + 1) + off: c * (W + 1) + off + W])
        if dbg_stage < 2:
            continue
        with nc.Block() as blk:
            @blk.tensor
            def _(t, w=w, cur=cur):
                mm0 = p.cnt.get("qev", 0)
                for cg in range(2 * NI):
                    pp = (ps_a, ps_b)[cg % 2]
                    if cg >= 2:
                        t.wait_ge(p.sem("qev"), mm0 + cg - 1)
                    for ic in range(NCH):
                        i = t.matmul(pp[:, 0:W], wq_sb[:, ic * QKW + cg * 128: ic * QKW + cg * 128 + 128],
                                     sh_bf[:, ic * W:(ic + 1) * W],
                                     start=(ic == 0), stop=(ic == NCH - 1))
                    p.inc(i, "qmm")
                for tc in range(NTC):
                    for vg in range(INNER // 512):
                        pp = (ps_a, ps_b)[(tc * (INNER // 512) + vg) % 2]
                        t.wait_ge(p.sem("qev"), mm0 + 2 * NI + tc * (INNER // 512) + vg - 1)
                        for ic in range(NCH):
                            i = t.matmul(pp[:, 0:512],
                                         sh_bf[:, ic * W + tc * 128: ic * W + (tc + 1) * 128],
                                         wq_sb[:, ic * QKW + 2 * INNER + vg * 512: ic * QKW + 2 * INNER + (vg + 1) * 512],
                                         start=(ic == 0), stop=(ic == NCH - 1))
                        p.inc(i, "qmm")
            @blk.vector
            def _(v, w=w, cur=cur):
                mm0 = p.cnt.get("qmm", 0) - (2 * NI + NTC * (INNER // 512))
                for cg in range(2 * NI):
                    pp = (ps_a, ps_b)[cg % 2]
                    v.wait_ge(p.sem("qmm"), mm0 + cg + 1)
                    dstb = qbuf if cg < NI else kring[cur]
                    hh = (cg % NI) * 2
                    v.tensor_copy(dstb[0:64, hh * W: hh * W + W], pp[0:64, 0:W])
                    i = v.tensor_copy(dstb[0:64, (hh + 1) * W: (hh + 2) * W], pp[64:128, 0:W])
                    p.inc(i, "qev")
                for k in range(NTC * (INNER // 512)):
                    pp = (ps_a, ps_b)[k % 2]
                    tc, vg = divmod(k, INNER // 512)
                    v.wait_ge(p.sem("qmm"), mm0 + 2 * NI + k + 1)
                    i = v.tensor_copy(vring[cur][:, tc * INNER + vg * 512: tc * INNER + (vg + 1) * 512],
                                      pp[:, 0:512])
                    p.inc(i, "qev")
        if dbg_stage < 3:
            continue
        with nc.Block() as blk:
            @blk.scalar
            def _(sc):
                sc.activation(xsq_b[0:64, 0:H * W], qbuf[0:64, :], AF.Square)
                i = sc.activation(xsq_b[0:64, H * W:2 * H * W], kring[cur][0:64, :], AF.Square)
                p.inc(i, "rsq")
            @blk.tensor
            def _(t):
                p.wait(t, "rsq")
                for tc in range(NTC):
                    for h in range(H):
                        t.matmul(ps_c[:, tc * 2 * H + h: tc * 2 * H + h + 1],
                                 xsq_b[0:64, h * W + tc * 128: h * W + (tc + 1) * 128],
                                 p.ones_colb[0:64, :], start=True, stop=True)
                        i = t.matmul(ps_c[:, tc * 2 * H + H + h: tc * 2 * H + H + h + 1],
                                     xsq_b[0:64, H * W + h * W + tc * 128: H * W + h * W + (tc + 1) * 128],
                                     p.ones_colb[0:64, :], start=True, stop=True)
                p.inc(i, "rss")
            @blk.scalar
            def _(sc):
                p.wait(sc, "rss")
                i = sc.activation(rr[:, :], ps_c[:, 0:2 * H * NTC], AF.Sqrt)
                p.inc(i, "rsr")
            @blk.vector
            def _(v):
                p.wait(v, "rsr")
                i = v.tensor_scalar(rr[:, :], rr[:, :], p.ccol[1e-12][:, 0:1], None, ALU.max)
                p.serial(v, i)
                i = v.reciprocal(rr[:, :], rr[:, :])
                p.serial(v, i)
                i = v.tensor_copy(rrb[:, :], rr[:, :])
                p.serial(v, i)
                for tc in range(NTC):
                    i = v.tensor_scalar(rr[:, tc * 2 * H: tc * 2 * H + H],
                                        rr[:, tc * 2 * H: tc * 2 * H + H], 8.0, None, ALU.mult)
                p.inc(i, "rrd")
        if dbg_stage < 4:
            continue
        with nc.Block() as blk:
            @blk.vector
            def _(v, r0=r0):
                swp = xsq_b[0:64, 0:H * W]
                for (buf, tof) in ((qbuf, 0), (kring[cur], 2 * R)):
                    cs = tab_bf[0:64, tof + r0:tof + r0 + W].unsqueeze(1).broadcast_to([64, H, W])
                    sn = tab_bf[0:64, tof + R + r0:tof + R + r0 + W].unsqueeze(1).broadcast_to([64, H, W])
                    bv = buf[0:64, :].rearrange("p (h n) -> p h n", h=H)
                    v.tensor_tensor(rot_a[0:64, :].rearrange("p (h n) -> p h n", h=H), bv, cs, ALU.mult)
                    i = v.tensor_tensor(rot_b[0:64, :].rearrange("p (h n) -> p h n", h=H), bv, sn, ALU.mult)
                    p.serial(v, i)
                    v.tensor_copy(swp[0:32, :], rot_b[32:64, :])
                    i = v.tensor_copy(swp[32:64, :], rot_b[0:32, :])
                    p.serial(v, i)
                    i = v.tensor_add(buf[0:64, :], rot_a[0:64, :], swp[0:64, :])
                    p.serial(v, i)
        if dbg_stage < 5:
            continue
        with nc.Block() as blk:
            c_rbc = p.cnt.get("rbc", 0)
            c_rb1 = p.cnt.get("rb1", 0)
            c_rb2 = p.cnt.get("rb2", 0)
            c_rbe = p.cnt.get("rbe", 0)
            @blk.tensor
            def _(t, cur=cur, c_rb1=c_rb1, c_rbe=c_rbe):
                n = 0
                for tc in range(NTC):
                    for h in range(H):
                        if n >= 2:
                            t.wait_ge(p.sem("rb1"), c_rb1 + n - 1)
                        c0 = tc * 2 * H + H + h
                        i = t.transpose(ps_t[0:1, (n % 2) * 128:(n % 2) * 128 + 128],
                                        rrb[:, c0:c0 + 1], p.identb[:, :])
                        p.inc(i, "rbc")
                        t.wait_ge(p.sem("rb1"), c_rb1 + n + 1)
                        if n >= 2:
                            t.wait_ge(p.sem("rbe"), c_rbe + n - 1)
                        i = t.matmul(ps_c[0:64, (n % 2) * 128:(n % 2) * 128 + 128],
                                     p.ones_rowb[0:1, 0:64],
                                     rbc_sb[0:1, (n % 2) * 128:(n % 2) * 128 + 128],
                                     start=True, stop=True)
                        p.inc(i, "rb2")
                        n += 1
            @blk.vector
            def _(v, cur=cur, c_rbc=c_rbc, c_rb2=c_rb2):
                n = 0
                for tc in range(NTC):
                    for h in range(H):
                        v.wait_ge(p.sem("rbc"), c_rbc + n + 1)
                        i = v.tensor_copy(rbc_sb[0:1, (n % 2) * 128:(n % 2) * 128 + 128],
                                          ps_t[0:1, (n % 2) * 128:(n % 2) * 128 + 128])
                        p.inc(i, "rb1")
                        v.wait_ge(p.sem("rb2"), c_rb2 + n + 1)
                        i = v.tensor_mul(kring[cur][0:64, h * W + tc * 128: h * W + (tc + 1) * 128],
                                         kring[cur][0:64, h * W + tc * 128: h * W + (tc + 1) * 128],
                                         ps_c[0:64, (n % 2) * 128:(n % 2) * 128 + 128])
                        p.inc(i, "rbe")
                        n += 1
        if w < QW0 or dbg_stage < 6:
            continue
        for s_ in range(NTC):
            P1W = W - 128 * s_
            c_smm = p.cnt.get("smm", 0)
            c_sbias = p.cnt.get("sbias", 0)
            c_sexp = p.cnt.get("sexp", 0)
            c_ptr = p.cnt.get("ptr", 0)
            c_ptev = p.cnt.get("ptev", 0)
            c_pvm = p.cnt.get("pvm", 0)
            c_aev = p.cnt.get("aev", 0)
            with nc.Block() as blk:
                @blk.tensor
                def _(t, w=w, s_=s_, cur=cur, prev=prev, P1W=P1W,
                      c_sexp=c_sexp, c_ptev=c_ptev, c_aev=c_aev):
                    for h in range(H + 1):
                        if h < H:
                            if h >= 1:
                                t.wait_ge(p.sem("sexp"), c_sexp + h)
                            q_ap = qbuf[0:64, h * W + s_ * 128: h * W + (s_ + 1) * 128]
                            segs = []
                            for a0, b0, ring, roff in ((0, P1W, prev, h * W + 128 * s_),
                                                       (P1W, BW, cur, h * W - P1W)):
                                while a0 < b0:
                                    m = min(b0, (a0 // 512 + 1) * 512)
                                    segs.append((a0, m, ring, roff))
                                    a0 = m
                            for a0, b0, ring, roff in segs:
                                i = t.matmul(PS[:, a0:b0], q_ap,
                                             kring[ring][0:64, roff + a0: roff + b0],
                                             start=True, stop=True)
                            p.inc(i, "smm")
                        if h >= 1:
                            hh = h - 1
                            t.wait_ge(p.sem("sexp"), c_sexp + hh + 1)
                            if hh >= 1:
                                t.wait_ge(p.sem("ptev"), c_ptev + hh)
                            if "t" not in dbg_s4:
                                for k in range(BW // 128):
                                    i = t.transpose(ps_t[:, k * 128:(k + 1) * 128],
                                                    p_sb[:, k * 128:(k + 1) * 128], p.identb[:, :])
                            else:
                                i = t.transpose(ps_t[:, 0:128], p_sb[:, 0:128], p.identb[:, :])
                            p.inc(i, "ptr")
                            t.wait_ge(p.sem("ptev"), c_ptev + hh + 1)
                            if hh >= 1:
                                t.wait_ge(p.sem("aev"), c_aev + hh)
                            if "p" not in dbg_s4:
                                for k in range(BW // 128):
                                    tkc = (s_ + k) % NTC
                                    slot = prev if k < NTC - s_ else cur
                                    i = t.matmul(ps_c[:, 0:64], pt_sb[:, k * 128:(k + 1) * 128],
                                                 vring[slot][:, tkc * INNER + h * 64 - 64: tkc * INNER + h * 64],
                                                 start=(k == 0), stop=(k == BW // 128 - 1))
                            else:
                                i = t.matmul(ps_c[:, 0:64], pt_sb[:, 0:128],
                                             vring[cur][:, 0:64], start=True, stop=True)
                            p.inc(i, "pvm")
                @blk.vector
                def _(v, w=w, s_=s_, P1W=P1W, c_smm=c_smm, c_ptr=c_ptr, c_pvm=c_pvm):
                    for h in range(H + 1):
                        if h < H:
                            v.wait_ge(p.sem("smm"), c_smm + h + 1)
                            if "m" not in dbg_s4:
                                i = v.tensor_scalar(PS[:, 0:P1W], PS[:, 0:P1W], wk_sb[:, w:w + 1], None, ALU.add)
                                p.serial(v, i)
                                v.tensor_add(PS[:, 0:128], PS[:, 0:128], mask_lo[:, :])
                                i = v.tensor_add(PS[:, BW - 128:BW], PS[:, BW - 128:BW], mask_hi[:, :])
                                p.serial(v, i)
                            if "r" not in dbg_s4:
                                i = v.reduce_max(l_col[:, 0:1], PS[:, 0:BW], AX.X, negate=True)
                                p.serial(v, i)
                                i = v.tensor_scalar(l_col[:, 8 + h:9 + h], l_col[:, 0:1],
                                                    rr[:, s_ * 2 * H + h: s_ * 2 * H + h + 1], None, ALU.mult)
                            else:
                                i = v.tensor_copy(l_col[:, 8 + h:9 + h], l_col[:, 0:1])
                            p.inc(i, "sbias")
                        if h >= 1:
                            hh = h - 1
                            if hh >= 1:
                                v.wait_ge(p.sem("pvm"), c_pvm + hh)
                            v.wait_ge(p.sem("ptr"), c_ptr + hh + 1)
                            if "v" not in dbg_s4:
                                i = v.tensor_copy(pt_sb[:, 0:BW], ps_t[:, 0:BW])
                            else:
                                i = v.tensor_copy(pt_sb[:, 0:128], ps_t[:, 0:128])
                            p.inc(i, "ptev")
                            if "c" not in dbg_s4:
                                i = v.reciprocal(l_col[:, 40 + hh:41 + hh], l_col[:, 24 + hh:25 + hh])
                                p.serial(v, i)
                            v.wait_ge(p.sem("pvm"), c_pvm + hh + 1)
                            if "o" not in dbg_s4:
                                i = v.tensor_scalar(o_sb[:, hh * 64:(hh + 1) * 64], ps_c[:, 0:64],
                                                    l_col[:, 40 + hh:41 + hh], None, ALU.mult)
                            else:
                                i = v.tensor_copy(l_col[:, 40 + hh:41 + hh], l_col[:, 24 + hh:25 + hh])
                            p.inc(i, "aev")
                @blk.scalar
                def _(sc, s_=s_, c_sbias=c_sbias, c_ptr=c_ptr):
                    for h in range(H):
                        sc.wait_ge(p.sem("sbias"), c_sbias + h + 1)
                        if h >= 1:
                            sc.wait_ge(p.sem("ptr"), c_ptr + h)
                        if "e" not in dbg_s4:
                            i = sc.activation(p_sb[:, 0:BW], PS[:, 0:BW], AF.Exp,
                                              bias=l_col[:, 8 + h:9 + h],
                                              scale=rr[:, s_ * 2 * H + h: s_ * 2 * H + h + 1],
                                              accum_out=l_col[:, 24 + h:25 + h])
                        else:
                            i = sc.copy(p_sb[:, 0:BW], PS[:, 0:BW])
                        p.inc(i, "sexp")
            if dbg_stage < 7 or "Z" in dbg_s4:
                continue
            c_otr = p.cnt.get("otr", 0)
            c_oev = p.cnt.get("oev", 0)
            if "X" in dbg_s4:
                with nc.Block() as blk:
                    @blk.tensor
                    def _(t, c_oev=c_oev):
                        for j in range(NI):
                            i = t.transpose(ps_t[:, j * 128:(j + 1) * 128],
                                            o_sb[:, j * 128:(j + 1) * 128], p.identb[:, :])
                            p.inc(i, "otr")
                        if "2" in dbg_s4:
                            t.wait_ge(p.sem("oev"), c_oev + NI)
                            for oc in range(NCH):
                                pp = (ps_a, ps_b)[oc % 2]
                                for ic in range(NI):
                                    i = t.matmul(pp[:, 0:128],
                                                 wo_sb[:, ic * D + oc * 128: ic * D + oc * 128 + 128],
                                                 of_sb[:, ic * 128:(ic + 1) * 128],
                                                 start=(ic == 0), stop=(ic == NI - 1))
                                p.inc(i, "wom")
                    @blk.vector
                    def _(v, c_otr=c_otr):
                        v.wait_ge(p.sem("otr"), c_otr + NI)
                        i = v.tensor_copy(of_sb[:, 0:INNER], ps_t[:, 0:INNER])
                        p.inc(i, "oev", NI)
                continue
            with nc.Block() as blk:
                @blk.tensor
                def _(t):
                    for j in range(NI):
                        i = t.transpose(ps_t[:, j * 128:(j + 1) * 128],
                                        o_sb[:, j * 128:(j + 1) * 128], p.identb[:, :])
                    p.inc(i, "otr")
                @blk.vector
                def _(v, c_otr=c_otr):
                    v.wait_ge(p.sem("otr"), c_otr + 1)
                    i = v.tensor_copy(of_sb[:, 0:INNER], ps_t[:, 0:INNER])
                    p.inc(i, "oev")
                @blk.tensor
                def _(t, wf0=p.cnt.get("wof", 0)):
                    if "W" in dbg_s4:
                        i = t.matmul(ps_a[:, 0:128], wo_sb[:, 0:128], of_sb[:, 0:128],
                                     start=True, stop=True)
                        p.inc(i, "wom", NCH)
                        return
                    p.wait(t, "oev")
                    for oc in range(NCH):
                        pp = (ps_a, ps_b)[oc % 2]
                        if oc >= 2:
                            t.wait_ge(p.sem("wof"), wf0 + oc - 1)
                        for ic in range(NI):
                            i = t.matmul(pp[:, 0:128], wo_sb[:, ic * D + oc * 128: ic * D + oc * 128 + 128],
                                         of_sb[:, ic * 128:(ic + 1) * 128],
                                         start=(ic == 0), stop=(ic == NI - 1))
                        p.inc(i, "wom")
                @blk.vector
                def _(v, s_=s_):
                    wm0 = p.cnt.get("wom", 0) - NCH
                    if "W" in dbg_s4 or "A" in dbg_s4:
                        v.wait_ge(p.sem("wom"), wm0 + NCH)
                        i = v.tensor_copy(l_col[:, 0:1], ps_a[:, 0:1])
                        p.inc(i, "wof", NCH)
                        return
                    for oc in range(NCH):
                        pp = (ps_a, ps_b)[oc % 2]
                        v.wait_ge(p.sem("wom"), wm0 + oc + 1)
                        i = v.tensor_add(hmid[:, oc * (W + 1) + s_ * 128 + 1: oc * (W + 1) + (s_ + 1) * 128 + 1],
                                         hbuf[:, oc * (W + 1) + s_ * 128 + 1: oc * (W + 1) + (s_ + 1) * 128 + 1],
                                         pp[:, 0:128])
                        p.inc(i, "wof")
        if dbg_stage < 8:
            continue
        def srcf(c, W=W):
            off = 1 if c < NCH // 2 else 0
            return hmid[:, c * (W + 1) + off: c * (W + 1) + off + W]
        emit_ln(p, env, srcf,
                (lambda c, W=W: hhat_b[:, c * W:(c + 1) * W]), W,
                gcol=(lambda c: lng[:, c:c + 1]),
                bcol=(lambda c: lng[:, NCH + c:NCH + c + 1]))
        with nc.Block() as blk:
            @blk.sync
            def _(s, r0=r0, w=w):
                i = s.dma_start(hmid_dr.ap()[:, :, r0:r0 + W].transpose([1, 0, 2]),
                                hmid.rearrange("p (c n) -> p c n", c=NCH)[:, :, 1:W + 1])
                p.inc(i, "sth", 16)
                i = s.dma_start(hhat_dr.ap()[:, :, r0:r0 + W].transpose([1, 0, 2]),
                                hhat_b.ap().rearrange("p (c n) -> p c n", c=NCH))
                p.inc(i, "sth", 16)
                p.wait(s, "sth")
            @blk.vector
            def _(v, W=W):
                for c in range(NCH):
                    v.tensor_copy(hmid[:, c * (W + 1):c * (W + 1) + 1],
                                  hmid[:, c * (W + 1) + W: c * (W + 1) + W + 1])

    if not dbg_ffn:
        return
    with nc.Block() as blk:
        @blk.gpsimd
        def _(g):
            i = g.dma_start(w1_sb.rearrange("p (c n) -> p c n", c=NCH),
                            w1p.ap().transpose([1, 0, 2]))
            p.inc(i, "lw1", 16)
            i = g.dma_start(w2_sb.rearrange("p (c n) -> p c n", c=HP2),
                            w2p.ap().transpose([1, 0, 2]))
            n = p.inc(i, "lw1", 16)
            g.wait_ge(p.sem("lw1"), n)
    NT2 = (FFNR + 255) // 256
    for it in range(NT2):
        TT2 = min(256, FFNR - it * 256)
        f0 = F0 + it * 256
        with nc.Block() as blk:
            @blk.sync
            def _(s, f0=f0, TT2=TT2):
                i = s.dma_start(hht.rearrange("p (c n) -> p c n", c=NCH)[:, :, 0:TT2],
                                hhat_dr.ap()[:, :, f0:f0 + TT2].transpose([1, 0, 2]))
                p.inc(i, "lff", 16)
                i = s.dma_start(hmt.rearrange("p (c n) -> p c n", c=NCH)[:, :, 0:TT2],
                                hmid_dr.ap()[:, :, f0:f0 + TT2].transpose([1, 0, 2]))
                p.inc(i, "lff", 16)
                p.wait(s, "lff")
        c_gev = p.cnt.get("gev", 0)
        c_w2e = p.cnt.get("w2e", 0)
        with nc.Block() as blk:
            @blk.tensor
            def _(t, TT2=TT2, c_gev=c_gev, c_w2e=c_w2e):
                for q in range(HP2):
                    if q >= 1:
                        t.wait_ge(p.sem("gev"), c_gev + q)
                    for half, pp in ((0, ps_a), (1, ps_b)):
                        cc = 2 * q + half
                        for ic in range(NCH):
                            i = t.matmul(pp[:, 0:TT2],
                                         w1_sb[:, ic * HGN * 128 + cc * 128: ic * HGN * 128 + (cc + 1) * 128],
                                         hht[:, ic * 256: ic * 256 + TT2],
                                         start=(ic == 0), stop=(ic == NCH - 1))
                        p.inc(i, "hgm")
                t.wait_ge(p.sem("gev"), c_gev + HP2)
                for ocp in range(2):
                    if ocp == 1:
                        t.wait_ge(p.sem("w2e"), c_w2e + 4)
                    for oc4 in range(4):
                        oc = ocp * 4 + oc4
                        for q in range(HP2):
                            i = t.matmul(PS[:, oc4 * 512: oc4 * 512 + TT2],
                                         w2_sb[:, q * D + oc * 128: q * D + (oc + 1) * 128],
                                         act_sb[:, q * 256: q * 256 + TT2],
                                         start=(q == 0), stop=(q == HP2 - 1))
                        p.inc(i, "w2m")
            @blk.scalar
            def _(sc, TT2=TT2):
                hg0 = p.cnt.get("hgm", 0) - 2 * HP2
                gv0 = c_gev
                for q in range(HP2):
                    for half, pp in ((0, ps_a), (1, ps_b)):
                        cc = 2 * q + half
                        sc.wait_ge(p.sem("hgm"), hg0 + 2 * q + half + 1)
                        if q >= 1:
                            sc.wait_ge(p.sem("gev"), gv0 + q)
                        i = sc.activation(gel[0:64, half * 256:half * 256 + TT2],
                                          pp[64:128, 0:TT2], AF.Gelu,
                                          bias=b1g_sb[64:128, cc:cc + 1])
                        p.inc(i, "gelu")
            @blk.vector
            def _(v, TT2=TT2, c_w2e=c_w2e):
                gl0 = p.cnt.get("gelu", 0) - 2 * HP2
                w20 = p.cnt.get("w2m", 0) - 8
                for q in range(HP2):
                    for half, pp in ((0, ps_a), (1, ps_b)):
                        cc = 2 * q + half
                        v.wait_ge(p.sem("gelu"), gl0 + 2 * q + half + 1)
                        i = v.scalar_tensor_tensor(act_sb[64 * half:64 * (half + 1), q * 256: q * 256 + TT2],
                                                   pp[0:64, 0:TT2], b1a_sb[0:64, cc:cc + 1],
                                                   gel[0:64, half * 256:half * 256 + TT2],
                                                   ALU.add, ALU.mult)
                    p.inc(i, "gev")
                for ocp in range(2):
                    for oc4 in range(4):
                        oc = ocp * 4 + oc4
                        v.wait_ge(p.sem("w2m"), w20 + ocp * 4 + oc4 + 1)
                        i = v.scalar_tensor_tensor(env["xsq"][:, oc * 256: oc * 256 + TT2],
                                                   PS[:, oc4 * 512: oc4 * 512 + TT2],
                                                   b2_sb[:, oc:oc + 1],
                                                   hmt[:, oc * 256: oc * 256 + TT2],
                                                   ALU.add, ALU.add)
                        p.inc(i, "w2e")
        with nc.Block() as blk:
            @blk.sync
            def _(s, it=it, f0=f0, TT2=TT2):
                own0 = R - 2048
                lo = max(f0, own0)
                if lo < f0 + TT2:
                    i = s.dma_start(hout.ap()[:, :, lo - own0: f0 + TT2 - own0].transpose([1, 0, 2]),
                                    env["xsq"].ap()[:, 0:NCH * 256].rearrange(
                                        "p (c n) -> p c n", c=NCH)[:, :, lo - f0: TT2])
                    p.inc(i, "stf", 16)
                    p.wait(s, "stf")


def layer_tables(W, own0):
    HALO_W = 2 if W == 512 else 3
    R = HALO_W * W + 2048
    pos = np.arange(own0 - HALO_W * W, own0 + 2048, dtype=np.float64)
    half = np.arange(0, 64, 2, dtype=np.float64)
    inv_freq = 1.0 / 10000.0 ** (half / 64.0)
    fr = pos[:, None] * inv_freq[None, :]
    base = (half + 0.4 * 64) / (1.4 * 64)
    power = (pos - 4096 // 2) / (W // 2)
    sc = base[None, :] ** power[:, None]
    cos, sin = np.cos(fr), np.sin(fr)
    pad = pos < 0
    qc = np.concatenate([cos * sc, cos * sc], 1)
    qs_ = np.concatenate([sin * sc, -sin * sc], 1)
    kc = np.concatenate([cos / sc, cos / sc], 1)
    ks_ = np.concatenate([sin / sc, -sin / sc], 1)
    for a_ in (qc, kc):
        a_[pad] = 1.0
    for a_ in (qs_, ks_):
        a_[pad] = 0.0
    tabs = np.concatenate([qc.T, qs_.T, kc.T, ks_.T], 1).astype(np.float32)
    return tabs


def layer_weights(W, H, wqkv, qs, ks, wo, lg, lb, w1, b1, w2, b2):
    INNER = H * 64
    HID = 2730
    HIDP = HP2 * 128
    d = {}
    d["wqkv"] = wchunks(wqkv.astype(np.float32))
    d["wo"] = wchunks(wo.astype(np.float32))
    d["lncol"] = np.concatenate([lg.reshape(NCH, 128).T, lb.reshape(NCH, 128).T], 1).astype(np.float32)
    qsk = np.zeros((64, 2), np.float32)
    qsk[:, 0] = qs; qsk[:, 1] = ks
    d["qskcol"] = qsk
    a = np.zeros((D, HIDP), np.float32); a[:, :HID] = w1[:, :HID]
    g = np.zeros((D, HIDP), np.float32); g[:, :HID] = w1[:, HID:2 * HID]
    w1p = np.zeros((D, HGN * 128), np.float32)
    for cc in range(HGN):
        w1p[:, 128 * cc:128 * cc + 64] = a[:, 64 * cc:64 * cc + 64]
        w1p[:, 128 * cc + 64:128 * cc + 128] = g[:, 64 * cc:64 * cc + 64]
    d["w1p"] = wchunks(w1p)
    b1a = np.zeros(HIDP, np.float32); b1a[:HID] = b1[:HID]
    b1g = np.zeros(HIDP, np.float32); b1g[:HID] = b1[HID:2 * HID]
    bac = np.zeros((128, HGN), np.float32)
    bgc = np.zeros((128, HGN), np.float32)
    for cc in range(HGN):
        bac[0:64, cc] = b1a[64 * cc:64 * cc + 64]; bac[64:128, cc] = bac[0:64, cc]
        bgc[0:64, cc] = b1g[64 * cc:64 * cc + 64]; bgc[64:128, cc] = bgc[0:64, cc]
    d["b1acol"] = bac; d["b1gcol"] = bgc
    w2p = np.zeros((HIDP, D), np.float32); w2p[:HID] = w2
    d["w2p"] = wchunks(w2p)
    d["b2col"] = b2.reshape(NCH, 128).T.astype(np.float32)
    return d


def layer_core_inputs(W, h_full, own0, first):
    HALO_W = 2 if W == 512 else 3
    R = HALO_W * W + 2048
    NWIN = R // W
    lo = own0 - HALO_W * W - 1
    hs = np.zeros((R + 1, D), np.float32)
    src_lo = max(lo, 0)
    hs[src_lo - lo: R + 1] = h_full[src_lo: own0 + 2048]
    wk = np.zeros((128, NWIN), np.float32)
    if first:
        wk[:, HALO_W] = NEG
    return {"h": fm(hs), "tabs": layer_tables(W, own0), "wkill": wk}


def build_flash(name="flash"):
    p = Prog(name)
    x = p.inp("x", (NCH, 128, S))
    wq = p.inp("wq", (NCH, 128, 3 * 512))
    O = p.outp("O", (S // 128, 128, 512), BF16)
    p.setup_consts(vals=(LN_EPS, 0.125, -0.125))
    PS = p.ps("PS", (128, 512))
    ps_a = p.ps("ps_a", (128, 512))
    ps_b = p.ps("ps_b", (128, 512))
    ps_c = p.ps("ps_c", (128, 512))
    ps_t = p.ps("ps_t", (128, 1024), BF16)
    pv = {"PS": PS, "ps_a": ps_a, "ps_b": ps_b, "ps_c": ps_c, "ps_t": ps_t}
    emit_flash(p, pv, x, wq, O, consts=True)
    return p


def emit_flash(p, pv, x, wq, O, consts=False, oscat=None):
    SQ = S
    NQT = SQ // 128
    H8 = 8
    IN8 = H8 * 64
    nc = p.nc
    PS, ps_a, ps_b, ps_c, ps_t = (pv["PS"], pv["ps_a"], pv["ps_b"], pv["ps_c"],
                                  pv["ps_t"])
    env = make_ln_env(p, 128, alloc_psum=False, pref="fl")
    q64 = p.sb("q64", (64, H8 * SQ), BF16)
    k64 = p.sb("k64", (64, H8 * SQ), BF16)
    v_sb = p.sb("v_sb", (128, (SQ // 128) * IN8), BF16)
    wq_sb = p.sb("wq_sb", (128, NCH * 3 * IN8), BF16)
    xt = p.sb("xt", (128, NCH * 128), F32)
    zb = p.sb("zb", (128, NCH * 128), BF16)
    mask_hi = p.sb("mask_hi", (128, 128), BF16)
    p_sb = p.sb("p_sb", (128, 512), BF16)
    pt_sb = p.sb("pt_sb", (128, 512), BF16)
    o_acc = p.sb("o_acc", (128, 64), F32)
    o_strip = p.sb("o_strip", (128, IN8), BF16)
    l_col = p.sb("l_col", (128, 12), F32)
    env["ps_stat"] = ps_c[:, 0:2]
    env["ps_tra"] = ps_a[0:1, 0:128]
    env["ps_trb"] = ps_b[0:1, 0:128]
    env["ps_bca"] = PS[:, 0:128]
    env["ps_bcb"] = PS[:, 128:256]

    with nc.Block() as blk:
        @blk.gpsimd
        def _(g):
            if consts:
                p.emit_consts(g)
            i = g.memset(mask_hi[:, :], 0.0)
            p.serial(g, i)
            i = g.affine_select(mask_hi[:, :], mask_hi[:, :], [[-1, 128]], ALU.is_ge,
                                NEG, base=0, channel_multiplier=1)
            p.serial(g, i)
            i = g.dma_start(wq_sb.ap().rearrange("p (c n) -> p c n", c=NCH),
                            wq.ap().transpose([1, 0, 2]))
            n = p.inc(i, "ldg", 16)
            g.wait_ge(p.sem("ldg"), n)

    for it in range(SQ // 128):
        t0 = it * 128
        with nc.Block() as blk:
            @blk.sync
            def _(s, t0=t0):
                i = s.dma_start(xt.ap().rearrange("p (c n) -> p c n", c=NCH),
                                x.ap()[:, :, t0:t0 + 128].transpose([1, 0, 2]))
                p.inc(i, "ldx", 16)
                p.wait(s, "ldx")
        emit_ln(p, env, lambda c: xt[:, c * 128:(c + 1) * 128],
                lambda c: zb[:, c * 128:(c + 1) * 128], 128)
        with nc.Block() as blk:
            @blk.tensor
            def _(t, it=it, t0=t0):
                mm0 = p.cnt.get("qev", 0)
                for cg in range(8):
                    pp = (ps_a, ps_b)[cg % 2]
                    if cg >= 2:
                        t.wait_ge(p.sem("qev"), mm0 + cg - 1)
                    for ic in range(NCH):
                        i = t.matmul(pp[:, 0:128],
                                     wq_sb[:, ic * 3 * IN8 + cg * 128: ic * 3 * IN8 + cg * 128 + 128],
                                     zb[:, ic * 128:(ic + 1) * 128],
                                     start=(ic == 0), stop=(ic == NCH - 1))
                    p.inc(i, "qmm")
                pp = ps_a
                t.wait_ge(p.sem("qev"), mm0 + 8 - 1)
                for ic in range(NCH):
                    i = t.matmul(pp[:, 0:512],
                                 zb[:, ic * 128:(ic + 1) * 128],
                                 wq_sb[:, ic * 3 * IN8 + 2 * IN8: ic * 3 * IN8 + 3 * IN8],
                                 start=(ic == 0), stop=(ic == NCH - 1))
                p.inc(i, "qmm")
            @blk.vector
            def _(v, it=it, t0=t0):
                mm0 = p.cnt.get("qmm", 0) - 9
                for cg in range(8):
                    pp = (ps_a, ps_b)[cg % 2]
                    v.wait_ge(p.sem("qmm"), mm0 + cg + 1)
                    dstb = q64 if cg < 4 else k64
                    hh = (cg % 4) * 2
                    v.tensor_copy(dstb[0:64, hh * SQ + t0: hh * SQ + t0 + 128], pp[0:64, 0:128])
                    i = v.tensor_copy(dstb[0:64, (hh + 1) * SQ + t0: (hh + 1) * SQ + t0 + 128],
                                      pp[64:128, 0:128])
                    p.inc(i, "qev")
                v.wait_ge(p.sem("qmm"), mm0 + 9)
                i = v.tensor_copy(v_sb[:, it * IN8:(it + 1) * IN8], ps_a[:, 0:512])
                p.inc(i, "qev")

    for qt in range(NQT):
        kend = (qt + 1) * 128
        ktiles = []
        k0 = 0
        while k0 < kend:
            k1 = min(k0 + 512, kend)
            ktiles.append((k0, k1))
            k0 = k1
        NKT = len(ktiles)
        c0 = {nm: p.cnt.get(nm, 0) for nm in
              ("smm", "sex", "ptr", "ptv", "pvm", "oup", "sal", "sbs")}
        with nc.Block() as blk:
            @blk.tensor
            def _(t, qt=qt, ktiles=ktiles, NKT=NKT, c0=c0):
                for h in range(H8):
                    hb = h * NKT
                    for kt in range(NKT + 1):
                        if kt < NKT:
                            k0, k1 = ktiles[kt]
                            if hb + kt >= 1:
                                t.wait_ge(p.sem("sex"), c0["sex"] + hb + kt)
                            i = t.matmul(PS[:, 0:k1 - k0],
                                         q64[0:64, h * SQ + qt * 128: h * SQ + (qt + 1) * 128],
                                         k64[0:64, h * SQ + k0: h * SQ + k1],
                                         start=True, stop=True)
                            p.inc(i, "smm")
                        if kt >= 1:
                            kk = kt - 1
                            k0, k1 = ktiles[kk]
                            t.wait_ge(p.sem("sex"), c0["sex"] + hb + kk + 1)
                            if hb + kk >= 1:
                                t.wait_ge(p.sem("ptv"), c0["ptv"] + hb + kk)
                            nch_k = (k1 - k0) // 128
                            for j in range(nch_k):
                                i = t.transpose(ps_t[:, j * 128:(j + 1) * 128],
                                                p_sb[:, j * 128:(j + 1) * 128], p.identb[:, :])
                            p.inc(i, "ptr")
                            t.wait_ge(p.sem("ptv"), c0["ptv"] + hb + kk + 1)
                            if hb + kk >= 1:
                                t.wait_ge(p.sem("oup"), c0["oup"] + hb + kk)
                            for j in range(nch_k):
                                vt = (k0 // 128) + j
                                i = t.matmul(ps_c[:, 0:64], pt_sb[:, j * 128:(j + 1) * 128],
                                             v_sb[:, vt * IN8 + h * 64: vt * IN8 + (h + 1) * 64],
                                             start=(j == 0), stop=(j == nch_k - 1))
                            p.inc(i, "pvm")
            @blk.vector
            def _(v, qt=qt, ktiles=ktiles, NKT=NKT, c0=c0, kend=kend):
                for h in range(H8):
                    hb = h * NKT
                    for kt in range(NKT + 1):
                        if kt < NKT:
                            k0, k1 = ktiles[kt]
                            v.wait_ge(p.sem("smm"), c0["smm"] + hb + kt + 1)
                            if k1 == kend:
                                i = v.tensor_add(PS[:, k1 - k0 - 128:k1 - k0],
                                                 PS[:, k1 - k0 - 128:k1 - k0], mask_hi[:, :])
                                p.serial(v, i)
                            i = v.reduce_max(l_col[:, 0:1], PS[:, 0:k1 - k0], AX.X, negate=True)
                            p.serial(v, i)
                            if kt == 0:
                                i = v.tensor_copy(l_col[:, 1:2], l_col[:, 0:1])
                            else:
                                v.wait_ge(p.sem("sal"), c0["sal"] + hb + kt)
                                i = v.tensor_copy(l_col[:, 3:4], l_col[:, 1:2])
                                p.serial(v, i)
                                i = v.tensor_tensor(l_col[:, 1:2], l_col[:, 1:2], l_col[:, 0:1], ALU.min)
                            p.serial(v, i)
                            if kt >= 2:
                                v.wait_ge(p.sem("sal"), c0["sal"] + hb + kt - 1)
                            bc = 2 + 6 * (kt % 2)
                            i = v.tensor_scalar(l_col[:, bc:bc + 1], l_col[:, 1:2],
                                                p.ccol[0.125][:, 0:1], None, ALU.mult)
                            p.inc(i, "sbs")
                        if kt >= 1:
                            kk = kt - 1
                            k0, k1 = ktiles[kk]
                            if hb + kk >= 1:
                                v.wait_ge(p.sem("pvm"), c0["pvm"] + hb + kk)
                            v.wait_ge(p.sem("ptr"), c0["ptr"] + hb + kk + 1)
                            i = v.tensor_copy(pt_sb[:, 0:k1 - k0], ps_t[:, 0:k1 - k0])
                            p.inc(i, "ptv")
                            v.wait_ge(p.sem("pvm"), c0["pvm"] + hb + kk + 1)
                            v.wait_ge(p.sem("sal"), c0["sal"] + hb + kk + 1)
                            if kk == 0:
                                v.tensor_copy(l_col[:, 5:6], l_col[:, 6:7])
                                i = v.tensor_copy(o_acc[:, 0:64], ps_c[:, 0:64])
                                p.serial(v, i)
                            else:
                                v.scalar_tensor_tensor(l_col[:, 5:6], l_col[:, 5:6],
                                                       l_col[:, 4:5], l_col[:, 6:7],
                                                       ALU.mult, ALU.add)
                                i = v.scalar_tensor_tensor(o_acc[:, 0:64], o_acc[:, 0:64],
                                                           l_col[:, 4:5], ps_c[:, 0:64],
                                                           ALU.mult, ALU.add)
                                p.serial(v, i)
                            i = v.engine_nop()
                            p.inc(i, "oup")
                    i = v.reciprocal(l_col[:, 7:8], l_col[:, 5:6])
                    p.serial(v, i)
                    i = v.tensor_scalar(o_strip[:, h * 64:(h + 1) * 64], o_acc[:, 0:64],
                                        l_col[:, 7:8], None, ALU.mult)
                    p.inc(i, "ofin")
            @blk.scalar
            def _(sc, ktiles=ktiles, NKT=NKT, c0=c0):
                for h in range(H8):
                    hb = h * NKT
                    for kt in range(NKT):
                        k0, k1 = ktiles[kt]
                        sc.wait_ge(p.sem("sbs"), c0["sbs"] + hb + kt + 1)
                        if hb + kt >= 1:
                            sc.wait_ge(p.sem("ptr"), c0["ptr"] + hb + kt)
                            sc.wait_ge(p.sem("oup"), c0["oup"] + hb + kt)
                        bc = 2 + 6 * (kt % 2)
                        i = sc.activation(p_sb[:, 0:k1 - k0], PS[:, 0:k1 - k0], AF.Exp,
                                          bias=l_col[:, bc:bc + 1], scale=p.ccol[0.125][:, 0:1],
                                          accum_out=l_col[:, 6:7])
                        p.inc(i, "sex")
                        if kt == 0:
                            i = sc.copy(l_col[:, 4:5], p.ones_col[:, 0:1])
                        else:
                            i = sc.activation(l_col[:, 4:5], l_col[:, 3:4], AF.Exp,
                                              bias=l_col[:, bc:bc + 1],
                                              scale=p.ccol[-0.125][:, 0:1])
                        p.inc(i, "sal")
        with nc.Block() as blk:
            if oscat is None:
                @blk.sync
                def _(s, qt=qt):
                    i = s.dma_start(O.ap()[qt, :, :], o_strip[:, :])
                    p.inc(i, "sto", 16)
                    p.wait(s, "sto")
            else:
                @blk.gpsimd
                def _(g, qt=qt):
                    g.wait_ge(p.sem("xld"), p.cnt["xld"])
                    i = g.indirect_dma_start(
                        oscat[0].ap(),
                        bass.IndirectOffsetOnAxis(ap=oscat[1][:, qt:qt + 1], axis=0),
                        o_strip[:, :], None)
                    p.inc(i, "sto", 16)
                    p.wait(g, "sto")


def build_mfront(name="G2"):
    NI2 = 16
    p = Prog(name)
    x1h = p.inp("x1h", (NCH, 128, TOK + 3))
    h2h = p.inp("h2h", (NCH, 128, TOK + 3))
    win = p.inp("win", (NCH, 128, 4096))
    convw = p.inp("convw", (128, 64))
    convb = p.inp("convb", (128, 16))
    xp = p.inp("xp", (NI2, 128, 96))
    dtw = p.inp("dtw", (64, 2048))
    dtb = p.inp("dtb", (128, 16))
    x2o = p.outp("x2", (NCH, 128, TOK))
    uo = p.outp("u", (NI2, 128, TOK))
    dlo = p.outp("dl", (NI2, 128, TOK))
    zzo = p.outp("zz", (NI2, 128, TOK))
    bco = p.outp("bc", (32, TOK))
    p.setup_consts(vals=(LN_EPS, -1.0))
    PS = p.ps("PS", (128, 1024))
    ps_a = p.ps("ps_a", (128, 512))
    ps_b = p.ps("ps_b", (128, 512))
    ps_c = p.ps("ps_c", (128, 512))
    pv = {"PS": PS, "ps_a": ps_a, "ps_b": ps_b, "ps_c": ps_c}
    emit_mfront(p, pv, x1h, h2h, win, convw, convb, xp, dtw, dtb,
                x2o, uo, dlo, zzo, bco, consts=True)
    return p


def emit_mfront(p, pv, x1h, h2h, win, convw, convb, xp, dtw, dtb,
                x2o, uo, dlo, zzo, bco, consts=False):
    NI2 = 16
    nc = p.nc
    PS, ps_a, ps_b, ps_c = pv["PS"], pv["ps_a"], pv["ps_b"], pv["ps_c"]
    env = make_ln_env(p, 512, alloc_psum=False)
    m_bf = p.sb("m_bf", (128, NCH * (TOK + 3)), BF16)
    w_sb = p.sb("w_sb", (128, NCH * 4096), BF16)
    xpw = p.sb("xpw", (128, NI2 * 96), BF16)
    dtw_sb = p.sb("dtw_sb", (64, 2048), BF16)
    cw_sb = p.sb("cw_sb", (128, 64), F32)
    cb_sb = p.sb("cb_sb", (128, 16), F32)
    db_sb = p.sb("db_sb", (128, 16), F32)
    x1_sb = p.sb("x1_sb", (128, NCH * 256), F32)
    h2_sb = p.sb("h2_sb", (128, NCH * 256), F32)
    x2_sb = x1_sb
    xz = p.sb("xz", (128, NI2 * 515), F32)
    cacc = p.sb("cacc", (128, 512), F32)
    u_sb = p.sb("u_sb", (128, 512), F32)
    ub = p.sb("ub", (128, NI2 * 512), BF16)
    zz_sb = p.sb("zz_sb", (128, 512), F32)
    xd_sb = p.sb("xd_sb", (96, 512), F32)
    xdb = p.sb("xdb", (96, 512), BF16)
    sp1 = p.sb("sp1", (128, 512), F32)
    sp2 = p.sb("sp2", (128, 512), F32)
    sp3 = p.sb("sp3", (128, 512), F32)
    dl_sb = p.sb("dl_sb", (128, 512), F32)
    env["ps_stat"] = ps_c[:, 0:8]
    env["ps_tra"] = ps_a[0:1, 0:512]
    env["ps_trb"] = ps_b[0:1, 0:512]
    env["ps_bca"] = PS[:, 0:512]
    env["ps_bcb"] = PS[:, 512:1024]

    with nc.Block() as blk:
        @blk.gpsimd
        def _(g):
            if consts:
                p.emit_consts(g)
            i = g.dma_start(w_sb.ap().rearrange("p (c n) -> p c n", c=NCH),
                            win.ap().transpose([1, 0, 2]))
            p.inc(i, "ldg", 16)
            i = g.dma_start(xpw.ap().rearrange("p (c n) -> p c n", c=NI2),
                            xp.ap().transpose([1, 0, 2]))
            p.inc(i, "ldg", 16)
            i = g.dma_start(dtw_sb.ap(), dtw.ap())
            p.inc(i, "ldg", 16)
            g.wait_ge(p.sem("ldg"), p.cnt["ldg"])
        @blk.sync
        def _(s):
            for ap_out, ap_in in ((cw_sb.ap(), convw.ap()), (cb_sb.ap(), convb.ap()),
                                  (db_sb.ap(), dtb.ap())):
                i = s.dma_start(ap_out, ap_in)
                p.inc(i, "ld", 16)
            p.wait(s, "ld")

    tiles1 = [(0, 128)] + [(3 + i * 256, 256) for i in range(TOK // 256)]
    for (c0, ww) in tiles1:
        with nc.Block() as blk:
            @blk.sync
            def _(s, c0=c0, ww=ww):
                i = s.dma_start(x1_sb.ap()[:, 0:NCH * ww].rearrange("p (c n) -> p c n", c=NCH),
                                x1h.ap()[:, :, c0:c0 + ww].transpose([1, 0, 2]))
                p.inc(i, "ldx", 16)
                i = s.dma_start(h2_sb.ap()[:, 0:NCH * ww].rearrange("p (c n) -> p c n", c=NCH),
                                h2h.ap()[:, :, c0:c0 + ww].transpose([1, 0, 2]))
                p.inc(i, "ldx", 16)
                p.wait(s, "ldx")
        with nc.Block() as blk:
            @blk.vector
            def _(v, ww=ww):
                for c in range(NCH):
                    i = v.tensor_add(x2_sb[:, c * ww:(c + 1) * ww],
                                     x1_sb[:, c * ww:(c + 1) * ww],
                                     h2_sb[:, c * ww:(c + 1) * ww])
        emit_ln(p, env, (lambda c, ww=ww: x2_sb[:, c * ww:(c + 1) * ww]),
                (lambda c, c0=c0, ww=ww: m_bf[:, c * (TOK + 3) + c0: c * (TOK + 3) + c0 + ww]),
                ww)
        if c0 >= 3:
            with nc.Block() as blk:
                @blk.sync
                def _(s, c0=c0, ww=ww):
                    i = s.dma_start(x2o.ap()[:, :, c0 - 3:c0 - 3 + ww].transpose([1, 0, 2]),
                                    x2_sb.ap()[:, 0:NCH * ww].rearrange("p (c n) -> p c n", c=NCH))
                    p.inc(i, "stx", 16)
                    p.wait(s, "stx")

    c_cev0 = p.cnt.get("cev", 0)
    c_cmm0 = p.cnt.get("cmm", 0)
    with nc.Block() as blk:
        @blk.tensor
        def _(t):
            for oc in range(NI2):
                pp = (ps_a, ps_b)[oc % 2]
                if oc >= 2:
                    t.wait_ge(p.sem("cev"), c_cev0 + oc - 1)
                for ic in range(NCH):
                    i = t.matmul(pp[:, 0:3],
                                 w_sb[:, ic * 4096 + oc * 128: ic * 4096 + oc * 128 + 128],
                                 m_bf[:, ic * (TOK + 3): ic * (TOK + 3) + 3],
                                 start=(ic == 0), stop=(ic == NCH - 1))
                p.inc(i, "cmm")
        @blk.vector
        def _(v):
            for oc in range(NI2):
                pp = (ps_a, ps_b)[oc % 2]
                v.wait_ge(p.sem("cmm"), c_cmm0 + oc + 1)
                i = v.tensor_copy(xz[:, oc * 515: oc * 515 + 3], pp[:, 0:3])
                p.inc(i, "cev")

    NT = TOK // 512
    for it in range(NT):
        t0 = it * 512
        c_xmm = p.cnt.get("xmm", 0)
        c_xev = p.cnt.get("xev", 0)
        with nc.Block() as blk:
            @blk.tensor
            def _(t, t0=t0, c_xev=c_xev):
                for oc in range(32):
                    pp = (ps_a, ps_b)[oc % 2]
                    if oc >= 2:
                        t.wait_ge(p.sem("xev"), c_xev + oc - 1)
                    for ic in range(NCH):
                        i = t.matmul(pp[:, 0:512],
                                     w_sb[:, ic * 4096 + oc * 128: ic * 4096 + oc * 128 + 128],
                                     m_bf[:, ic * (TOK + 3) + t0 + 3: ic * (TOK + 3) + t0 + 3 + 512],
                                     start=(ic == 0), stop=(ic == NCH - 1))
                    p.inc(i, "xmm")
            @blk.vector
            def _(v, it=it, t0=t0, c_xmm=c_xmm):
                for oc in range(NI2):
                    pp = (ps_a, ps_b)[oc % 2]
                    v.wait_ge(p.sem("xmm"), c_xmm + oc + 1)
                    i = v.tensor_copy(xz[:, oc * 515 + 3: oc * 515 + 515], pp[:, 0:512])
                    p.inc(i, "xev")
            @blk.scalar
            def _(sc, it=it, t0=t0, c_xmm=c_xmm, c_stz=p.cnt.get("stz", 0)):
                for oc in range(NI2, 32):
                    pp = (ps_a, ps_b)[oc % 2]
                    sc.wait_ge(p.sem("xmm"), c_xmm + oc + 1)
                    if oc >= NI2 + 1:
                        sc.wait_ge(p.sem("stz"), c_stz + 16 * (oc - NI2))
                    i = sc.activation(zz_sb[:, 0:512], pp[:, 0:512], AF.Silu)
                    p.inc(i, "xev")
            @blk.sync
            def _(s, it=it, t0=t0, c_xev=c_xev):
                for oc in range(NI2, 32):
                    s.wait_ge(p.sem("xev"), c_xev + oc + 1)
                    i = s.dma_start(zzo.ap()[oc - NI2, :, t0:t0 + 512], zz_sb[:, 0:512])
                    p.inc(i, "stz", 16)
                p.wait(s, "stz")
        c_cv = p.cnt.get("cv", 0)
        c_slu = p.cnt.get("slu", 0)
        c_ubc = p.cnt.get("ubc", 0)
        c_stu = p.cnt.get("stu", 0)
        with nc.Block() as blk:
            @blk.vector
            def _(v, c_slu=c_slu):
                for c in range(NI2 + 1):
                    if c < NI2:
                        base = c * 515
                        if c >= 1:
                            v.wait_ge(p.sem("slu"), c_slu + c)
                        i = v.tensor_scalar(cacc[:, 0:512], xz[:, base:base + 512],
                                            cw_sb[:, c * 4:c * 4 + 1], None, ALU.mult)
                        p.serial(v, i)
                        for tau in range(1, 4):
                            i = v.scalar_tensor_tensor(cacc[:, 0:512], xz[:, base + tau:base + tau + 512],
                                                       cw_sb[:, c * 4 + tau:c * 4 + tau + 1],
                                                       cacc[:, 0:512], ALU.mult, ALU.add)
                            if tau < 3:
                                p.serial(v, i)
                        i = v.tensor_copy(xz[:, base:base + 3], xz[:, base + 512:base + 515])
                        p.inc(i, "cv")
                    if c >= 1:
                        v.wait_ge(p.sem("slu"), c_slu + c)
                        i = v.tensor_copy(ub[:, (c - 1) * 512:c * 512], u_sb[:, 0:512])
                        p.inc(i, "ubc")
            @blk.scalar
            def _(sc, c_cv=c_cv, c_ubc=c_ubc, c_stu=c_stu):
                for c in range(NI2):
                    sc.wait_ge(p.sem("cv"), c_cv + c + 1)
                    if c >= 1:
                        sc.wait_ge(p.sem("ubc"), c_ubc + c)
                        sc.wait_ge(p.sem("stu"), c_stu + 16 * c)
                    i = sc.activation(u_sb[:, 0:512], cacc[:, 0:512], AF.Silu,
                                      bias=cb_sb[:, c:c + 1])
                    p.inc(i, "slu")
            @blk.sync
            def _(s, it=it, t0=t0, c_slu=c_slu):
                for c in range(NI2):
                    s.wait_ge(p.sem("slu"), c_slu + c + 1)
                    i = s.dma_start(uo.ap()[c, :, t0:t0 + 512], u_sb[:, 0:512])
                    p.inc(i, "stu", 16)
                p.wait(s, "stu")
        with nc.Block() as blk:
            @blk.tensor
            def _(t):
                for ic in range(NI2):
                    i = t.matmul(PS[0:96, 512:1024], xpw[:, ic * 96:(ic + 1) * 96],
                                 ub[:, ic * 512:(ic + 1) * 512],
                                 start=(ic == 0), stop=(ic == NI2 - 1))
                p.inc(i, "xdm")
            @blk.vector
            def _(v):
                p.wait(v, "xdm")
                v.tensor_copy(xd_sb[:, 0:512], PS[0:96, 512:1024])
                i = v.tensor_copy(xdb[:, 0:512], PS[0:96, 512:1024])
                p.inc(i, "xde")
            @blk.sync
            def _(s, t0=t0):
                p.wait(s, "xde")
                i = s.dma_start(bco.ap()[:, t0:t0 + 512], xd_sb[64:96, 0:512])
                p.inc(i, "stb", 16)
                p.wait(s, "stb")
        c_dmm = p.cnt.get("dmm", 0)
        c_dev = p.cnt.get("dev", 0)
        with nc.Block() as blk:
            @blk.tensor
            def _(t, c_dev=c_dev):
                for oc in range(NI2):
                    pp = (ps_a, ps_b)[oc % 2]
                    if oc >= 2:
                        t.wait_ge(p.sem("dev"), c_dev + oc - 1)
                    i = t.matmul(pp[:, 0:512], dtw_sb[:, oc * 128:(oc + 1) * 128],
                                 xdb[0:64, 0:512], start=True, stop=True)
                    p.inc(i, "dmm")
            @blk.vector
            def _(v, c_dmm=c_dmm, c_dln=p.cnt.get("dln", 0), c_std=p.cnt.get("std", 0)):
                for oc in range(NI2):
                    pp = (ps_a, ps_b)[oc % 2]
                    v.wait_ge(p.sem("dmm"), c_dmm + oc + 1)
                    i = v.tensor_scalar(sp1[:, 0:512], pp[:, 0:512], db_sb[:, oc:oc + 1],
                                        None, ALU.add)
                    p.inc(i, "dab")
                    v.wait_ge(p.sem("dab"), p.cnt["dab"])
                    i = v.tensor_scalar_max(sp3[:, 0:512], sp1[:, 0:512], 0.0)
                    p.serial(v, i)
                    v.wait_ge(p.sem("dln"), c_dln + oc + 1)
                    if oc >= 1:
                        v.wait_ge(p.sem("std"), c_std + 16 * oc)
                    i = v.tensor_add(dl_sb[:, 0:512], sp3[:, 0:512], sp2[:, 0:512])
                    p.inc(i, "dev")
            @blk.scalar
            def _(sc, c_dmm=c_dmm, c_dev=c_dev):
                da0 = p.cnt.get("dab", 0) - NI2
                for oc in range(NI2):
                    sc.wait_ge(p.sem("dab"), da0 + oc + 1)
                    if oc >= 1:
                        sc.wait_ge(p.sem("dev"), c_dev + oc)
                    i = sc.activation(sp2[:, 0:512], sp1[:, 0:512], AF.Abs)
                    p.serial(sc, i, tag="scser")
                    i = sc.activation(sp2[:, 0:512], sp2[:, 0:512], AF.Exp,
                                      scale=p.ccol[-1.0][:, 0:1])
                    p.serial(sc, i, tag="scser")
                    i = sc.activation(sp2[:, 0:512], sp2[:, 0:512], AF.Ln,
                                      bias=p.ones_col[:, 0:1])
                    p.inc(i, "dln")
            @blk.sync
            def _(s, t0=t0, c_dev=c_dev):
                for oc in range(NI2):
                    s.wait_ge(p.sem("dev"), c_dev + oc + 1)
                    i = s.dma_start(dlo.ap()[oc, :, t0:t0 + 512], dl_sb[:, 0:512])
                    p.inc(i, "std", 16)
                p.wait(s, "std")
    return p


def build_add(name="G4"):
    p = Prog(name)
    nc = p.nc
    a = p.inp("a", (NCH, 128, TOK))
    b = p.inp("b", (NCH, 128, TOK))
    o = p.outp("o", (NCH, 128, TOK))
    a_sb = p.sb("a_sb", (128, NCH * 512), F32)
    b_sb = p.sb("b_sb", (128, NCH * 512), F32)
    o_sb = p.sb("o_sb", (128, NCH * 512), F32)
    for it in range(TOK // 512):
        t0 = it * 512
        with nc.Block() as blk:
            @blk.sync
            def _(s, t0=t0):
                i = s.dma_start(a_sb.ap().rearrange("p (c n) -> p c n", c=NCH),
                                a.ap()[:, :, t0:t0 + 512].transpose([1, 0, 2]))
                p.inc(i, "ld", 16)
                i = s.dma_start(b_sb.ap().rearrange("p (c n) -> p c n", c=NCH),
                                b.ap()[:, :, t0:t0 + 512].transpose([1, 0, 2]))
                p.inc(i, "ld", 16)
                p.wait(s, "ld")
        with nc.Block() as blk:
            @blk.vector
            def _(v):
                i = v.tensor_add(o_sb[:, :], a_sb[:, :], b_sb[:, :])
        with nc.Block() as blk:
            @blk.sync
            def _(s, t0=t0):
                i = s.dma_start(o.ap()[:, :, t0:t0 + 512].transpose([1, 0, 2]),
                                o_sb.ap().rearrange("p (c n) -> p c n", c=NCH))
                p.inc(i, "st", 16)
                p.wait(s, "st")
    return p


def build_scan(name="P4"):
    TC = 512
    NTCH = S // TC
    p = Prog(name)
    nc = p.nc
    dl = p.inp("dl", (B, 8, 32, S))
    u = p.inp("u", (B, 8, 32, S))
    zz = p.inp("zz", (B, 8, 32, S))
    bc = p.inp("bc", (B, 32, S))
    alog = p.inp("alog", (128, 32))
    dp = p.inp("dp", (8, 32))
    e8 = p.inp("e8", (8, 128))
    s8 = p.inp("s8", (128, 8))
    y = p.outp("y", (B, 8, 32, S))

    p.setup_consts(vals=(-1.0,))
    GD = 8
    d_sb = p.sb("d_sb", (8, GD * TC), F32)
    u_sb = p.sb("u_sb", (8, GD * TC), F32)
    z_sb = p.sb("z_sb", (8, GD * TC), F32)
    zs_sb = p.sb("zs_sb", (8, GD * TC), F32)
    yst = p.sb("yst", (8, GD * TC), F32)
    bctb = p.sb("bctb", (16, TC), F32)
    bctc = p.sb("bctc", (16, TC), F32)
    bbc = p.sb("bbc", (128, TC), F32)
    cbc = p.sb("cbc", (128, TC), F32)
    a_sb = p.sb("a_sb", (128, 32), F32)
    dp_sb = p.sb("dp_sb", (8, 32), F32)
    e8f = p.sb("e8f", (8, 128), F32)
    ebf = p.sb("ebf", (16, 128), F32)
    s8f = p.sb("s8f", (128, 8), F32)
    dA = p.sb("dA", (128, TC), F32)
    g_sb = p.sb("g_sb", (128, TC), F32)
    t1 = p.sb("t1", (128, TC), F32)
    h_sb = p.sb("h_sb", (128, TC), F32)
    t2 = p.sb("t2", (128, TC), F32)
    t3 = p.sb("t3", (8, TC), F32)
    state = p.sb("state", (128, 32 * B), F32)
    ps_a = p.ps("ps_a", (128, 512))
    ps_b = p.ps("ps_b", (128, 512))
    ps_c = p.ps("ps_c", (128, 512))
    ps_d = p.ps("ps_d", (128, 512))

    with nc.Block() as blk:
        @blk.sync
        def _(s):
            for ap_out, ap_in in ((a_sb.ap(), alog.ap()), (dp_sb.ap(), dp.ap()),
                                  (e8f.ap(), e8.ap()), (s8f.ap(), s8.ap())):
                i = s.dma_start(ap_out, ap_in)
                p.inc(i, "ld", 16)
            p.wait(s, "ld")
        @blk.gpsimd
        def _(g):
            p.emit_consts(g)
            g.memset(state[:, :], 0.0)
    ebi = p.inp("eb", (16, 128))
    with nc.Block() as blk:
        @blk.sync
        def _(s):
            i = s.dma_start(ebf.ap(), ebi.ap())
            p.inc(i, "ld2", 16)
            p.wait(s, "ld2")
    with nc.Block() as blk:
        @blk.scalar
        def _(sc):
            i = sc.activation(a_sb[:, :], a_sb[:, :], AF.Exp)
            p.inc(i, "ae")
        @blk.vector
        def _(v):
            p.wait(v, "ae")
            i = v.tensor_scalar(a_sb[:, :], a_sb[:, :], -1.0, None, ALU.mult)
            p.serial(v, i)

    for b_ in range(B):
        for tc in range(NTCH):
            t0 = tc * TC
            with nc.Block() as blk:
                @blk.sync
                def _(s, b_=b_, t0=t0):
                    i = s.dma_start(bctb.ap(), bc.ap()[b_][0:16, t0:t0 + TC])
                    p.inc(i, "ldb", 16)
                    i = s.dma_start(bctc.ap(), bc.ap()[b_][16:32, t0:t0 + TC])
                    p.inc(i, "ldb", 16)
                    p.wait(s, "ldb")
            with nc.Block() as blk:
                @blk.tensor
                def _(t):
                    t.matmul(ps_c[:, 0:TC], ebf[:, :], bctb[:, 0:TC], start=True, stop=True)
                    i = t.matmul(ps_d[:, 0:TC], ebf[:, :], bctc[:, 0:TC], start=True, stop=True)
                    p.inc(i, "bcm")
                @blk.vector
                def _(v):
                    p.wait(v, "bcm")
                    v.tensor_copy(bbc[:, :], ps_c[:, 0:TC])
                    i = v.tensor_copy(cbc[:, :], ps_d[:, 0:TC])
                    p.inc(i, "bce")
            for grp in range(32 // GD):
                g0 = grp * GD
                with nc.Block() as blk:
                    @blk.sync
                    def _(s, b_=b_, t0=t0, g0=g0):
                        for sb_, dr in ((d_sb, dl), (u_sb, u), (z_sb, zz)):
                            i = s.dma_start(sb_.ap().rearrange("p (k n) -> p k n", k=GD),
                                            dr.ap()[b_][:, g0:g0 + GD, t0:t0 + TC])
                            p.inc(i, "ldt", 16)
                        p.wait(s, "ldt")
                    @blk.scalar
                    def _(sc):
                        p.wait(sc, "ldt")
                        i = sc.activation(zs_sb[:, :], z_sb[:, :], AF.Silu)
                        p.inc(i, "zs")
                for dk in range(GD):
                    dblk = g0 + dk
                    with nc.Block() as blk:
                        @blk.tensor
                        def _(t, dk=dk):
                            i = t.matmul(ps_a[:, 0:TC], e8f[:, :],
                                         d_sb[0:8, dk * TC:(dk + 1) * TC], start=True, stop=True)
                            p.inc(i, "dbm")
                            i = t.matmul(ps_b[:, 0:TC], e8f[:, :],
                                         u_sb[0:8, dk * TC:(dk + 1) * TC], start=True, stop=True)
                            p.inc(i, "ubm")
                        @blk.scalar
                        def _(sc, dblk=dblk):
                            p.wait(sc, "dbm")
                            i = sc.activation(dA[:, :], ps_a[:, 0:TC], AF.Exp,
                                              scale=a_sb[:, dblk:dblk + 1])
                            p.inc(i, "dae")
                        @blk.vector
                        def _(v, b_=b_, dblk=dblk):
                            p.wait(v, "ubm")
                            i = v.tensor_mul(t1[:, :], ps_b[:, 0:TC], bbc[:, :])
                            p.serial(v, i)
                            i = v.tensor_mul(g_sb[:, :], t1[:, :], ps_a[:, 0:TC])
                            p.serial(v, i)
                            p.wait(v, "dae")
                            sl = state[:, b_ * 32 + dblk: b_ * 32 + dblk + 1]
                            i = v.tensor_tensor_scan(h_sb[:, :], dA[:, :], g_sb[:, :],
                                                     sl, ALU.mult, ALU.add)
                            p.serial(v, i)
                            v.tensor_copy(sl, h_sb[:, TC - 1:TC])
                            i = v.tensor_mul(t2[:, :], h_sb[:, :], cbc[:, :])
                            p.inc(i, "ym")
                        @blk.tensor
                        def _(t):
                            p.wait(t, "ym")
                            i = t.matmul(ps_d[0:8, 0:TC], s8f[:, :], t2[:, :],
                                         start=True, stop=True)
                            p.inc(i, "yrm")
                        @blk.vector
                        def _(v, dk=dk, dblk=dblk):
                            p.wait(v, "yrm")
                            i = v.scalar_tensor_tensor(t3[:, :], u_sb[0:8, dk * TC:(dk + 1) * TC],
                                                       dp_sb[:, dblk:dblk + 1], ps_d[0:8, 0:TC],
                                                       ALU.mult, ALU.add)
                            p.serial(v, i)
                            i = v.tensor_mul(yst[0:8, dk * TC:(dk + 1) * TC], t3[:, :],
                                             zs_sb[0:8, dk * TC:(dk + 1) * TC])
                            p.inc(i, "yf")
                with nc.Block() as blk:
                    @blk.sync
                    def _(s, b_=b_, t0=t0, g0=g0):
                        i = s.dma_start(y.ap()[b_][:, g0:g0 + GD, t0:t0 + TC],
                                        yst.ap().rearrange("p (k n) -> p k n", k=GD))
                        p.inc(i, "sty", 16)
                        p.wait(s, "sty")
    return p


I32 = mybir.dt.int32
XPOLL_MAX = 30000000


def setup_xchg(p, nev):
    p.x_nev = nev
    p.x_widx = p.inp("xg_widx", (128, 1), I32)
    p.x_gidx = p.inp("xg_gidx", (128, 1), I32)
    p.x_fidx = p.inp("xg_fidx", (128, 1), I32)
    p.x_magic = p.inp("xg_magic", (128, 2 * nev), I32)
    p.x_mask = p.inp("xg_mask", (128, 1), F32)
    p.xi_sb = p.sb("xg_i_sb", (128, 3), I32)
    p.xm_sb = p.sb("xg_m_sb", (128, 2 * nev), I32)
    p.xk_sb = p.sb("xg_k_sb", (128, 1), F32)
    p.x_flags = [p.nc.dram_tensor("xg_flag%d" % e, [2, 2], I32, kind="Internal",
                                  addr_space="Shared") for e in range(nev)]
    p.x_regs = None


def emit_xchg_load(p, s):
    for k, t in enumerate((p.x_widx, p.x_gidx, p.x_fidx)):
        i = s.dma_start(p.xi_sb[:, k:k + 1], t.ap())
        p.inc(i, "xld", 16)
    i = s.dma_start(p.xm_sb[:, :], p.x_magic.ap())
    p.inc(i, "xld", 16)
    i = s.dma_start(p.xk_sb[:, :], p.x_mask.ap())
    p.inc(i, "xld", 16)


def emit_xsend(p, g, sh, src_ap, rows=128):
    g.wait_ge(p.sem("xld"), p.cnt["xld"])
    i = g.indirect_dma_start(
        sh.ap(), bass.IndirectOffsetOnAxis(ap=p.xi_sb[0:rows, 0:1], axis=0),
        src_ap, None)
    return p.inc(i, "xws", 16)


def emit_xflag(p, g, ev):
    g.wait_ge(p.sem("xws"), p.cnt["xws"])
    i = g.indirect_dma_start(
        p.x_flags[ev].ap(),
        bass.IndirectOffsetOnAxis(ap=p.xi_sb[0:2, 2:3], axis=0),
        p.xm_sb[0:2, 2 * ev:2 * ev + 2], None)
    p.inc(i, "xws", 16)


def emit_xpoll(p, g, ev, done_sem, done_val=1):
    nc = p.nc
    if p.x_regs is None:
        p.x_regs = tuple(g.alloc_register("xr%d" % k) for k in range(5))
    f0, f1, c0, c1, er = p.x_regs
    mr = c1
    g.wait_ge(p.sem("xws"), p.cnt["xws"])
    g.reg_mov(er, XPOLL_MAX)
    with nc.Fori(0, er, engines=[mybir.EngineType.Pool]):
        g.reg_load(f0, p.x_flags[ev].ap()[0:1, 0:1])
        g.reg_load(f1, p.x_flags[ev].ap()[1:2, 0:1])
        g.reg_load(mr, p.xm_sb[0:1, 2 * ev:2 * ev + 1])
        g.reg_alu(f0, f0, mr, ALU.subtract)
        g.reg_alu(f1, f1, mr, ALU.subtract)
        g.reg_alu(c0, f0, 0, ALU.is_equal)
        g.reg_alu(f1, f1, 0, ALU.is_equal)
        g.reg_alu(c0, c0, f1, ALU.mult)
        g.reg_alu(c0, c0, 0, ALU.is_equal)
        g.reg_alu(er, c0, XPOLL_MAX, ALU.mult)
    i = g.sem_inc(p.sem(done_sem), done_val)
    p.cnt[done_sem] = p.cnt.get(done_sem, 0) + done_val


def emit_xrecv(p, g, sh, dst_ap, rows=128):
    i = g.indirect_dma_start(
        dst_ap, None, sh.ap(),
        bass.IndirectOffsetOnAxis(ap=p.xi_sb[0:rows, 1:2], axis=0))
    return p.inc(i, "xws", 16)


def xchg_host_inputs(me, nev, rng, mg=None):
    pa = me ^ 1
    if mg is None:
        mg = rng.integers(1 << 20, 1 << 30, size=nev).astype(np.int32)
    return {
        "xg_widx": (me * 128 + np.arange(128)).astype(np.int32).reshape(128, 1),
        "xg_gidx": (pa * 128 + np.arange(128)).astype(np.int32).reshape(128, 1),
        "xg_fidx": np.full((128, 1), me, np.int32),
        "xg_magic": np.tile(np.repeat(mg, 2), (128, 1)),
        "xg_mask": np.full((128, 1), 0.0 if me == 0 else 1.0, np.float32),
    }


def emit_phase_gate(p, stores):
    nc = p.nc
    with nc.Block() as blk:
        @blk.sync
        def _(s):
            for nm in stores:
                if p.cnt.get(nm, 0):
                    s.wait_ge(p.sem(nm), p.cnt[nm])
            p.semtick(s, "phsy")
    k0 = p.cnt.get("phg", 0)
    ksy = p.cnt.get("phsy", 0)
    with nc.Block() as blk:
        @blk.tensor
        def _(t):
            t.wait_ge(p.sem("phsy"), ksy)
            p.semtick(t, "phg")
            t.wait_ge(p.sem("phg"), k0 + 4)
        @blk.vector
        def _(v):
            v.wait_ge(p.sem("phsy"), ksy)
            p.semtick(v, "phg")
            v.wait_ge(p.sem("phg"), k0 + 4)
        @blk.scalar
        def _(sc):
            sc.wait_ge(p.sem("phsy"), ksy)
            p.semtick(sc, "phg")
            sc.wait_ge(p.sem("phg"), k0 + 4)
        @blk.gpsimd
        def _(g):
            g.wait_ge(p.sem("phsy"), ksy)
            p.semtick(g, "phg")
            g.wait_ge(p.sem("phg"), k0 + 4)
    with nc.Block() as blk:
        @blk.sync
        def _(s):
            s.wait_ge(p.sem("phg"), k0 + 4)


def emit_barrier(p):
    nc = p.nc
    k0 = p.cnt.get("phg", 0)
    with nc.Block() as blk:
        @blk.tensor
        def _(t):
            p.semtick(t, "phg")
            t.wait_ge(p.sem("phg"), k0 + 4)
        @blk.vector
        def _(v):
            p.semtick(v, "phg")
            v.wait_ge(p.sem("phg"), k0 + 4)
        @blk.scalar
        def _(sc):
            p.semtick(sc, "phg")
            sc.wait_ge(p.sem("phg"), k0 + 4)
        @blk.gpsimd
        def _(g):
            p.semtick(g, "phg")
            g.wait_ge(p.sem("phg"), k0 + 4)


def emit_scan2(p, pv, uo, dlo, zzo, bco, acolP, ebc2, dpcolP, y_dr, sh_state, ev,
               no_xchg=False, no_xchg_dbg=("",)):
    nc = p.nc
    TTC = 512
    NTC2 = TOK // TTC
    psb = (pv["sc_ps0"][:, 0:512], pv["sc_ps0"][:, 512:1024])
    psc = (pv["sc_ps1"][:, 0:512], pv["sc_ps1"][:, 512:1024])
    acol = p.sb("sc_acol", (128, 256), F32)
    dpc = p.sb("sc_dpc", (128, 16), F32)
    eb_sb = p.sb("sc_eb", (32, 4096), F32)
    bct = p.sb("sc_bct", (32, TTC), F32)
    state = p.sb("sc_state", (128, 256), F32)
    state2 = p.sb("sc_state2", (128, 256), F32)
    bba = p.sb("sc_bba", (128, 16 * TTC), F32)
    cca = p.sb("sc_cca", (128, 16 * TTC), F32)
    dl_g = [p.sb("sc_dl%d" % k, (128, TTC), F32) for k in range(2)]
    u_g = [p.sb("sc_u%d" % k, (128, TTC), F32) for k in range(2)]
    zz_g = [p.sb("sc_zz%d" % k, (128, TTC), F32) for k in range(2)]
    du_g = [p.sb("sc_du%d" % k, (128, TTC), F32) for k in range(2)]
    dA_t = [p.sb("sc_dA%d" % k, (128, TTC), F32) for k in range(2)]
    gn_t = [p.sb("sc_gn%d" % k, (128, TTC), F32) for k in range(2)]
    h_t = [p.sb("sc_h%d" % k, (128, TTC), F32) for k in range(2)]
    tp_v = p.sb("sc_tpv", (128, TTC), F32)
    tp_g = p.sb("sc_tpg", (128, TTC), F32)
    yacc = p.sb("sc_yacc", (128, TTC), F32)
    t3_sb = p.sb("sc_t3", (128, TTC), F32)
    yb_sb = p.sb("sc_yb", (128, TTC), BF16)

    def C(nm):
        return p.cnt.get(nm, 0)

    with nc.Block() as blk:
        @blk.sync
        def _(s):
            for ap_o, t in ((acol.ap(), acolP), (dpc.ap(), dpcolP), (eb_sb.ap(), ebc2)):
                i = s.dma_start(ap_o, t.ap())
                p.inc(i, "sld", 16)
        @blk.scalar
        def _(sc):
            p.wait(sc, "sld")
            i = sc.activation(acol[:, :], acol[:, :], AF.Exp)
            p.inc(i, "sexp0")
        @blk.vector
        def _(v):
            p.wait(v, "sexp0")
            i = v.tensor_scalar(acol[:, :], acol[:, :], -1.0, None, ALU.mult)
            p.inc(i, "saneg")
            i = v.memset(state[:, :], 0.0)
            p.inc(i, "sst")

    for pp in range(2):
        st = state if pp == 0 else state2
        if pp == 1 and no_xchg:
            with nc.Block() as blk:
                @blk.vector
                def _(v):
                    v.wait_ge(p.sem("sst"), C("sst"))
                    i = v.memset(state2[:, :], 0.0)
                    p.inc(i, "sst")
        elif pp == 1:
            with nc.Block() as blk:
                @blk.gpsimd
                def _(g):
                    g.wait_ge(p.sem("sst"), C("sst"))
                    emit_xsend(p, g, sh_state, state[:, :])
                    emit_xflag(p, g, ev)
                    if "P" not in no_xchg_dbg[0]:
                        emit_xpoll(p, g, ev, "xpl")
                    emit_xrecv(p, g, sh_state, state2[:, :])
                    g.wait_ge(p.sem("xws"), C("xws"))
                    p.semtick(g, "xrc")
                @blk.vector
                def _(v):
                    p.wait(v, "xrc")
                    i = v.tensor_scalar(state2[:, :], state2[:, :],
                                        p.xk_sb[:, 0:1], None, ALU.mult)
                    p.inc(i, "sst")
        for tc in range(NTC2):
            t0 = tc * TTC
            nsel = 32 if pp == 1 else 16
            ndma = 48 if pp == 1 else 32
            b = {nm: C(nm) for nm in
                 ("slb", "slg_0", "slg_1", "sdu_0", "sdu_1", "smmb", "smmc",
                  "sbev", "sbrd", "sda_0", "sda_1", "sgn_0", "sgn_1",
                  "sscn_0", "sscn_1", "syf3", "syb", "syst", "stp", "sya",
                  "sst")}

            def kp(gch, n):
                return gch * 8 + n // 2

            with nc.Block() as blk:
                @blk.sync
                def _(s, pp=pp, tc=tc, t0=t0, b=b, ndma=ndma):
                    s.wait_ge(p.sem("smmb"), b["smmb"])
                    if pp == 1:
                        s.wait_ge(p.sem("smmc"), b["smmc"])
                    i = s.dma_start(bct.ap(), bco.ap()[:, t0:t0 + TTC])
                    p.inc(i, "slb", 16)
                    for gch in range(16):
                        gpar = gch % 2
                        if gch >= 2:
                            s.wait_ge(p.sem("sscn_0"), b["sscn_0"] + (gch - 1) * 8 + 1)
                            s.wait_ge(p.sem("sscn_1"), b["sscn_1"] + (gch - 1) * 8)
                            s.wait_ge(p.sem("sgn_0"), b["sgn_0"] + (gch - 1) * 8 + 1)
                            s.wait_ge(p.sem("sgn_1"), b["sgn_1"] + (gch - 1) * 8)
                            if pp == 1:
                                s.wait_ge(p.sem("syf3"), b["syf3"] + gch - 1)
                                s.wait_ge(p.sem("syb"), b["syb"] + gch - 1)
                        else:
                            s.wait_ge(p.sem("sscn_0"), b["sscn_0"])
                            s.wait_ge(p.sem("sscn_1"), b["sscn_1"])
                            s.wait_ge(p.sem("sgn_0"), b["sgn_0"])
                            s.wait_ge(p.sem("sgn_1"), b["sgn_1"])
                            if pp == 1:
                                s.wait_ge(p.sem("syf3"), b["syf3"])
                                s.wait_ge(p.sem("syb"), b["syb"])
                        i = s.dma_start(dl_g[gpar][:, :], dlo.ap()[gch][:, t0:t0 + TTC])
                        p.inc(i, "slg_%d" % gpar, 16)
                        i = s.dma_start(u_g[gpar][:, :], uo.ap()[gch][:, t0:t0 + TTC])
                        p.inc(i, "slg_%d" % gpar, 16)
                        if pp == 1:
                            i = s.dma_start(zz_g[gpar][:, :], zzo.ap()[gch][:, t0:t0 + TTC])
                            p.inc(i, "slg_%d" % gpar, 16)
                @blk.tensor
                def _(t, pp=pp, nsel=nsel, b=b):
                    p.wait(t, "slb")
                    for k in range(nsel):
                        isC = k >= 16
                        n = k % 16
                        pspair = psc if isC else psb
                        par = n % 2
                        if k >= 2:
                            t.wait_ge(p.sem("sbev"), b["sbev"] + k - 1)
                        i = t.matmul(pspair[par][:, 0:TTC],
                                     eb_sb[:, (2048 if isC else 0) + n * 128:
                                           (2048 if isC else 0) + (n + 1) * 128],
                                     bct[:, 0:TTC], start=True, stop=True)
                        p.inc(i, "smmc" if isC else "smmb")
                @blk.scalar
                def _(sc, pp=pp, tc=tc, t0=t0, b=b, ndma=ndma):
                    sc.wait_ge(p.sem("saneg"), C("saneg"))
                    for gch in range(16):
                        gpar = gch % 2
                        sc.wait_ge(p.sem("slg_%d" % gpar),
                                   b["slg_%d" % gpar] + (gch // 2 + 1) * ndma)
                        for n in range(16):
                            npar = n % 2
                            col = gch * 16 + n
                            k = kp(gch, n)
                            if k >= 2:
                                sc.wait_ge(p.sem("sscn_%d" % npar),
                                           b["sscn_%d" % npar] + k - 1)
                            i = sc.activation(dA_t[npar][:, :], dl_g[gpar][:, :],
                                              AF.Exp, scale=acol[:, col:col + 1])
                            p.inc(i, "sda_%d" % npar)
                        if pp == 1:
                            sc.wait_ge(p.sem("syb"), b["syb"] + gch + 1)
                            i = sc.dma_start(y_dr.ap()[gch][:, t0:t0 + TTC], yb_sb[:, :])
                            p.inc(i, "syst", 16)
                @blk.gpsimd
                def _(g, pp=pp, b=b, ndma=ndma):
                    g.wait_ge(p.sem("sbrd"), b["sbrd"] + 1)
                    for gch in range(16):
                        gpar = gch % 2
                        if pp == 1:
                            g.wait_ge(p.sem("slg_%d" % gpar),
                                      b["slg_%d" % gpar] + (gch // 2 + 1) * ndma)
                        g.wait_ge(p.sem("sdu_%d" % gpar), b["sdu_%d" % gpar] + gch // 2 + 1)
                        for n in range(16):
                            npar = n % 2
                            k = kp(gch, n)
                            if k >= 2:
                                g.wait_ge(p.sem("sscn_%d" % npar),
                                          b["sscn_%d" % npar] + k - 1)
                            i = g.tensor_mul(gn_t[npar][:, :], du_g[gpar][:, :],
                                             bba[:, n * TTC:(n + 1) * TTC])
                            p.inc(i, "sgn_%d" % npar)
                            if pp == 1 and npar == 1:
                                g.wait_ge(p.sem("sscn_1"), b["sscn_1"] + k + 1)
                                if gch * 16 + n - 1 > 0:
                                    g.wait_ge(p.sem("sya"), b["sya"] + gch * 16 + n - 1)
                                i = g.tensor_mul(tp_g[:, :], h_t[1][:, :],
                                                 cca[:, n * TTC:(n + 1) * TTC])
                                p.inc(i, "stp")
                        if pp == 1:
                            g.wait_ge(p.sem("syf3"), b["syf3"] + gch + 1)
                            if b["syst"] + 16 * gch > 0:
                                g.wait_ge(p.sem("syst"), b["syst"] + 16 * gch)
                            i = g.tensor_mul(yb_sb[:, :], t3_sb[:, :], zz_g[gpar][:, :])
                            p.inc(i, "syb")
                @blk.vector
                def _(v, pp=pp, tc=tc, b=b, nsel=nsel, ndma=ndma):
                    v.wait_ge(p.sem("sgn_0"), b["sgn_0"])
                    v.wait_ge(p.sem("sgn_1"), b["sgn_1"])
                    if pp == 1:
                        v.wait_ge(p.sem("stp"), b["stp"])
                    nb = ncc = 0
                    for k in range(nsel):
                        isC = k >= 16
                        n = k % 16
                        pspair = psc if isC else psb
                        par = n % 2
                        if isC:
                            ncc += 1
                            v.wait_ge(p.sem("smmc"), b["smmc"] + ncc)
                            dst = cca
                        else:
                            nb += 1
                            v.wait_ge(p.sem("smmb"), b["smmb"] + nb)
                            dst = bba
                        i = v.tensor_copy(dst[:, n * TTC:(n + 1) * TTC],
                                          pspair[par][:, 0:TTC])
                        p.inc(i, "sbev")
                    p.semtick(v, "sbrd")
                    if pp == 1 and tc == 0:
                        v.wait_ge(p.sem("sst"), b["sst"])
                    for gch in range(16):
                        gpar = gch % 2
                        v.wait_ge(p.sem("slg_%d" % gpar),
                                  b["slg_%d" % gpar] + (gch // 2 + 1) * ndma)
                        gt = b["sgn_0"] + ((gch - 1) * 8 if gch >= 2 else 0)
                        v.wait_ge(p.sem("sgn_0"), gt)
                        gt = b["sgn_1"] + ((gch - 1) * 8 if gch >= 2 else 0)
                        v.wait_ge(p.sem("sgn_1"), gt)
                        i = v.tensor_mul(du_g[gpar][:, :], dl_g[gpar][:, :], u_g[gpar][:, :])
                        p.inc(i, "sdu_%d" % gpar)
                        for n in range(16):
                            npar = n % 2
                            col = gch * 16 + n
                            k = kp(gch, n)
                            v.wait_ge(p.sem("sda_%d" % npar), b["sda_%d" % npar] + k + 1)
                            v.wait_ge(p.sem("sgn_%d" % npar), b["sgn_%d" % npar] + k + 1)
                            i = v.tensor_tensor_scan(h_t[npar][:, :], dA_t[npar][:, :],
                                                     gn_t[npar][:, :],
                                                     st[:, col:col + 1],
                                                     ALU.mult, ALU.add)
                            p.serial(v, i)
                            i = v.tensor_copy(st[:, col:col + 1],
                                              h_t[npar][:, TTC - 1:TTC])
                            p.inc(i, "sscn_%d" % npar)
                            if pp == 0 and tc == NTC2 - 1:
                                p.semtick(v, "sst")
                            if pp == 1:
                                if npar == 0:
                                    i = v.tensor_mul(tp_v[:, :], h_t[0][:, :],
                                                     cca[:, n * TTC:(n + 1) * TTC])
                                    p.serial(v, i)
                                    src_t = tp_v
                                else:
                                    v.wait_ge(p.sem("stp"), b["stp"] + gch * 8 + n // 2 + 1)
                                    src_t = tp_g
                                if n == 0:
                                    i = v.tensor_copy(yacc[:, :], src_t[:, :])
                                else:
                                    i = v.tensor_tensor(yacc[:, :], yacc[:, :],
                                                        src_t[:, :], ALU.add)
                                p.inc(i, "sya")
                        if pp == 1:
                            v.wait_ge(p.sem("syb"), b["syb"] + gch)
                            i = v.scalar_tensor_tensor(t3_sb[:, :], u_g[gpar][:, :],
                                                       dpc[:, gch:gch + 1],
                                                       yacc[:, :], ALU.mult, ALU.add)
                            p.inc(i, "syf3")


def build_g2sg3(name="G2S3", dbg=""):
    NI2 = 16
    p = Prog(name)
    nc = p.nc
    x1h = p.inp("x1h", (NCH, 128, TOK + 3))
    h2h = p.inp("h2h", (NCH, 128, TOK + 3))
    win = p.inp("win", (NCH, 128, 4096))
    convw = p.inp("convw", (128, 64))
    convb = p.inp("convb", (128, 16))
    xp = p.inp("xp", (NI2, 128, 96))
    dtw = p.inp("dtw", (64, 2048))
    dtb = p.inp("dtb", (128, 16))
    acolP = p.inp("acolP", (128, 256))
    ebc2 = p.inp("ebc2", (32, 4096))
    dpcolP = p.inp("dpcolP", (128, 16))
    wout = p.inp("wout", (NI2, 128, D))
    xo = p.outp("xo", (NCH, 128, TOK))
    z = p.outp("z", (NCH, 128, TOK))
    x2o = p.scratch("x2s", (NCH, 128, TOK))
    uo = p.scratch("us", (NI2, 128, TOK))
    dlo = p.scratch("dls", (NI2, 128, TOK))
    zzo = p.scratch("zzs", (NI2, 128, TOK))
    bco = p.scratch("bcs", (32, TOK))
    y_dr = p.scratch("ys", (NI2, 128, TOK), BF16)
    dbgo = p.outp("dbgo", (128, 520))
    p.dbgo = dbgo
    sh_state = nc.dram_tensor("sh_state", [256, 256], F32, kind="Internal",
                              addr_space="Shared")

    p.setup_consts(vals=(LN_EPS, -1.0))
    setup_xchg(p, 1)
    ar = Arena(p, "big_arena", 211072)
    PS = p.ps("PS", (128, 1024))
    ps_a = p.ps("ps_a", (128, 512))
    ps_b = p.ps("ps_b", (128, 512))
    ps_c = p.ps("ps_c", (128, 512))
    ps_D = p.ps("ps_D", (128, 1024))
    pv = {"PS": PS, "ps_a": ps_a, "ps_b": ps_b, "ps_c": ps_c,
          "sc_ps0": PS, "sc_ps1": ps_D}

    with nc.Block() as blk:
        @blk.sync
        def _(s):
            emit_xchg_load(p, s)

    p.set_arena(ar)
    emit_mfront(p, pv, x1h, h2h, win, convw, convb, xp, dtw, dtb,
                x2o, uo, dlo, zzo, bco, consts=True)
    emit_phase_gate(p, ("stx", "stu", "std", "stz", "stb"))
    ar.reset()
    if "S" not in dbg:
        emit_scan2(p, pv, uo, dlo, zzo, bco, acolP, ebc2, dpcolP, y_dr, sh_state, 0,
                   no_xchg=("X" in dbg), no_xchg_dbg=(dbg,))
        emit_phase_gate(p, ("syst",))
    ar.reset()
    if "G" in dbg:
        with nc.Block() as blk:
            @blk.sync
            def _(s):
                i = s.dma_start(xo.ap(), x2o.ap())
                p.inc(i, "dst", 16)
                i = s.dma_start(z.ap(), x2o.ap())
                p.inc(i, "dst", 16)
                p.wait(s, "dst")
        return p
    env3 = make_ln_env(p, 512, alloc_psum=False, pref="g3")
    env3["ps_stat"] = ps_c[:, 0:8]
    env3["ps_tra"] = ps_D[0:1, 0:512]
    env3["ps_trb"] = ps_D[0:1, 512:1024]
    env3["ps_bca"] = PS[:, 0:512]
    env3["ps_bcb"] = PS[:, 512:1024]
    emit_projresln(p, env3, ps_a, x2o, y_dr, wout, xo, z, NI2, True, pref="g3")
    return p


def scan_host_weights(Alog, Dp):
    alogP = np.ascontiguousarray(
        np.asarray(Alog, np.float32).reshape(16, 128, 16).transpose(1, 0, 2)
        .reshape(128, 256))
    ebc2 = np.zeros((32, 4096), np.float32)
    for n_ in range(16):
        ebc2[n_, n_ * 128:(n_ + 1) * 128] = 1.0
        ebc2[16 + n_, 2048 + n_ * 128:2048 + (n_ + 1) * 128] = 1.0
    dpcolP = np.ascontiguousarray(np.asarray(Dp, np.float32).reshape(16, 128).T)
    return {"acolP": alogP, "ebc2": ebc2, "dpcolP": dpcolP}


RA = 3 * 1024 // 2 * 2 + 0


def build_mega(name="MEGA"):
    NI2 = 16
    RA_ = 2 * 512 + 2048
    RB_ = 3 * 256 + 2048
    HA = 2 * 512 + 1
    HB = 3 * 256 + 1
    p = Prog(name)
    nc = p.nc
    x_full = p.inp("x", (NCH, 128, S))
    xg1 = p.inp("xg1", (NCH, 128, TOK))
    wqfl = p.inp("wq", (NCH, 128, 1536))
    g1w = p.inp("g1w", (NCH, 128, D))
    lio = {}
    for st_, nl in (("A", 2), ("B", 6)):
        for l in range(nl):
            for nm, shp in (("wqkv", (NCH, 128, 3 * (512 if st_ == "A" else 1024))),
                            ("wo", ((4 if st_ == "A" else 8), 128, D)),
                            ("lncol", (128, 2 * NCH)), ("qskcol", (64, 2)),
                            ("w1p", (NCH, 128, HGN * 128)), ("b1acol", (128, HGN)),
                            ("b1gcol", (128, HGN)), ("w2p", (HP2, 128, D)),
                            ("b2col", (128, NCH))):
                lio["%s%d_%s" % (st_, l, nm)] = p.inp("%s%d_%s" % (st_, l, nm), shp)
    A_tabs = p.inp("A_tabs", (64, 4 * RA_))
    A_wkill = p.inp("A_wkill", (128, RA_ // 512))
    B_tabs = p.inp("B_tabs", (64, 4 * RB_))
    B_wkill = p.inp("B_wkill", (128, RB_ // 256))
    win = p.inp("win", (NCH, 128, 4096))
    convw = p.inp("convw", (128, 64))
    convb = p.inp("convb", (128, 16))
    xp = p.inp("xp", (NI2, 128, 96))
    dtw = p.inp("dtw", (64, 2048))
    dtb = p.inp("dtb", (128, 16))
    acolP = p.inp("acolP", (128, 256))
    ebc2 = p.inp("ebc2", (32, 4096))
    dpcolP = p.inp("dpcolP", (128, 16))
    wout = p.inp("wout", (NI2, 128, D))
    oscat_i = p.inp("oscat_idx", (128, 32), I32)
    ogat_i = p.inp("ogat_idx", (128, 32), I32)
    out = p.outp("out", (NCH, 128, TOK))
    O_sh = nc.dram_tensor("O_sh", [2 * 4096, 512], BF16, kind="Internal",
                          addr_space="Shared")
    act_dr = p.scratch("act_dr", (NCH, 128, TOK), BF16)
    x1h = p.scratch("x1h", (NCH, 128, TOK + 3))
    hA = [p.scratch("hA%d" % k, (NCH, 128, RA_ + 1)) for k in range(2)]
    hmidA = p.scratch("hmidA", (NCH, 128, RA_))
    hhatA = p.scratch("hhatA", (NCH, 128, RA_), BF16)
    x2s = p.scratch("x2s", (NCH, 128, TOK))
    us = p.scratch("us", (NI2, 128, TOK))
    dls = p.scratch("dls", (NI2, 128, TOK))
    zzs = p.scratch("zzs", (NI2, 128, TOK))
    bcs = p.scratch("bcs", (32, TOK))
    ys = p.scratch("ys", (NI2, 128, TOK), BF16)
    x3_dr = p.scratch("x3_dr", (NCH, 128, TOK))
    hB = [p.scratch("hB%d" % k, (NCH, 128, RB_ + 1)) for k in range(2)]
    hmidB = p.scratch("hmidB", (NCH, 128, RB_))
    hhatB = p.scratch("hhatB", (NCH, 128, RB_), BF16)
    sh_state = nc.dram_tensor("sh_state", [256, 256], F32, kind="Internal",
                              addr_space="Shared")
    sh_E1 = nc.dram_tensor("sh_E1", [256, NCH * (3 + HA)], F32, kind="Internal",
                           addr_space="Shared")
    sh_E2 = nc.dram_tensor("sh_E2", [256, NCH * HA], F32, kind="Internal",
                           addr_space="Shared")
    sh_E3 = nc.dram_tensor("sh_E3", [256, NCH * 3], F32, kind="Internal",
                           addr_space="Shared")
    sh_EB = [nc.dram_tensor("sh_EB%d" % k, [256, NCH * HB], F32, kind="Internal",
                            addr_space="Shared") for k in range(6)]
    NEV = 11
    p.setup_consts(vals=(LN_EPS, 1e-12, -1.0, 0.125, -0.125))
    setup_xchg(p, NEV)
    osc_sb = p.sb("osc_sb", (128, 32), I32)
    oga_sb = p.sb("oga_sb", (128, 32), I32)
    ar = Arena(p, "big_arena", 210624)
    PS = p.ps("PS", (128, 2048))
    ps_a = p.ps("ps_a", (128, 512))
    ps_b = p.ps("ps_b", (128, 512))
    ps_c = p.ps("ps_c", (128, 512))
    ps_t = p.ps("ps_t", (128, 1024), BF16)
    ALLST = ("sto", "oat", "st", "sth", "stf", "stx", "stu", "std", "stz",
             "stb", "syst", "sout", "xws", "hal")

    def gate():
        emit_phase_gate(p, ALLST)
        ar.reset()
        p.new_phase()

    with nc.Block() as blk:
        @blk.sync
        def _(s):
            emit_xchg_load(p, s)
            i = s.dma_start(osc_sb[:, :], oscat_i.ap())
            p.inc(i, "xld", 16)
            i = s.dma_start(oga_sb[:, :], ogat_i.ap())
            p.inc(i, "xld", 16)

    p.set_arena(ar)
    pvf = {"PS": PS[:, 0:512], "ps_a": ps_a, "ps_b": ps_b, "ps_c": ps_c,
           "ps_t": ps_t}
    emit_flash(p, pvf, x_full, wqfl, None, consts=True, oscat=(O_sh, osc_sb))
    with nc.Block() as blk:
        @blk.gpsimd
        def _(g):
            g.wait_ge(p.sem("sto"), p.cnt["sto"])
            emit_xflag(p, g, 0)
    gate()
    og_sb = p.sb("og_sb", (128, 1024), BF16)
    at_sb = p.sb("at_sb", (128, 1024), BF16)
    with nc.Block() as blk:
        @blk.gpsimd
        def _(g):
            emit_xpoll(p, g, 0, "xpl0")
        @blk.vector
        def _(v):
            v.wait_ge(p.sem("xpl0"), p.cnt.get("xpl0", 0))
            p.semtick(v, "xpl0v")
    for tt in range(16):
        c_og = p.cnt.get("ogl", 0)
        c_otr = p.cnt.get("oatr", 0)
        c_oev = p.cnt.get("oaev", 0)
        c_ost = p.cnt.get("oat", 0)
        with nc.Block() as blk:
            @blk.gpsimd
            def _(g, tt=tt, c_oev=c_oev):
                if c_oev:
                    g.wait_ge(p.sem("oaev"), c_oev)
                for s_ in range(2):
                    i = g.indirect_dma_start(
                        og_sb[:, s_ * 512:(s_ + 1) * 512], None, O_sh.ap(),
                        bass.IndirectOffsetOnAxis(ap=oga_sb[:, 2 * tt + s_:2 * tt + s_ + 1],
                                                  axis=0))
                    p.inc(i, "ogl", 16)
            @blk.tensor
            def _(t, c_og=c_og, c_oev=c_oev):
                t.wait_ge(p.sem("ogl"), c_og + 32)
                if c_oev:
                    t.wait_ge(p.sem("oaev"), c_oev)
                for c8 in range(8):
                    i = t.transpose(ps_t[:, c8 * 128:(c8 + 1) * 128],
                                    og_sb[:, c8 * 128:(c8 + 1) * 128], p.identb[:, :])
                    p.inc(i, "oatr")
            @blk.vector
            def _(v, c_otr=c_otr, c_ost=c_ost):
                v.wait_ge(p.sem("oatr"), c_otr + 8)
                if c_ost:
                    v.wait_ge(p.sem("oat"), c_ost)
                i = v.tensor_copy(at_sb[:, :], ps_t[:, 0:1024])
                p.inc(i, "oaev")
            @blk.sync
            def _(s, tt=tt):
                s.wait_ge(p.sem("oaev"), p.cnt["oaev"])
                i = s.dma_start(act_dr.ap()[:, :, tt * 128:(tt + 1) * 128].transpose([1, 0, 2]),
                                at_sb.ap().rearrange("p (c n) -> p c n", c=NCH))
                p.inc(i, "oat", 16)
    gate()
    env1 = make_ln_env(p, 512, alloc_psum=False, pref="g1")
    env1["ps_stat"] = ps_c[:, 0:8]
    env1["ps_tra"] = PS[0:1, 1024:1536]
    env1["ps_trb"] = PS[0:1, 1536:2048]
    env1["ps_bca"] = PS[:, 0:512]
    env1["ps_bcb"] = PS[:, 512:1024]
    emit_projresln(p, env1, ps_a, xg1, act_dr, g1w,
                   ArView(x1h.ap()[:, :, 3:3 + TOK]),
                   ArView(hA[0].ap()[:, :, HA:HA + TOK]), NCH, True, pref="g1")
    gate()
    exw = p.sb("exw", (128, NCH * (3 + HA)), F32)
    CW1 = 3 + HA
    with nc.Block() as blk:
        @blk.sync
        def _(s):
            p.wait(s, "st")
            i = s.dma_start(exw.ap().rearrange("p (c n) -> p c n", c=NCH)[:, :, 0:3],
                            x1h.ap()[:, :, TOK:TOK + 3].transpose([1, 0, 2]))
            p.inc(i, "hal", 16)
            i = s.dma_start(exw.ap().rearrange("p (c n) -> p c n", c=NCH)[:, :, 3:CW1],
                            hA[0].ap()[:, :, TOK:TOK + HA].transpose([1, 0, 2]))
            p.inc(i, "hal", 16)
        @blk.gpsimd
        def _(g):
            g.wait_ge(p.sem("hal"), p.cnt["hal"])
            emit_xsend(p, g, sh_E1, exw[:, :])
            emit_xflag(p, g, 1)
            emit_xpoll(p, g, 1, "xpl")
            emit_xrecv(p, g, sh_E1, exw[:, :])
            g.wait_ge(p.sem("xws"), p.cnt["xws"])
            p.semtick(g, "xrc")
        @blk.vector
        def _(v):
            p.wait(v, "xrc")
            i = v.tensor_scalar(exw[:, :], exw[:, :], p.xk_sb[:, 0:1], None, ALU.mult)
            p.inc(i, "hmk")
        @blk.sync
        def _(s):
            p.wait(s, "hmk")
            i = s.dma_start(x1h.ap()[:, :, 0:3].transpose([1, 0, 2]),
                            exw.ap().rearrange("p (c n) -> p c n", c=NCH)[:, :, 0:3])
            p.inc(i, "hal", 16)
            i = s.dma_start(hA[0].ap()[:, :, 0:HA].transpose([1, 0, 2]),
                            exw.ap().rearrange("p (c n) -> p c n", c=NCH)[:, :, 3:CW1])
            p.inc(i, "hal", 16)
            p.wait(s, "hal")
    gate()

    def tail_xchg(sh, src_ap_cols, dst_ap_cols, ncols, ev):
        tsb = p.sb("tx%d" % ev, (128, NCH * ncols), F32)
        with nc.Block() as blk:
            @blk.sync
            def _(s):
                for nm in ALLST:
                    if p.cnt.get(nm, 0):
                        s.wait_ge(p.sem(nm), p.cnt[nm])
                i = s.dma_start(tsb.ap().rearrange("p (c n) -> p c n", c=NCH),
                                src_ap_cols.transpose([1, 0, 2]))
                p.inc(i, "hal", 16)
            @blk.gpsimd
            def _(g):
                g.wait_ge(p.sem("hal"), p.cnt["hal"])
                emit_xsend(p, g, sh, tsb[:, :])
                emit_xflag(p, g, ev)
                emit_xpoll(p, g, ev, "xpl")
                emit_xrecv(p, g, sh, tsb[:, :])
                g.wait_ge(p.sem("xws"), p.cnt["xws"])
                p.semtick(g, "xrc")
            @blk.vector
            def _(v):
                p.wait(v, "xrc")
                i = v.tensor_scalar(tsb[:, :], tsb[:, :], p.xk_sb[:, 0:1], None,
                                    ALU.mult)
                p.inc(i, "hmk")
            @blk.sync
            def _(s):
                p.wait(s, "hmk")
                i = s.dma_start(dst_ap_cols.transpose([1, 0, 2]),
                                tsb.ap().rearrange("p (c n) -> p c n", c=NCH))
                p.inc(i, "hal", 16)
                p.wait(s, "hal")

    pvl = {"PS": PS, "ps_a": ps_a, "ps_b": ps_b, "ps_c": ps_c, "ps_t": ps_t}
    for l in range(2):
        io = {k: lio["A%d_%s" % (l, k)] for k in
              ("wqkv", "wo", "lncol", "qskcol", "w1p", "b1acol", "b1gcol",
               "w2p", "b2col")}
        io["tabs"] = A_tabs
        io["wkill"] = A_wkill
        io["h"] = hA[l % 2]
        io["hout"] = ArView(hA[(l + 1) % 2].ap()[:, :, HA:HA + TOK])
        io["hmid_dr"] = hmidA
        io["hhat_dr"] = hhatA
        emit_layer(p, pvl, 512, 8, io)
        gate()
        if l == 0:
            tail_xchg(sh_E2, hA[1].ap()[:, :, TOK:TOK + HA],
                      hA[1].ap()[:, :, 0:HA], HA, 2)
            gate()
    tail_xchg(sh_E3, hA[0].ap()[:, :, TOK + HA - 3:TOK + HA],
              hA[0].ap()[:, :, HA - 3:HA], 3, 3)
    gate()
    pv2 = {"PS": PS[:, 0:1024], "ps_a": ps_a, "ps_b": ps_b, "ps_c": ps_c}
    emit_mfront(p, pv2, x1h, ArView(hA[0].ap()[:, :, HA - 3:HA + TOK]),
                win, convw, convb, xp, dtw, dtb, x2s, us, dls, zzs, bcs)
    gate()
    pv3 = {"sc_ps0": PS[:, 0:1024], "sc_ps1": PS[:, 1024:2048]}
    emit_scan2(p, pv3, us, dls, zzs, bcs, acolP, ebc2, dpcolP, ys, sh_state, 4)
    gate()
    env3 = make_ln_env(p, 512, alloc_psum=False, pref="g3")
    env3["ps_stat"] = ps_c[:, 0:8]
    env3["ps_tra"] = PS[0:1, 1024:1536]
    env3["ps_trb"] = PS[0:1, 1536:2048]
    env3["ps_bca"] = PS[:, 0:512]
    env3["ps_bcb"] = PS[:, 512:1024]
    emit_projresln(p, env3, ps_a, x2s, ys, wout, x3_dr,
                   ArView(hB[0].ap()[:, :, HB:HB + TOK]), NI2, True, pref="g3")
    gate()
    tail_xchg(sh_EB[0], hB[0].ap()[:, :, TOK:TOK + HB],
              hB[0].ap()[:, :, 0:HB], HB, 5)
    gate()
    for l in range(6):
        io = {k: lio["B%d_%s" % (l, k)] for k in
              ("wqkv", "wo", "lncol", "qskcol", "w1p", "b1acol", "b1gcol",
               "w2p", "b2col")}
        io["tabs"] = B_tabs
        io["wkill"] = B_wkill
        io["h"] = hB[l % 2]
        io["hout"] = ArView(hB[(l + 1) % 2].ap()[:, :, HB:HB + TOK])
        io["hmid_dr"] = hmidB
        io["hhat_dr"] = hhatB
        emit_layer(p, pvl, 256, 16, io)
        gate()
        if l < 5:
            tail_xchg(sh_EB[l + 1], hB[(l + 1) % 2].ap()[:, :, TOK:TOK + HB],
                      hB[(l + 1) % 2].ap()[:, :, 0:HB], HB, 6 + l)
            gate()
    hfin = hB[0]
    ga = p.sb("g4a", (128, NCH * 512), F32)
    gb = p.sb("g4b", (128, NCH * 512), F32)
    go = p.sb("g4o", (128, NCH * 512), F32)
    for it in range(TOK // 512):
        t0 = it * 512
        c_g4 = p.cnt.get("g4e", 0)
        with nc.Block() as blk:
            @blk.sync
            def _(s, t0=t0, c_g4=c_g4):
                if c_g4:
                    s.wait_ge(p.sem("g4e"), c_g4)
                i = s.dma_start(ga.ap().rearrange("p (c n) -> p c n", c=NCH),
                                x3_dr.ap()[:, :, t0:t0 + 512].transpose([1, 0, 2]))
                p.inc(i, "g4l", 16)
                i = s.dma_start(gb.ap().rearrange("p (c n) -> p c n", c=NCH),
                                hfin.ap()[:, :, HB + t0:HB + t0 + 512].transpose([1, 0, 2]))
                p.inc(i, "g4l", 16)
            @blk.vector
            def _(v, c_g4=c_g4):
                v.wait_ge(p.sem("g4l"), p.cnt["g4l"])
                if p.cnt.get("sout", 0):
                    v.wait_ge(p.sem("sout"), p.cnt["sout"])
                i = v.tensor_add(go[:, :], ga[:, :], gb[:, :])
                p.inc(i, "g4e")
            @blk.sync
            def _(s, t0=t0):
                s.wait_ge(p.sem("g4e"), p.cnt["g4e"])
                i = s.dma_start(out.ap()[:, :, t0:t0 + 512].transpose([1, 0, 2]),
                                go.ap().rearrange("p (c n) -> p c n", c=NCH))
                p.inc(i, "sout", 16)
    with nc.Block() as blk:
        @blk.sync
        def _(s):
            s.wait_ge(p.sem("sout"), p.cnt["sout"])
        @blk.gpsimd
        def _(g):
            g.wait_ge(p.sem("sout"), p.cnt["sout"])
    return p


_PROGS = {}

def _prog(key, fn):
    if key not in _PROGS:
        _PROGS[key] = fn()
    return _PROGS[key]


def _run(p, in_maps, n=8):
    return run_bass_kernel_spmd(p.nc, in_maps, core_ids=list(range(n))).results


def _assemble_tok(results, key, nch=NCH):
    full = np.zeros((B, S, nch * 128), np.float32)
    for b_ in range(B):
        for hf in range(2):
            full[b_, hf * TOK:(hf + 1) * TOK] = unfm(results[b_ * 2 + hf][key].astype(np.float32))
    return full


def kernel(L0_attn_x, L0_attn_W_qkv, L0_attn_W_o, L0_ffn_wqkv, L0_ffn_q_scale,
           L0_ffn_k_scale, L0_ffn_wo, L0_ffn_ln_g, L0_ffn_ln_b, L0_ffn_w1,
           L0_ffn_b1, L0_ffn_w2, L0_ffn_b2, L1_attn_in_proj_w, L1_attn_conv_w,
           L1_attn_conv_b, L1_attn_x_proj_w, L1_attn_dt_proj_w, L1_attn_dt_proj_b,
           L1_attn_A_log, L1_attn_Dp, L1_attn_out_proj_w, L1_ffn_Wqkv,
           L1_ffn_q_scale, L1_ffn_k_scale, L1_ffn_Wo, L1_ffn_ln_g, L1_ffn_ln_b,
           L1_ffn_W1, L1_ffn_b1, L1_ffn_W2, L1_ffn_b2):
    x = np.asarray(L0_attn_x, np.float32)
    pm = _prog("MEGA", build_mega)
    Wq = np.asarray(L0_attn_W_qkv, np.float32)
    wo_ch = wchunks(np.asarray(L0_attn_W_o, np.float32))
    wdA = [layer_weights(512, 8, np.asarray(L0_ffn_wqkv[l], np.float32),
                         np.asarray(L0_ffn_q_scale[l], np.float32),
                         np.asarray(L0_ffn_k_scale[l], np.float32),
                         np.asarray(L0_ffn_wo[l], np.float32),
                         np.asarray(L0_ffn_ln_g[l], np.float32),
                         np.asarray(L0_ffn_ln_b[l], np.float32),
                         np.asarray(L0_ffn_w1[l], np.float32),
                         np.asarray(L0_ffn_b1[l], np.float32),
                         np.asarray(L0_ffn_w2[l], np.float32),
                         np.asarray(L0_ffn_b2[l], np.float32)) for l in range(2)]
    wdB = [layer_weights(256, 16, np.asarray(L1_ffn_Wqkv[l], np.float32),
                         np.asarray(L1_ffn_q_scale[l], np.float32),
                         np.asarray(L1_ffn_k_scale[l], np.float32),
                         np.asarray(L1_ffn_Wo[l], np.float32),
                         np.asarray(L1_ffn_ln_g[l], np.float32),
                         np.asarray(L1_ffn_ln_b[l], np.float32),
                         np.asarray(L1_ffn_W1[l], np.float32),
                         np.asarray(L1_ffn_b1[l], np.float32),
                         np.asarray(L1_ffn_W2[l], np.float32),
                         np.asarray(L1_ffn_b2[l], np.float32)) for l in range(6)]
    win_t = np.ascontiguousarray(np.asarray(L1_attn_in_proj_w, np.float32).T)
    cw = np.asarray(L1_attn_conv_w, np.float32)[:, 0, :]
    convw_col = np.zeros((128, 64), np.float32)
    for c in range(16):
        for tau in range(4):
            convw_col[:, c * 4 + tau] = cw[c * 128:(c + 1) * 128, tau]
    wd2 = {"win": wchunks(win_t), "convw": convw_col,
           "convb": np.ascontiguousarray(np.asarray(L1_attn_conv_b, np.float32).reshape(16, 128).T),
           "xp": wchunks(np.ascontiguousarray(np.asarray(L1_attn_x_proj_w, np.float32).T)),
           "dtw": np.ascontiguousarray(np.asarray(L1_attn_dt_proj_w, np.float32).T),
           "dtb": np.ascontiguousarray(np.asarray(L1_attn_dt_proj_b, np.float32).reshape(16, 128).T)}
    wd2.update(scan_host_weights(L1_attn_A_log, L1_attn_Dp))
    wd2["wout"] = wchunks(np.ascontiguousarray(np.asarray(L1_attn_out_proj_w, np.float32).T))
    rng = np.random.default_rng()
    mg11 = rng.integers(1 << 20, 1 << 30, size=11).astype(np.int32)
    tabsA = [layer_tables(512, hf * TOK) for hf in range(2)]
    tabsB = [layer_tables(256, hf * TOK) for hf in range(2)]
    ins = []
    for b_ in range(B):
        for hf in range(2):
            d_ = {}
            d_["x"] = fm(x[b_])
            d_["xg1"] = fm(x[b_, hf * TOK:(hf + 1) * TOK])
            hg = hf
            wsl = np.concatenate([Wq[:, hg * 512:(hg + 1) * 512],
                                  Wq[:, 1024 + hg * 512:1024 + (hg + 1) * 512],
                                  Wq[:, 2048 + hg * 512:2048 + (hg + 1) * 512]], 1)
            d_["wq"] = wchunks(wsl)
            d_["g1w"] = np.ascontiguousarray(np.concatenate(
                [wo_ch[hg * 4:hg * 4 + 4], wo_ch[(1 - hg) * 4:(1 - hg) * 4 + 4]], 0))
            for st_, wds, W_ in (("A", wdA, 512), ("B", wdB, 256)):
                for l, wd in enumerate(wds):
                    for k, v in wd.items():
                        d_["%s%d_%s" % (st_, l, k)] = v
            HALO_A, HALO_B = 2, 3
            wkA = np.zeros((128, (2 * 512 + 2048) // 512), np.float32)
            wkB = np.zeros((128, (3 * 256 + 2048) // 256), np.float32)
            if hf == 0:
                wkA[:, HALO_A] = NEG
                wkB[:, HALO_B] = NEG
            d_["A_tabs"] = tabsA[hf]
            d_["A_wkill"] = wkA
            d_["B_tabs"] = tabsB[hf]
            d_["B_wkill"] = wkB
            d_.update(wd2)
            d_.update(xchg_host_inputs(hf, 11, rng, mg11))
            me, pa = hf, hf ^ 1
            osc = np.zeros((128, 32), np.int32)
            oga = np.zeros((128, 32), np.int32)
            for qt in range(32):
                osc[:, qt] = me * 4096 + qt * 128 + np.arange(128)
            for tt in range(16):
                for s_ in range(2):
                    slot = me if s_ == 0 else pa
                    oga[:, 2 * tt + s_] = slot * 4096 + hf * 2048 + tt * 128 + np.arange(128)
            d_["oscat_idx"] = osc
            d_["ogat_idx"] = oga
            ins.append(d_)
    r = _run(pm, ins)
    return _assemble_tok(r, "out").astype(np.float32)


def kernel_multi(L0_attn_x, L0_attn_W_qkv, L0_attn_W_o, L0_ffn_wqkv, L0_ffn_q_scale,
           L0_ffn_k_scale, L0_ffn_wo, L0_ffn_ln_g, L0_ffn_ln_b, L0_ffn_w1,
           L0_ffn_b1, L0_ffn_w2, L0_ffn_b2, L1_attn_in_proj_w, L1_attn_conv_w,
           L1_attn_conv_b, L1_attn_x_proj_w, L1_attn_dt_proj_w, L1_attn_dt_proj_b,
           L1_attn_A_log, L1_attn_Dp, L1_attn_out_proj_w, L1_ffn_Wqkv,
           L1_ffn_q_scale, L1_ffn_k_scale, L1_ffn_Wo, L1_ffn_ln_g, L1_ffn_ln_b,
           L1_ffn_W1, L1_ffn_b1, L1_ffn_W2, L1_ffn_b2):
    x = np.asarray(L0_attn_x, np.float32)

    pf = _prog("flash", build_flash)
    Wq = np.asarray(L0_attn_W_qkv, np.float32)
    ins = []
    for b_ in range(B):
        for hg in range(2):
            wsl = np.concatenate([Wq[:, hg * 512:(hg + 1) * 512],
                                  Wq[:, 1024 + hg * 512:1024 + (hg + 1) * 512],
                                  Wq[:, 2048 + hg * 512:2048 + (hg + 1) * 512]], 1)
            ins.append({"x": fm(x[b_]), "wq": wchunks(wsl)})
    rf = _run(pf, ins)
    O_full = np.zeros((B, S, D), np.float32)
    for b_ in range(B):
        for hg in range(2):
            O_full[b_, :, hg * 512:(hg + 1) * 512] = \
                rf[b_ * 2 + hg]["O"].astype(np.float32).reshape(S, 512)

    pg1 = _prog("G1", lambda: build_proj_res_ln("G1", NCH, False))
    wo_ch = wchunks(np.asarray(L0_attn_W_o, np.float32))
    ins = []
    for b_ in range(B):
        for hf in range(2):
            sl = slice(hf * TOK, (hf + 1) * TOK)
            ins.append({"x": fm(x[b_, sl]), "act": fm(O_full[b_, sl]), "w": wo_ch})
    r1 = _run(pg1, ins)
    x1 = _assemble_tok(r1, "xo")
    h = _assemble_tok(r1, "z")

    pa = _prog("layerA", lambda: build_layer("layerA", 512, 8))
    for l in range(2):
        wd = layer_weights(512, 8, np.asarray(L0_ffn_wqkv[l], np.float32),
                           np.asarray(L0_ffn_q_scale[l], np.float32),
                           np.asarray(L0_ffn_k_scale[l], np.float32),
                           np.asarray(L0_ffn_wo[l], np.float32),
                           np.asarray(L0_ffn_ln_g[l], np.float32),
                           np.asarray(L0_ffn_ln_b[l], np.float32),
                           np.asarray(L0_ffn_w1[l], np.float32),
                           np.asarray(L0_ffn_b1[l], np.float32),
                           np.asarray(L0_ffn_w2[l], np.float32),
                           np.asarray(L0_ffn_b2[l], np.float32))
        ins = []
        for b_ in range(B):
            for hf in range(2):
                d_ = dict(wd)
                d_.update(layer_core_inputs(512, h[b_], hf * TOK, hf == 0))
                ins.append(d_)
        rl = _run(pa, ins)
        h = _assemble_tok(rl, "hout")

    pgs = _prog("G2S3", build_g2sg3)
    win_t = np.ascontiguousarray(np.asarray(L1_attn_in_proj_w, np.float32).T)
    cw = np.asarray(L1_attn_conv_w, np.float32)[:, 0, :]
    convw_col = np.zeros((128, 64), np.float32)
    for c in range(16):
        for tau in range(4):
            convw_col[:, c * 4 + tau] = cw[c * 128:(c + 1) * 128, tau]
    wd2 = {"win": wchunks(win_t), "convw": convw_col,
           "convb": np.ascontiguousarray(np.asarray(L1_attn_conv_b, np.float32).reshape(16, 128).T),
           "xp": wchunks(np.ascontiguousarray(np.asarray(L1_attn_x_proj_w, np.float32).T)),
           "dtw": np.ascontiguousarray(np.asarray(L1_attn_dt_proj_w, np.float32).T),
           "dtb": np.ascontiguousarray(np.asarray(L1_attn_dt_proj_b, np.float32).reshape(16, 128).T)}
    wd2.update(scan_host_weights(L1_attn_A_log, L1_attn_Dp))
    wd2["wout"] = wchunks(np.ascontiguousarray(np.asarray(L1_attn_out_proj_w, np.float32).T))
    rng = np.random.default_rng()
    mg1 = rng.integers(1 << 20, 1 << 30, size=1).astype(np.int32)
    ins = []
    for b_ in range(B):
        for hf in range(2):
            lo = hf * TOK - 3
            x1h = np.zeros((TOK + 3, D), np.float32)
            h2h = np.zeros((TOK + 3, D), np.float32)
            x1h[max(0, -lo):] = x1[b_, max(lo, 0): lo + TOK + 3]
            h2h[max(0, -lo):] = h[b_, max(lo, 0): lo + TOK + 3]
            d_ = dict(wd2); d_["x1h"] = fm(x1h); d_["h2h"] = fm(h2h)
            d_.update(xchg_host_inputs(hf, 1, rng, mg1))
            ins.append(d_)
    r3 = _run(pgs, ins)
    x3 = _assemble_tok(r3, "xo")
    h = _assemble_tok(r3, "z")

    pb = _prog("layerB", lambda: build_layer("layerB", 256, 16))
    for l in range(6):
        wd = layer_weights(256, 16, np.asarray(L1_ffn_Wqkv[l], np.float32),
                           np.asarray(L1_ffn_q_scale[l], np.float32),
                           np.asarray(L1_ffn_k_scale[l], np.float32),
                           np.asarray(L1_ffn_Wo[l], np.float32),
                           np.asarray(L1_ffn_ln_g[l], np.float32),
                           np.asarray(L1_ffn_ln_b[l], np.float32),
                           np.asarray(L1_ffn_W1[l], np.float32),
                           np.asarray(L1_ffn_b1[l], np.float32),
                           np.asarray(L1_ffn_W2[l], np.float32),
                           np.asarray(L1_ffn_b2[l], np.float32))
        ins = []
        for b_ in range(B):
            for hf in range(2):
                d_ = dict(wd)
                d_.update(layer_core_inputs(256, h[b_], hf * TOK, hf == 0))
                ins.append(d_)
        rl = _run(pb, ins)
        h = _assemble_tok(rl, "hout")

    pg4 = _prog("G4", build_add)
    ins = []
    for b_ in range(B):
        for hf in range(2):
            sl = slice(hf * TOK, (hf + 1) * TOK)
            ins.append({"a": fm(x3[b_, sl]), "b": fm(h[b_, sl])})
    r5 = _run(pg4, ins)
    out = _assemble_tok(r5, "o")
    return out.astype(np.float32)


# version 09 (final)
# speedup vs baseline: 1.1689x; 1.1689x over previous
import numpy as np
import ml_dtypes
import concourse.bass as bass
import concourse.mybir as mybir
from concourse.bass_utils import run_bass_kernel_spmd

F32 = mybir.dt.float32
BF16 = mybir.dt.bfloat16
AF = mybir.ActivationFunctionType
ALU = mybir.AluOpType
AX = mybir.AxisListType

B, S, D = 4, 4096, 1024
NCH = D // 128
TOK = 2048
NEG = -1e30
LN_EPS = 1e-5


def fm(x):
    r, d = x.shape
    return np.ascontiguousarray(x.T.reshape(d // 128, 128, r))

def unfm(x):
    n, _, r = x.shape
    return np.ascontiguousarray(x.reshape(n * 128, r).T)

def wchunks(w):
    din, dout = w.shape
    return np.ascontiguousarray(w.reshape(din // 128, 128, dout))


class ArView:
    def __init__(self, ap):
        self._ap = ap

    def ap(self):
        return self._ap

    def __getitem__(self, idx):
        return self._ap[idx]


class Arena:
    def __init__(self, p, name, nbytes):
        self.base = p.nc.alloc_sbuf_tensor(name, [128, nbytes // 4], mybir.dt.float32)
        self.nbytes = nbytes
        self.off = 0

    def reset(self):
        self.off = 0

    def take(self, shape, dtype):
        rows, cols = shape
        es = mybir.dt.size(dtype)
        off = (self.off + 31) & ~31
        need = cols * es
        assert off + need <= self.nbytes, (
            "arena overflow: %d + %d > %d" % (off, need, self.nbytes))
        self.off = off + need
        ap = self.base.ap()
        if dtype != mybir.dt.float32:
            ap = ap.bitcast(dtype)
        return ArView(ap[0:rows, off // es: off // es + cols])


GLOBAL_SEMS = {"ser", "scser", "xld", "xws", "xpl", "xpl0", "xpl0v", "xrc",
               "hal", "hmk", "phg", "phsy", "sto", "oat", "st", "sth", "stf",
               "stx", "stu", "std", "stz", "stb", "syst", "sout"}
SEM_POOL_N = 70


class CntMap:
    def __init__(self, prog):
        self.p = prog
        self.d = {}

    def get(self, k, dflt=0):
        return self.d.get(self.p._csem(k), dflt)

    def __getitem__(self, k):
        return self.d[self.p._csem(k)]

    def __setitem__(self, k, v):
        self.d[self.p._csem(k)] = v

    def __contains__(self, k):
        return self.p._csem(k) in self.d


class Prog:
    def __init__(self, name="prog"):
        self.nc = bass.Bass("TRN2", target_bir_lowering=False, name=name)
        self.cnt = CntMap(self)
        self.sems = {}
        self.canon = {}
        self.slot_rr = 0
        self.arena = None

    def _csem(self, name):
        if name in GLOBAL_SEMS or name.startswith("zp"):
            return name
        s = self.canon.get(name)
        if s is None:
            s = "zp%d" % (self.slot_rr % SEM_POOL_N)
            self.slot_rr += 1
            self.canon[name] = s
        return s

    def new_phase(self):
        self.canon = {}

    def sem(self, name):
        cn = self._csem(name)
        if cn not in self.sems:
            self.sems[cn] = self.nc.alloc_semaphore(cn)
            if cn not in self.cnt:
                self.cnt[name] = 0
        return self.sems[cn]

    def inc(self, instr, name, k=1):
        s = self.sem(name)
        instr.then_inc(s, k)
        self.cnt[name] += k
        return self.cnt[name]

    def wait(self, eng, name, val=None):
        eng.wait_ge(self.sem(name), self.cnt[name] if val is None else val)

    def semtick(self, eng, name, k=1):
        eng.sem_inc(self.sem(name), k)
        self.cnt[name] = self.cnt.get(name, 0) + k
        return self.cnt[name]

    def serial(self, eng, instr, tag="ser"):
        n = self.inc(instr, tag)
        eng.wait_ge(self.sem(tag), n)

    def sb(self, name, shape, dtype):
        ar = getattr(self, "arena", None)
        if ar is not None:
            return ar.take(shape, dtype)
        return self.nc.alloc_sbuf_tensor(name, list(shape), dtype)

    def set_arena(self, ar):
        self.arena = ar

    def ps(self, name, shape, dtype=F32):
        return self.nc.alloc_psum_tensor(name, list(shape), dtype)

    def inp(self, name, shape, dtype=F32):
        return self.nc.dram_tensor(name, list(shape), dtype, kind="ExternalInput")

    def outp(self, name, shape, dtype=F32):
        return self.nc.dram_tensor(name, list(shape), dtype, kind="ExternalOutput")

    def scratch(self, name, shape, dtype=F32):
        return self.nc.dram_tensor(name, list(shape), dtype, kind="Internal")

    def setup_consts(self, vals=()):
        self.cvals = list(vals)
        self.ccol = {}
        for k, v in enumerate(self.cvals):
            self.ccol[v] = self.sb("c_cc%d" % k, (128, 1), F32)
        self.ones_col = self.sb("c_ones_col", (128, 1), F32)
        self.ones_colb = self.sb("c_ones_colb", (128, 1), BF16)
        self.ones_row = self.sb("c_ones_row", (1, 128), F32)
        self.ones_rowb = self.sb("c_ones_rowb", (1, 128), BF16)
        self.identb = self.sb("c_identb", (128, 128), BF16)
        self.identf = self.sb("c_identf", (128, 128), F32)

    def emit_consts(self, g):
        for v, t in self.ccol.items():
            g.memset(t[:, :], float(v))
        g.memset(self.ones_col[:, :], 1.0)
        g.memset(self.ones_colb[:, :], 1.0)
        g.memset(self.ones_row[:, :], 1.0)
        g.memset(self.ones_rowb[:, :], 1.0)
        g.memset(self.identb[:, :], 1.0)
        i = g.memset(self.identf[:, :], 1.0)
        self.serial(g, i)
        g.affine_select(self.identb[:, :], self.identb[:, :], [[-1, 128]], ALU.is_equal,
                        0.0, base=0, channel_multiplier=1)
        i = g.affine_select(self.identf[:, :], self.identf[:, :], [[-1, 128]], ALU.is_equal,
                        0.0, base=0, channel_multiplier=1)
        self.serial(g, i)


def emit_ln(p, env, src, dst, TT, gcol=None, bcol=None, dstb=None):
    nc = p.nc
    ntc = TT // 128
    b = {nm: p.cnt.get(nm, 0) for nm in
         ("lnsq", "lnst", "lnv", "lns", "lnr", "lnt", "lnrw", "lnbc", "lnap")}
    with nc.Block() as blk:
        @blk.scalar
        def _(sc):
            if b["lnap"]:
                sc.wait_ge(p.sem("lnap"), b["lnap"])
            for c in range(NCH):
                i = sc.activation(env["xsq"][:, c * TT:(c + 1) * TT], src(c), AF.Square)
            p.inc(i, "lnsq")
            sc.wait_ge(p.sem("lnv"), b["lnv"] + 1)
            i = sc.activation(env["scol"][:, 0:ntc], env["vcol"][:, 0:ntc], AF.Sqrt,
                              bias=p.ccol[LN_EPS][:, 0:1])
            p.inc(i, "lns")
        @blk.tensor
        def _(t):
            t.wait_ge(p.sem("lnsq"), b["lnsq"] + 1)
            for tc in range(ntc):
                for c in range(NCH):
                    t.matmul(env["ps_stat"][:, 2 * tc:2 * tc + 1],
                             src(c)[:, tc * 128:(tc + 1) * 128],
                             p.ones_col[:, :], start=(c == 0), stop=(c == NCH - 1))
                for c in range(NCH):
                    i = t.matmul(env["ps_stat"][:, 2 * tc + 1:2 * tc + 2],
                                 env["xsq"][:, c * TT + tc * 128: c * TT + (tc + 1) * 128],
                                 p.ones_col[:, :], start=(c == 0), stop=(c == NCH - 1))
            p.inc(i, "lnst")
            t.wait_ge(p.sem("lnr"), b["lnr"] + 1)
            for tc in range(ntc):
                t.transpose(env["ps_tra"][:, tc * 128:(tc + 1) * 128],
                            env["rcol"][:, tc:tc + 1], p.identf[:, :])
                i = t.transpose(env["ps_trb"][:, tc * 128:(tc + 1) * 128],
                                env["ncol"][:, tc:tc + 1], p.identf[:, :])
            p.inc(i, "lnt")
            t.wait_ge(p.sem("lnrw"), b["lnrw"] + 1)
            t.matmul(env["ps_bca"][:, 0:TT], p.ones_row[:, :], env["rows_a"][:, 0:TT],
                     start=True, stop=True)
            i = t.matmul(env["ps_bcb"][:, 0:TT], p.ones_row[:, :], env["rows_b"][:, 0:TT],
                         start=True, stop=True)
            p.inc(i, "lnbc")
        @blk.vector
        def _(v):
            v.wait_ge(p.sem("lnst"), b["lnst"] + 1)
            stp = env["ps_stat"]
            v.tensor_scalar(env["mcol"][:, 0:ntc], stp[:, 0:2 * ntc:2], 1.0 / D, None, ALU.mult)
            i = v.tensor_scalar(env["qcol"][:, 0:ntc], stp[:, 1:2 * ntc:2], 1.0 / D, None, ALU.mult)
            p.serial(v, i)
            i = v.tensor_mul(env["tcol"][:, 0:ntc], env["mcol"][:, 0:ntc], env["mcol"][:, 0:ntc])
            p.serial(v, i)
            i = v.tensor_sub(env["vcol"][:, 0:ntc], env["qcol"][:, 0:ntc], env["tcol"][:, 0:ntc])
            p.inc(i, "lnv")
            v.wait_ge(p.sem("lns"), b["lns"] + 1)
            i = v.reciprocal(env["rcol"][:, 0:ntc], env["scol"][:, 0:ntc])
            p.serial(v, i)
            i = v.tensor_mul(env["tcol"][:, 0:ntc], env["mcol"][:, 0:ntc], env["rcol"][:, 0:ntc])
            p.serial(v, i)
            i = v.tensor_scalar(env["ncol"][:, 0:ntc], env["tcol"][:, 0:ntc], -1.0, None, ALU.mult)
            p.inc(i, "lnr")
            v.wait_ge(p.sem("lnt"), b["lnt"] + 1)
            v.tensor_copy(env["rows_a"][:, 0:TT], env["ps_tra"][0:1, 0:TT])
            i = v.tensor_copy(env["rows_b"][:, 0:TT], env["ps_trb"][0:1, 0:TT])
            p.inc(i, "lnrw")
            v.wait_ge(p.sem("lnbc"), b["lnbc"] + 1)
            for c in range(NCH):
                i = v.tensor_mul(env["xsq"][:, c * TT:(c + 1) * TT], src(c), env["ps_bca"][:, 0:TT])
            p.serial(v, i)
            if gcol is None:
                for c in range(NCH):
                    i = v.tensor_add(dst(c), env["xsq"][:, c * TT:(c + 1) * TT], env["ps_bcb"][:, 0:TT])
            else:
                for c in range(NCH):
                    i = v.tensor_add(env["xsq"][:, c * TT:(c + 1) * TT],
                                     env["xsq"][:, c * TT:(c + 1) * TT], env["ps_bcb"][:, 0:TT])
                p.serial(v, i)
                for c in range(NCH):
                    i = v.tensor_scalar(dst(c), env["xsq"][:, c * TT:(c + 1) * TT],
                                        gcol(c), bcol(c), ALU.mult, ALU.add)
            if dstb is not None:
                p.serial(v, i)
                for c in range(NCH):
                    i = v.tensor_copy(dstb(c), dst(c))
            p.inc(i, "lnap")


def make_ln_env(p, TT=512, alloc_psum=True, pref=""):
    env = {}
    ntc = TT // 128
    env["xsq"] = p.sb(pref + "ln_xsq", (128, NCH * TT), F32)
    for nm in ("mcol", "qcol", "tcol", "vcol", "scol", "rcol", "ncol"):
        env[nm] = p.sb(pref + "ln_" + nm, (128, ntc), F32)
    env["rows_a"] = p.sb(pref + "ln_rows_a", (1, TT), F32)
    env["rows_b"] = p.sb(pref + "ln_rows_b", (1, TT), F32)
    if alloc_psum:
        env["ps_stat"] = p.ps("lnp_stat", (128, 2 * ntc))
        env["ps_tra"] = p.ps("lnp_tra", (1, TT))
        env["ps_trb"] = p.ps("lnp_trb", (1, TT))
        env["ps_bca"] = p.ps("lnp_bca", (128, TT))
        env["ps_bcb"] = p.ps("lnp_bcb", (128, TT))
    return env


def build_proj_res_ln(name, nin, abf):
    p = Prog(name)
    x = p.inp("x", (NCH, 128, TOK))
    a_in = p.inp("act", (nin, 128, TOK), BF16 if abf else F32)
    w = p.inp("w", (nin, 128, D))
    xo = p.outp("xo", (NCH, 128, TOK))
    z = p.outp("z", (NCH, 128, TOK))
    p.setup_consts(vals=(LN_EPS,))
    env = make_ln_env(p, 512)
    ps_o = p.ps("ps_o", (128, 512))
    emit_projresln(p, env, ps_o, x, a_in, w, xo, z, nin, abf, consts=True)
    return p


def emit_projresln(p, env, ps_o, x, a_in, w, xo, z, nin, abf, consts=False, pref="pr"):
    nc = p.nc
    TT = 512
    NT = TOK // TT
    x_sb = p.sb(pref + "x_sb", (128, NCH * TT), F32)
    a_sb = p.sb(pref + "a_sb", (128, nin * TT), BF16)
    af_sb = p.sb(pref + "af_sb", (128, nin * TT), F32) if not abf else None
    w_sb = p.sb(pref + "w_sb", (128, nin * D), BF16)
    xo_sb = p.sb(pref + "xo_sb", (128, NCH * TT), F32)
    z_sb = p.sb(pref + "z_sb", (128, NCH * TT), F32)

    with nc.Block() as b:
        @b.gpsimd
        def _(g):
            if consts:
                p.emit_consts(g)
            i = g.dma_start(w_sb.ap().rearrange("p (c n) -> p c n", c=nin),
                            w.ap().transpose([1, 0, 2]))
            n = p.inc(i, "ldw", 16)
            g.wait_ge(p.sem("ldw"), n)

    for it in range(NT):
        t0 = it * TT
        with nc.Block() as b:
            @b.sync
            def _(s, t0=t0):
                i = s.dma_start(x_sb.ap().rearrange("p (c n) -> p c n", c=NCH),
                                x.ap()[:, :, t0:t0 + TT].transpose([1, 0, 2]))
                p.inc(i, "ldx", 16)
                i = s.dma_start((a_sb if abf else af_sb).ap().rearrange(
                    "p (c n) -> p c n", c=nin),
                    a_in.ap()[:, :, t0:t0 + TT].transpose([1, 0, 2]))
                p.inc(i, "ldx", 16)
                p.wait(s, "ldx")
        if not abf:
            with nc.Block() as b:
                @b.vector
                def _(v):
                    v.tensor_copy(a_sb[:, 0:nin * TT], af_sb[:, 0:nin * TT])
        mm0 = p.cnt.get("g1mm", 0)
        f0 = p.cnt.get("g1f", 0)
        with nc.Block() as b:
            @b.tensor
            def _(t):
                for oc in range(NCH):
                    if oc >= 1:
                        t.wait_ge(p.sem("g1f"), f0 + oc)
                    for ic in range(nin):
                        i = t.matmul(ps_o[:, :],
                                     w_sb[:, ic * D + oc * 128: ic * D + oc * 128 + 128],
                                     a_sb[:, ic * TT:(ic + 1) * TT],
                                     start=(ic == 0), stop=(ic == nin - 1))
                    p.inc(i, "g1mm")
            @b.vector
            def _(v):
                for oc in range(NCH):
                    v.wait_ge(p.sem("g1mm"), mm0 + oc + 1)
                    i = v.tensor_add(xo_sb[:, oc * TT:(oc + 1) * TT],
                                     x_sb[:, oc * TT:(oc + 1) * TT], ps_o[:, :])
                    p.inc(i, "g1f")
        emit_ln(p, env, lambda c: xo_sb[:, c * TT:(c + 1) * TT],
                lambda c: z_sb[:, c * TT:(c + 1) * TT], TT)
        with nc.Block() as b:
            @b.sync
            def _(s, t0=t0):
                i = s.dma_start(xo.ap()[:, :, t0:t0 + TT].transpose([1, 0, 2]),
                                xo_sb.ap().rearrange("p (c n) -> p c n", c=NCH))
                p.inc(i, "st", 16)
                i = s.dma_start(z.ap()[:, :, t0:t0 + TT].transpose([1, 0, 2]),
                                z_sb.ap().rearrange("p (c n) -> p c n", c=NCH))
                p.inc(i, "st", 16)
                p.wait(s, "st")


HGN = 44
HP2 = 22
FFNR = 2176

def build_layer(name, W, H, dbg_max_win=None, dbg_ffn=True, dbg_stage=99, dbg_s4=""):
    HALO_W = 2 if W == 512 else 3
    R = HALO_W * W + 2048
    NWIN = R // W
    INNER = H * 64
    NI = INNER // 128
    QKW = 3 * INNER
    p = Prog(name)
    io = {}
    io["h"] = p.inp("h", (NCH, 128, R + 1))
    io["wqkv"] = p.inp("wqkv", (NCH, 128, QKW))
    io["wo"] = p.inp("wo", (NI, 128, D))
    io["lncol"] = p.inp("lncol", (128, 2 * NCH))
    io["qskcol"] = p.inp("qskcol", (64, 2))
    io["tabs"] = p.inp("tabs", (64, 4 * R))
    io["wkill"] = p.inp("wkill", (128, NWIN))
    io["w1p"] = p.inp("w1p", (NCH, 128, HGN * 128))
    io["b1acol"] = p.inp("b1acol", (128, HGN))
    io["b1gcol"] = p.inp("b1gcol", (128, HGN))
    io["w2p"] = p.inp("w2p", (HP2, 128, D))
    io["b2col"] = p.inp("b2col", (128, NCH))
    io["hout"] = p.outp("hout", (NCH, 128, 2048))
    io["hmid_dr"] = p.scratch("hmid_dr", (NCH, 128, R))
    io["hhat_dr"] = p.scratch("hhat_dr", (NCH, 128, R), BF16)
    p.setup_consts(vals=(LN_EPS, 1e-12))
    PS = p.ps("PS", (128, 2048))
    ps_a = p.ps("ps_a", (128, 512))
    ps_b = p.ps("ps_b", (128, 512))
    ps_c = p.ps("ps_c", (128, 512))
    ps_t = p.ps("ps_t", (128, 1024), BF16)
    pv = {"PS": PS, "ps_a": ps_a, "ps_b": ps_b, "ps_c": ps_c, "ps_t": ps_t}
    emit_layer(p, pv, W, H, io, consts=True, dbg_max_win=dbg_max_win,
               dbg_ffn=dbg_ffn, dbg_stage=dbg_stage, dbg_s4=dbg_s4)
    return p


def emit_layer(p, pv, W, H, io, consts=False, dbg_max_win=None, dbg_ffn=True,
               dbg_stage=99, dbg_s4=""):
    INNER = H * 64
    NI = INNER // 128
    HALO_W = 2 if W == 512 else 3
    R = HALO_W * W + 2048
    NWIN = R // W
    QW0 = HALO_W - 1
    F0 = R - FFNR
    NTC = W // 128
    BW = W + 128
    QKW = 3 * INNER
    nc = p.nc
    h_in = io["h"]
    wqkv = io["wqkv"]
    wo = io["wo"]
    lncol = io["lncol"]
    qskcol = io["qskcol"]
    tabs = io["tabs"]
    wkill = io["wkill"]
    w1p = io["w1p"]
    b1acol = io["b1acol"]
    b1gcol = io["b1gcol"]
    w2p = io["w2p"]
    b2col = io["b2col"]
    hout = io["hout"]
    hmid_dr = io["hmid_dr"]
    hhat_dr = io["hhat_dr"]
    PS = pv["PS"]
    ps_a, ps_b, ps_c, ps_t = pv["ps_a"], pv["ps_b"], pv["ps_c"], pv["ps_t"]
    env = make_ln_env(p, 512, alloc_psum=False, pref="ly")
    xsq_b = env["xsq"].ap().bitcast(BF16)

    PH2_F32 = NCH * HGN * 128 // 2 + HP2 * D // 2
    ph1 = [("wq", NCH * QKW // 2), ("wo", NI * D // 2), ("tabbf", 2 * R),
           ("qbuf", H * W // 2), ("kr0", H * W // 2), ("kr1", H * W // 2),
           ("vr0", NTC * INNER // 2), ("vr1", NTC * INNER // 2),
           ("rota", H * W // 2)]
    need = sum(n for _, n in ph1)
    base2 = max(PH2_F32, need)
    EXT = [("hmt", NCH * 256), ("hht", NCH * 256 // 2), ("gel", 512), ("act", HP2 * 128)]
    AR_F32 = base2 + sum(n for _, n in EXT)
    arena = p.sb("arena", (128, AR_F32), F32)
    arb = arena.ap().bitcast(BF16)
    offs = {}
    o = 0
    for nm, n in ph1:
        offs[nm] = o
        o += n
    def bfv(nm, ncols, rows=128):
        ofs = 2 * offs[nm]
        return arb[0:rows, ofs:ofs + ncols]
    wq_sb = bfv("wq", NCH * QKW)
    wo_sb = bfv("wo", NI * D)
    tab_bf = bfv("tabbf", 4 * R, 64)
    qbuf = bfv("qbuf", H * W, 64)
    kring = [bfv("kr0", H * W, 64), bfv("kr1", H * W, 64)]
    vring = [bfv("vr0", NTC * INNER), bfv("vr1", NTC * INNER)]
    rot_a = bfv("rota", H * W, 64)
    rot_b = xsq_b[0:64, H * W:2 * H * W]
    w1_sb = arb[:, 0:NCH * HGN * 128]
    w2_sb = arb[:, NCH * HGN * 128: NCH * HGN * 128 + HP2 * D]

    lng = p.sb("lng_sb", (128, 2 * NCH), F32)
    qsk = p.sb("qsk_sb", (64, 2), F32)
    wk_sb = p.sb("wk_sb", (128, NWIN), F32)
    b1a_sb = p.sb("b1a_sb", (128, HGN), F32)
    b1g_sb = p.sb("b1g_sb", (128, HGN), F32)
    b2_sb = p.sb("b2_sb", (128, NCH), F32)
    mask_lo = p.sb("mask_lo", (128, 128), F32)
    mask_hi = p.sb("mask_hi", (128, 128), F32)
    gap0 = sum(n for _, n in ph1)
    if PH2_F32 - gap0 >= 2 * (NCH * (W + 1)):
        hbuf = arena.ap()[:, gap0:gap0 + NCH * (W + 1)]
        hmid = arena.ap()[:, gap0 + NCH * (W + 1):gap0 + 2 * NCH * (W + 1)]
        gap0 += 2 * (NCH * (W + 1))
    else:
        hbuf = p.sb("hbuf", (128, NCH * (W + 1)), F32).ap()
        hmid = p.sb("hmid", (128, NCH * (W + 1)), F32).ap()
    o_sb = p.sb("o_sb", (128, INNER), BF16).ap()
    of_sb = p.sb("of_sb", (128, INNER), BF16).ap()
    if PH2_F32 - gap0 >= NCH * W // 2:
        sh_bf = arb[:, 2 * gap0:2 * gap0 + NCH * W]
    else:
        sh_bf = p.sb("sh_bf", (128, NCH * W), BF16).ap()
    hhat_b = p.sb("hhat_b", (128, NCH * W), BF16)
    rr = p.sb("rr", (128, 2 * H * NTC), F32)
    rrb = p.sb("rrb", (128, 2 * H * NTC), BF16)
    rbc_sb = p.sb("rbc_sb", (1, 256), BF16)
    p_sb = p.sb("p_sb", (128, BW), BF16)
    pt_sb = p.sb("pt_sb", (128, BW), BF16)
    l_col = p.sb("l_col", (128, 64), F32)
    eo = base2
    eoffs = {}
    for nm, n in EXT:
        eoffs[nm] = eo
        eo += n
    hmt = arena.ap()[:, eoffs["hmt"]:eoffs["hmt"] + NCH * 256]
    hht = arb[:, 2 * eoffs["hht"]:2 * eoffs["hht"] + NCH * 256]
    gel = arena.ap()[0:64, eoffs["gel"]:eoffs["gel"] + 512]
    act_sb = arb[0:128, 2 * eoffs["act"]:2 * eoffs["act"] + HP2 * 256]

    env["ps_stat"] = ps_c[:, 0:8]
    env["ps_tra"] = ps_a[0:1, 0:512]
    env["ps_trb"] = ps_b[0:1, 0:512]
    env["ps_bca"] = PS[:, 1024:1536]
    env["ps_bcb"] = PS[:, 1536:2048]

    with nc.Block() as blk:
        @blk.gpsimd
        def _(g):
            if consts:
                p.emit_consts(g)
            g.memset(hmid[:, 0:NCH * (W + 1)], 0.0)
            g.memset(mask_lo[:, :], 0.0)
            i = g.memset(mask_hi[:, :], 0.0)
            p.serial(g, i)
            g.affine_select(mask_lo[:, :], mask_lo[:, :], [[1, 128]], ALU.is_ge, NEG,
                            base=0, channel_multiplier=-1)
            i = g.affine_select(mask_hi[:, :], mask_hi[:, :], [[-1, 128]], ALU.is_ge, NEG,
                            base=0, channel_multiplier=1)
            p.serial(g, i)
            i = g.dma_start(wq_sb.rearrange("p (c n) -> p c n", c=NCH),
                            wqkv.ap().transpose([1, 0, 2]))
            p.inc(i, "ldg", 16)
            i = g.dma_start(wo_sb.rearrange("p (c n) -> p c n", c=NI),
                            wo.ap().transpose([1, 0, 2]))
            p.inc(i, "ldg", 16)
        @blk.sync
        def _(s):
            for ap_out, ap_in in ((lng.ap(), lncol.ap()),
                                  (qsk.ap(), qskcol.ap()), (wk_sb.ap(), wkill.ap()),
                                  (b1a_sb.ap(), b1acol.ap()), (b1g_sb.ap(), b1gcol.ap()), (b2_sb.ap(), b2col.ap())):
                i = s.dma_start(ap_out, ap_in)
                p.inc(i, "ld", 16)
            p.wait(s, "ld")
        @blk.gpsimd
        def _(g):
            p.wait(g, "ldg")
    CH_T = 2048
    for part in range(2):
        for c0 in range(0, 2 * R, CH_T):
            cw = min(CH_T, 2 * R - c0)
            with nc.Block() as blk:
                @blk.sync
                def _(s, part=part, c0=c0, cw=cw):
                    i = s.dma_start(env["xsq"][0:64, 0:cw],
                                    tabs.ap()[:, part * 2 * R + c0: part * 2 * R + c0 + cw])
                    p.inc(i, "ldt", 16)
                    p.wait(s, "ldt")
            with nc.Block() as blk:
                @blk.vector
                def _(v, part=part, c0=c0, cw=cw):
                    i = v.tensor_scalar(env["xsq"][0:64, 0:cw], env["xsq"][0:64, 0:cw],
                                        qsk[:, part:part + 1], None, ALU.mult)
                    p.serial(v, i)
                    v.tensor_copy(tab_bf[0:64, part * 2 * R + c0: part * 2 * R + c0 + cw],
                                  env["xsq"][0:64, 0:cw])

    for w in range(NWIN if dbg_max_win is None else min(NWIN, dbg_max_win)):
        cur, prev = w % 2, (w - 1) % 2
        r0 = w * W
        with nc.Block() as blk:
            @blk.sync
            def _(s, r0=r0):
                i = s.dma_start(hbuf.rearrange("p (c n) -> p c n", c=NCH),
                                h_in.ap()[:, :, r0:r0 + W + 1].transpose([1, 0, 2]))
                p.inc(i, "ldh", 16)
                p.wait(s, "ldh")
        with nc.Block() as blk:
            @blk.vector
            def _(v):
                for c in range(NCH):
                    off = 1 if c < NCH // 2 else 0
                    v.tensor_copy(sh_bf[:, c * W:(c + 1) * W],
                                  hbuf[:, c * (W + 1) + off: c * (W + 1) + off + W])
        if dbg_stage < 2:
            continue
        with nc.Block() as blk:
            @blk.tensor
            def _(t, w=w, cur=cur):
                mm0 = p.cnt.get("qev", 0)
                for cg in range(2 * NI):
                    pp = (ps_a, ps_b)[cg % 2]
                    if cg >= 2:
                        t.wait_ge(p.sem("qev"), mm0 + cg - 1)
                    for ic in range(NCH):
                        i = t.matmul(pp[:, 0:W], wq_sb[:, ic * QKW + cg * 128: ic * QKW + cg * 128 + 128],
                                     sh_bf[:, ic * W:(ic + 1) * W],
                                     start=(ic == 0), stop=(ic == NCH - 1))
                    p.inc(i, "qmm")
                for tc in range(NTC):
                    for vg in range(INNER // 512):
                        pp = (ps_a, ps_b)[(tc * (INNER // 512) + vg) % 2]
                        t.wait_ge(p.sem("qev"), mm0 + 2 * NI + tc * (INNER // 512) + vg - 1)
                        for ic in range(NCH):
                            i = t.matmul(pp[:, 0:512],
                                         sh_bf[:, ic * W + tc * 128: ic * W + (tc + 1) * 128],
                                         wq_sb[:, ic * QKW + 2 * INNER + vg * 512: ic * QKW + 2 * INNER + (vg + 1) * 512],
                                         start=(ic == 0), stop=(ic == NCH - 1))
                        p.inc(i, "qmm")
            @blk.vector
            def _(v, w=w, cur=cur):
                mm0 = p.cnt.get("qmm", 0) - (2 * NI + NTC * (INNER // 512))
                for cg in range(2 * NI):
                    pp = (ps_a, ps_b)[cg % 2]
                    v.wait_ge(p.sem("qmm"), mm0 + cg + 1)
                    dstb = qbuf if cg < NI else kring[cur]
                    hh = (cg % NI) * 2
                    v.tensor_copy(dstb[0:64, hh * W: hh * W + W], pp[0:64, 0:W])
                    i = v.tensor_copy(dstb[0:64, (hh + 1) * W: (hh + 2) * W], pp[64:128, 0:W])
                    p.inc(i, "qev")
                for k in range(NTC * (INNER // 512)):
                    pp = (ps_a, ps_b)[k % 2]
                    tc, vg = divmod(k, INNER // 512)
                    v.wait_ge(p.sem("qmm"), mm0 + 2 * NI + k + 1)
                    i = v.tensor_copy(vring[cur][:, tc * INNER + vg * 512: tc * INNER + (vg + 1) * 512],
                                      pp[:, 0:512])
                    p.inc(i, "qev")
        if dbg_stage < 3:
            continue
        with nc.Block() as blk:
            @blk.scalar
            def _(sc):
                sc.activation(xsq_b[0:64, 0:H * W], qbuf[0:64, :], AF.Square)
                i = sc.activation(xsq_b[0:64, H * W:2 * H * W], kring[cur][0:64, :], AF.Square)
                p.inc(i, "rsq")
            @blk.tensor
            def _(t):
                p.wait(t, "rsq")
                for tc in range(NTC):
                    for h in range(H):
                        t.matmul(ps_c[:, tc * 2 * H + h: tc * 2 * H + h + 1],
                                 xsq_b[0:64, h * W + tc * 128: h * W + (tc + 1) * 128],
                                 p.ones_colb[0:64, :], start=True, stop=True)
                        i = t.matmul(ps_c[:, tc * 2 * H + H + h: tc * 2 * H + H + h + 1],
                                     xsq_b[0:64, H * W + h * W + tc * 128: H * W + h * W + (tc + 1) * 128],
                                     p.ones_colb[0:64, :], start=True, stop=True)
                p.inc(i, "rss")
            @blk.scalar
            def _(sc):
                p.wait(sc, "rss")
                i = sc.activation(rr[:, :], ps_c[:, 0:2 * H * NTC], AF.Sqrt)
                p.inc(i, "rsr")
            @blk.vector
            def _(v):
                p.wait(v, "rsr")
                i = v.tensor_scalar(rr[:, :], rr[:, :], p.ccol[1e-12][:, 0:1], None, ALU.max)
                p.serial(v, i)
                i = v.reciprocal(rr[:, :], rr[:, :])
                p.serial(v, i)
                i = v.tensor_copy(rrb[:, :], rr[:, :])
                p.serial(v, i)
                for tc in range(NTC):
                    i = v.tensor_scalar(rr[:, tc * 2 * H: tc * 2 * H + H],
                                        rr[:, tc * 2 * H: tc * 2 * H + H], 8.0, None, ALU.mult)
                p.inc(i, "rrd")
        if dbg_stage < 4:
            continue
        with nc.Block() as blk:
            @blk.vector
            def _(v, r0=r0):
                swp = xsq_b[0:64, 0:H * W]
                for (buf, tof) in ((qbuf, 0), (kring[cur], 2 * R)):
                    cs = tab_bf[0:64, tof + r0:tof + r0 + W].unsqueeze(1).broadcast_to([64, H, W])
                    sn = tab_bf[0:64, tof + R + r0:tof + R + r0 + W].unsqueeze(1).broadcast_to([64, H, W])
                    bv = buf[0:64, :].rearrange("p (h n) -> p h n", h=H)
                    v.tensor_tensor(rot_a[0:64, :].rearrange("p (h n) -> p h n", h=H), bv, cs, ALU.mult)
                    i = v.tensor_tensor(rot_b[0:64, :].rearrange("p (h n) -> p h n", h=H), bv, sn, ALU.mult)
                    p.serial(v, i)
                    v.tensor_copy(swp[0:32, :], rot_b[32:64, :])
                    i = v.tensor_copy(swp[32:64, :], rot_b[0:32, :])
                    p.serial(v, i)
                    i = v.tensor_add(buf[0:64, :], rot_a[0:64, :], swp[0:64, :])
                    p.serial(v, i)
        if dbg_stage < 5:
            continue
        with nc.Block() as blk:
            c_rbc = p.cnt.get("rbc", 0)
            c_rb1 = p.cnt.get("rb1", 0)
            c_rb2 = p.cnt.get("rb2", 0)
            c_rbe = p.cnt.get("rbe", 0)
            @blk.tensor
            def _(t, cur=cur, c_rb1=c_rb1, c_rbe=c_rbe):
                n = 0
                for tc in range(NTC):
                    for h in range(H):
                        if n >= 2:
                            t.wait_ge(p.sem("rb1"), c_rb1 + n - 1)
                        c0 = tc * 2 * H + H + h
                        i = t.transpose(ps_t[0:1, (n % 2) * 128:(n % 2) * 128 + 128],
                                        rrb[:, c0:c0 + 1], p.identb[:, :])
                        p.inc(i, "rbc")
                        t.wait_ge(p.sem("rb1"), c_rb1 + n + 1)
                        if n >= 2:
                            t.wait_ge(p.sem("rbe"), c_rbe + n - 1)
                        i = t.matmul(ps_c[0:64, (n % 2) * 128:(n % 2) * 128 + 128],
                                     p.ones_rowb[0:1, 0:64],
                                     rbc_sb[0:1, (n % 2) * 128:(n % 2) * 128 + 128],
                                     start=True, stop=True)
                        p.inc(i, "rb2")
                        n += 1
            @blk.vector
            def _(v, cur=cur, c_rbc=c_rbc, c_rb2=c_rb2):
                n = 0
                for tc in range(NTC):
                    for h in range(H):
                        v.wait_ge(p.sem("rbc"), c_rbc + n + 1)
                        i = v.tensor_copy(rbc_sb[0:1, (n % 2) * 128:(n % 2) * 128 + 128],
                                          ps_t[0:1, (n % 2) * 128:(n % 2) * 128 + 128])
                        p.inc(i, "rb1")
                        v.wait_ge(p.sem("rb2"), c_rb2 + n + 1)
                        i = v.tensor_mul(kring[cur][0:64, h * W + tc * 128: h * W + (tc + 1) * 128],
                                         kring[cur][0:64, h * W + tc * 128: h * W + (tc + 1) * 128],
                                         ps_c[0:64, (n % 2) * 128:(n % 2) * 128 + 128])
                        p.inc(i, "rbe")
                        n += 1
        if w < QW0 or dbg_stage < 6:
            continue
        for s_ in range(NTC):
            P1W = W - 128 * s_
            c_smm = p.cnt.get("smm", 0)
            c_sbias = p.cnt.get("sbias", 0)
            c_sexp = p.cnt.get("sexp", 0)
            c_ptr = p.cnt.get("ptr", 0)
            c_ptev = p.cnt.get("ptev", 0)
            c_pvm = p.cnt.get("pvm", 0)
            c_aev = p.cnt.get("aev", 0)
            with nc.Block() as blk:
                @blk.tensor
                def _(t, w=w, s_=s_, cur=cur, prev=prev, P1W=P1W,
                      c_sexp=c_sexp, c_ptev=c_ptev, c_aev=c_aev):
                    for h in range(H + 1):
                        if h < H:
                            if h >= 1:
                                t.wait_ge(p.sem("sexp"), c_sexp + h)
                            q_ap = qbuf[0:64, h * W + s_ * 128: h * W + (s_ + 1) * 128]
                            segs = []
                            for a0, b0, ring, roff in ((0, P1W, prev, h * W + 128 * s_),
                                                       (P1W, BW, cur, h * W - P1W)):
                                while a0 < b0:
                                    m = min(b0, (a0 // 512 + 1) * 512)
                                    segs.append((a0, m, ring, roff))
                                    a0 = m
                            for a0, b0, ring, roff in segs:
                                i = t.matmul(PS[:, a0:b0], q_ap,
                                             kring[ring][0:64, roff + a0: roff + b0],
                                             start=True, stop=True)
                            p.inc(i, "smm")
                        if h >= 1:
                            hh = h - 1
                            t.wait_ge(p.sem("sexp"), c_sexp + hh + 1)
                            if hh >= 1:
                                t.wait_ge(p.sem("ptev"), c_ptev + hh)
                            if "t" not in dbg_s4:
                                for k in range(BW // 128):
                                    i = t.transpose(ps_t[:, k * 128:(k + 1) * 128],
                                                    p_sb[:, k * 128:(k + 1) * 128], p.identb[:, :])
                            else:
                                i = t.transpose(ps_t[:, 0:128], p_sb[:, 0:128], p.identb[:, :])
                            p.inc(i, "ptr")
                            t.wait_ge(p.sem("ptev"), c_ptev + hh + 1)
                            if hh >= 1:
                                t.wait_ge(p.sem("aev"), c_aev + hh)
                            if "p" not in dbg_s4:
                                for k in range(BW // 128):
                                    tkc = (s_ + k) % NTC
                                    slot = prev if k < NTC - s_ else cur
                                    i = t.matmul(ps_c[:, 0:64], pt_sb[:, k * 128:(k + 1) * 128],
                                                 vring[slot][:, tkc * INNER + h * 64 - 64: tkc * INNER + h * 64],
                                                 start=(k == 0), stop=(k == BW // 128 - 1))
                            else:
                                i = t.matmul(ps_c[:, 0:64], pt_sb[:, 0:128],
                                             vring[cur][:, 0:64], start=True, stop=True)
                            p.inc(i, "pvm")
                @blk.vector
                def _(v, w=w, s_=s_, P1W=P1W, c_smm=c_smm, c_ptr=c_ptr, c_pvm=c_pvm):
                    for h in range(H + 1):
                        if h < H:
                            v.wait_ge(p.sem("smm"), c_smm + h + 1)
                            if "m" not in dbg_s4:
                                i = v.tensor_scalar(PS[:, 0:P1W], PS[:, 0:P1W], wk_sb[:, w:w + 1], None, ALU.add)
                                p.serial(v, i)
                                v.tensor_add(PS[:, 0:128], PS[:, 0:128], mask_lo[:, :])
                                i = v.tensor_add(PS[:, BW - 128:BW], PS[:, BW - 128:BW], mask_hi[:, :])
                                p.serial(v, i)
                            if "r" not in dbg_s4:
                                i = v.reduce_max(l_col[:, 0:1], PS[:, 0:BW], AX.X, negate=True)
                                p.serial(v, i)
                                i = v.tensor_scalar(l_col[:, 8 + h:9 + h], l_col[:, 0:1],
                                                    rr[:, s_ * 2 * H + h: s_ * 2 * H + h + 1], None, ALU.mult)
                            else:
                                i = v.tensor_copy(l_col[:, 8 + h:9 + h], l_col[:, 0:1])
                            p.inc(i, "sbias")
                        if h >= 1:
                            hh = h - 1
                            if hh >= 1:
                                v.wait_ge(p.sem("pvm"), c_pvm + hh)
                            v.wait_ge(p.sem("ptr"), c_ptr + hh + 1)
                            if "v" not in dbg_s4:
                                i = v.tensor_copy(pt_sb[:, 0:BW], ps_t[:, 0:BW])
                            else:
                                i = v.tensor_copy(pt_sb[:, 0:128], ps_t[:, 0:128])
                            p.inc(i, "ptev")
                            if "c" not in dbg_s4:
                                i = v.reciprocal(l_col[:, 40 + hh:41 + hh], l_col[:, 24 + hh:25 + hh])
                                p.serial(v, i)
                            v.wait_ge(p.sem("pvm"), c_pvm + hh + 1)
                            if "o" not in dbg_s4:
                                i = v.tensor_scalar(o_sb[:, hh * 64:(hh + 1) * 64], ps_c[:, 0:64],
                                                    l_col[:, 40 + hh:41 + hh], None, ALU.mult)
                            else:
                                i = v.tensor_copy(l_col[:, 40 + hh:41 + hh], l_col[:, 24 + hh:25 + hh])
                            p.inc(i, "aev")
                @blk.scalar
                def _(sc, s_=s_, c_sbias=c_sbias, c_ptr=c_ptr):
                    for h in range(H):
                        sc.wait_ge(p.sem("sbias"), c_sbias + h + 1)
                        if h >= 1:
                            sc.wait_ge(p.sem("ptr"), c_ptr + h)
                        if "e" not in dbg_s4:
                            i = sc.activation(p_sb[:, 0:BW], PS[:, 0:BW], AF.Exp,
                                              bias=l_col[:, 8 + h:9 + h],
                                              scale=rr[:, s_ * 2 * H + h: s_ * 2 * H + h + 1],
                                              accum_out=l_col[:, 24 + h:25 + h])
                        else:
                            i = sc.copy(p_sb[:, 0:BW], PS[:, 0:BW])
                        p.inc(i, "sexp")
            if dbg_stage < 7 or "Z" in dbg_s4:
                continue
            c_otr = p.cnt.get("otr", 0)
            c_oev = p.cnt.get("oev", 0)
            if "X" in dbg_s4:
                with nc.Block() as blk:
                    @blk.tensor
                    def _(t, c_oev=c_oev):
                        for j in range(NI):
                            i = t.transpose(ps_t[:, j * 128:(j + 1) * 128],
                                            o_sb[:, j * 128:(j + 1) * 128], p.identb[:, :])
                            p.inc(i, "otr")
                        if "2" in dbg_s4:
                            t.wait_ge(p.sem("oev"), c_oev + NI)
                            for oc in range(NCH):
                                pp = (ps_a, ps_b)[oc % 2]
                                for ic in range(NI):
                                    i = t.matmul(pp[:, 0:128],
                                                 wo_sb[:, ic * D + oc * 128: ic * D + oc * 128 + 128],
                                                 of_sb[:, ic * 128:(ic + 1) * 128],
                                                 start=(ic == 0), stop=(ic == NI - 1))
                                p.inc(i, "wom")
                    @blk.vector
                    def _(v, c_otr=c_otr):
                        v.wait_ge(p.sem("otr"), c_otr + NI)
                        i = v.tensor_copy(of_sb[:, 0:INNER], ps_t[:, 0:INNER])
                        p.inc(i, "oev", NI)
                continue
            with nc.Block() as blk:
                @blk.tensor
                def _(t):
                    for j in range(NI):
                        i = t.transpose(ps_t[:, j * 128:(j + 1) * 128],
                                        o_sb[:, j * 128:(j + 1) * 128], p.identb[:, :])
                    p.inc(i, "otr")
                @blk.vector
                def _(v, c_otr=c_otr):
                    v.wait_ge(p.sem("otr"), c_otr + 1)
                    i = v.tensor_copy(of_sb[:, 0:INNER], ps_t[:, 0:INNER])
                    p.inc(i, "oev")
                @blk.tensor
                def _(t, wf0=p.cnt.get("wof", 0)):
                    if "W" in dbg_s4:
                        i = t.matmul(ps_a[:, 0:128], wo_sb[:, 0:128], of_sb[:, 0:128],
                                     start=True, stop=True)
                        p.inc(i, "wom", NCH)
                        return
                    p.wait(t, "oev")
                    for oc in range(NCH):
                        pp = (ps_a, ps_b)[oc % 2]
                        if oc >= 2:
                            t.wait_ge(p.sem("wof"), wf0 + oc - 1)
                        for ic in range(NI):
                            i = t.matmul(pp[:, 0:128], wo_sb[:, ic * D + oc * 128: ic * D + oc * 128 + 128],
                                         of_sb[:, ic * 128:(ic + 1) * 128],
                                         start=(ic == 0), stop=(ic == NI - 1))
                        p.inc(i, "wom")
                @blk.vector
                def _(v, s_=s_):
                    wm0 = p.cnt.get("wom", 0) - NCH
                    if "W" in dbg_s4 or "A" in dbg_s4:
                        v.wait_ge(p.sem("wom"), wm0 + NCH)
                        i = v.tensor_copy(l_col[:, 0:1], ps_a[:, 0:1])
                        p.inc(i, "wof", NCH)
                        return
                    for oc in range(NCH):
                        pp = (ps_a, ps_b)[oc % 2]
                        v.wait_ge(p.sem("wom"), wm0 + oc + 1)
                        i = v.tensor_add(hmid[:, oc * (W + 1) + s_ * 128 + 1: oc * (W + 1) + (s_ + 1) * 128 + 1],
                                         hbuf[:, oc * (W + 1) + s_ * 128 + 1: oc * (W + 1) + (s_ + 1) * 128 + 1],
                                         pp[:, 0:128])
                        p.inc(i, "wof")
        if dbg_stage < 8:
            continue
        def srcf(c, W=W):
            off = 1 if c < NCH // 2 else 0
            return hmid[:, c * (W + 1) + off: c * (W + 1) + off + W]
        emit_ln(p, env, srcf,
                (lambda c, W=W: hhat_b[:, c * W:(c + 1) * W]), W,
                gcol=(lambda c: lng[:, c:c + 1]),
                bcol=(lambda c: lng[:, NCH + c:NCH + c + 1]))
        with nc.Block() as blk:
            @blk.sync
            def _(s, r0=r0, w=w):
                i = s.dma_start(hmid_dr.ap()[:, :, r0:r0 + W].transpose([1, 0, 2]),
                                hmid.rearrange("p (c n) -> p c n", c=NCH)[:, :, 1:W + 1])
                p.inc(i, "sth", 16)
                i = s.dma_start(hhat_dr.ap()[:, :, r0:r0 + W].transpose([1, 0, 2]),
                                hhat_b.ap().rearrange("p (c n) -> p c n", c=NCH))
                p.inc(i, "sth", 16)
                p.wait(s, "sth")
            @blk.vector
            def _(v, W=W):
                for c in range(NCH):
                    v.tensor_copy(hmid[:, c * (W + 1):c * (W + 1) + 1],
                                  hmid[:, c * (W + 1) + W: c * (W + 1) + W + 1])

    if not dbg_ffn:
        return
    with nc.Block() as blk:
        @blk.gpsimd
        def _(g):
            i = g.dma_start(w1_sb.rearrange("p (c n) -> p c n", c=NCH),
                            w1p.ap().transpose([1, 0, 2]))
            p.inc(i, "lw1", 16)
            i = g.dma_start(w2_sb.rearrange("p (c n) -> p c n", c=HP2),
                            w2p.ap().transpose([1, 0, 2]))
            n = p.inc(i, "lw1", 16)
            g.wait_ge(p.sem("lw1"), n)
    NT2 = (FFNR + 255) // 256
    for it in range(NT2):
        TT2 = min(256, FFNR - it * 256)
        f0 = F0 + it * 256
        with nc.Block() as blk:
            @blk.sync
            def _(s, f0=f0, TT2=TT2):
                i = s.dma_start(hht.rearrange("p (c n) -> p c n", c=NCH)[:, :, 0:TT2],
                                hhat_dr.ap()[:, :, f0:f0 + TT2].transpose([1, 0, 2]))
                p.inc(i, "lff", 16)
                i = s.dma_start(hmt.rearrange("p (c n) -> p c n", c=NCH)[:, :, 0:TT2],
                                hmid_dr.ap()[:, :, f0:f0 + TT2].transpose([1, 0, 2]))
                p.inc(i, "lff", 16)
                p.wait(s, "lff")
        c_gev = p.cnt.get("gev", 0)
        c_w2e = p.cnt.get("w2e", 0)
        with nc.Block() as blk:
            @blk.tensor
            def _(t, TT2=TT2, c_gev=c_gev, c_w2e=c_w2e):
                for q in range(HP2):
                    if q >= 1:
                        t.wait_ge(p.sem("gev"), c_gev + q)
                    for half, pp in ((0, ps_a), (1, ps_b)):
                        cc = 2 * q + half
                        for ic in range(NCH):
                            i = t.matmul(pp[:, 0:TT2],
                                         w1_sb[:, ic * HGN * 128 + cc * 128: ic * HGN * 128 + (cc + 1) * 128],
                                         hht[:, ic * 256: ic * 256 + TT2],
                                         start=(ic == 0), stop=(ic == NCH - 1))
                        p.inc(i, "hgm")
                t.wait_ge(p.sem("gev"), c_gev + HP2)
                for ocp in range(2):
                    if ocp == 1:
                        t.wait_ge(p.sem("w2e"), c_w2e + 4)
                    for oc4 in range(4):
                        oc = ocp * 4 + oc4
                        for q in range(HP2):
                            i = t.matmul(PS[:, oc4 * 512: oc4 * 512 + TT2],
                                         w2_sb[:, q * D + oc * 128: q * D + (oc + 1) * 128],
                                         act_sb[:, q * 256: q * 256 + TT2],
                                         start=(q == 0), stop=(q == HP2 - 1))
                        p.inc(i, "w2m")
            @blk.scalar
            def _(sc, TT2=TT2):
                hg0 = p.cnt.get("hgm", 0) - 2 * HP2
                gv0 = c_gev
                for q in range(HP2):
                    for half, pp in ((0, ps_a), (1, ps_b)):
                        cc = 2 * q + half
                        sc.wait_ge(p.sem("hgm"), hg0 + 2 * q + half + 1)
                        if q >= 1:
                            sc.wait_ge(p.sem("gev"), gv0 + q)
                        i = sc.activation(gel[0:64, half * 256:half * 256 + TT2],
                                          pp[64:128, 0:TT2], AF.Gelu,
                                          bias=b1g_sb[64:128, cc:cc + 1])
                        p.inc(i, "gelu")
            @blk.vector
            def _(v, TT2=TT2, c_w2e=c_w2e):
                gl0 = p.cnt.get("gelu", 0) - 2 * HP2
                w20 = p.cnt.get("w2m", 0) - 8
                for q in range(HP2):
                    for half, pp in ((0, ps_a), (1, ps_b)):
                        cc = 2 * q + half
                        v.wait_ge(p.sem("gelu"), gl0 + 2 * q + half + 1)
                        i = v.scalar_tensor_tensor(act_sb[64 * half:64 * (half + 1), q * 256: q * 256 + TT2],
                                                   pp[0:64, 0:TT2], b1a_sb[0:64, cc:cc + 1],
                                                   gel[0:64, half * 256:half * 256 + TT2],
                                                   ALU.add, ALU.mult)
                    p.inc(i, "gev")
                for ocp in range(2):
                    for oc4 in range(4):
                        oc = ocp * 4 + oc4
                        v.wait_ge(p.sem("w2m"), w20 + ocp * 4 + oc4 + 1)
                        i = v.scalar_tensor_tensor(env["xsq"][:, oc * 256: oc * 256 + TT2],
                                                   PS[:, oc4 * 512: oc4 * 512 + TT2],
                                                   b2_sb[:, oc:oc + 1],
                                                   hmt[:, oc * 256: oc * 256 + TT2],
                                                   ALU.add, ALU.add)
                        p.inc(i, "w2e")
        with nc.Block() as blk:
            @blk.sync
            def _(s, it=it, f0=f0, TT2=TT2):
                own0 = R - 2048
                lo = max(f0, own0)
                if lo < f0 + TT2:
                    i = s.dma_start(hout.ap()[:, :, lo - own0: f0 + TT2 - own0].transpose([1, 0, 2]),
                                    env["xsq"].ap()[:, 0:NCH * 256].rearrange(
                                        "p (c n) -> p c n", c=NCH)[:, :, lo - f0: TT2])
                    p.inc(i, "stf", 16)
                    p.wait(s, "stf")


def layer_tables(W, own0):
    HALO_W = 2 if W == 512 else 3
    R = HALO_W * W + 2048
    pos = np.arange(own0 - HALO_W * W, own0 + 2048, dtype=np.float64)
    half = np.arange(0, 64, 2, dtype=np.float64)
    inv_freq = 1.0 / 10000.0 ** (half / 64.0)
    fr = pos[:, None] * inv_freq[None, :]
    base = (half + 0.4 * 64) / (1.4 * 64)
    power = (pos - 4096 // 2) / (W // 2)
    sc = base[None, :] ** power[:, None]
    cos, sin = np.cos(fr), np.sin(fr)
    pad = pos < 0
    qc = np.concatenate([cos * sc, cos * sc], 1)
    qs_ = np.concatenate([sin * sc, -sin * sc], 1)
    kc = np.concatenate([cos / sc, cos / sc], 1)
    ks_ = np.concatenate([sin / sc, -sin / sc], 1)
    for a_ in (qc, kc):
        a_[pad] = 1.0
    for a_ in (qs_, ks_):
        a_[pad] = 0.0
    tabs = np.concatenate([qc.T, qs_.T, kc.T, ks_.T], 1).astype(np.float32)
    return tabs


def layer_weights(W, H, wqkv, qs, ks, wo, lg, lb, w1, b1, w2, b2):
    INNER = H * 64
    HID = 2730
    HIDP = HP2 * 128
    d = {}
    d["wqkv"] = wchunks(wqkv.astype(np.float32))
    d["wo"] = wchunks(wo.astype(np.float32))
    d["lncol"] = np.concatenate([lg.reshape(NCH, 128).T, lb.reshape(NCH, 128).T], 1).astype(np.float32)
    qsk = np.zeros((64, 2), np.float32)
    qsk[:, 0] = qs; qsk[:, 1] = ks
    d["qskcol"] = qsk
    a = np.zeros((D, HIDP), np.float32); a[:, :HID] = w1[:, :HID]
    g = np.zeros((D, HIDP), np.float32); g[:, :HID] = w1[:, HID:2 * HID]
    w1p = np.zeros((D, HGN * 128), np.float32)
    for cc in range(HGN):
        w1p[:, 128 * cc:128 * cc + 64] = a[:, 64 * cc:64 * cc + 64]
        w1p[:, 128 * cc + 64:128 * cc + 128] = g[:, 64 * cc:64 * cc + 64]
    d["w1p"] = wchunks(w1p)
    b1a = np.zeros(HIDP, np.float32); b1a[:HID] = b1[:HID]
    b1g = np.zeros(HIDP, np.float32); b1g[:HID] = b1[HID:2 * HID]
    bac = np.zeros((128, HGN), np.float32)
    bgc = np.zeros((128, HGN), np.float32)
    for cc in range(HGN):
        bac[0:64, cc] = b1a[64 * cc:64 * cc + 64]; bac[64:128, cc] = bac[0:64, cc]
        bgc[0:64, cc] = b1g[64 * cc:64 * cc + 64]; bgc[64:128, cc] = bgc[0:64, cc]
    d["b1acol"] = bac; d["b1gcol"] = bgc
    w2p = np.zeros((HIDP, D), np.float32); w2p[:HID] = w2
    d["w2p"] = wchunks(w2p)
    d["b2col"] = b2.reshape(NCH, 128).T.astype(np.float32)
    return d


def layer_core_inputs(W, h_full, own0, first):
    HALO_W = 2 if W == 512 else 3
    R = HALO_W * W + 2048
    NWIN = R // W
    lo = own0 - HALO_W * W - 1
    hs = np.zeros((R + 1, D), np.float32)
    src_lo = max(lo, 0)
    hs[src_lo - lo: R + 1] = h_full[src_lo: own0 + 2048]
    wk = np.zeros((128, NWIN), np.float32)
    if first:
        wk[:, HALO_W] = NEG
    return {"h": fm(hs), "tabs": layer_tables(W, own0), "wkill": wk}


def build_flash(name="flash"):
    p = Prog(name)
    x = p.inp("x", (NCH, 128, S))
    wq = p.inp("wq", (NCH, 128, 3 * 512))
    O = p.outp("O", (S // 128, 128, 512), BF16)
    p.setup_consts(vals=(LN_EPS, 0.125, -0.125))
    PS = p.ps("PS", (128, 512))
    PS2 = p.ps("PS2", (128, 512))
    ps_a = p.ps("ps_a", (128, 512))
    ps_b = p.ps("ps_b", (128, 512))
    ps_c = p.ps("ps_c", (128, 512))
    ps_t = p.ps("ps_t", (128, 1024), BF16)
    pv = {"PS": PS, "PS2": PS2, "ps_a": ps_a, "ps_b": ps_b, "ps_c": ps_c,
          "ps_t": ps_t}
    emit_flash(p, pv, x, wq, O, consts=True)
    return p


def emit_flash(p, pv, x, wq, O, consts=False, oscat=None):
    SQ = S
    NQT = SQ // 128
    H8 = 8
    IN8 = H8 * 64
    nc = p.nc
    PS, ps_a, ps_b, ps_c, ps_t = (pv["PS"], pv["ps_a"], pv["ps_b"], pv["ps_c"],
                                  pv["ps_t"])
    PSr = (PS, pv["PS2"])
    env = make_ln_env(p, 128, alloc_psum=False, pref="fl")
    q64 = p.sb("q64", (64, H8 * SQ), BF16)
    k64 = p.sb("k64", (64, H8 * SQ), BF16)
    v_sb = p.sb("v_sb", (128, (SQ // 128) * IN8), BF16)
    wq_sb = p.sb("wq_sb", (128, NCH * 3 * IN8), BF16)
    xt = p.sb("xt", (128, NCH * 128), F32)
    zb = p.sb("zb", (128, NCH * 128), BF16)
    mask_hi = p.sb("mask_hi", (128, 128), BF16)
    p_sb = p.sb("p_sb", (128, 512), BF16)
    pt_sb = p.sb("pt_sb", (128, 512), BF16)
    o_acc = p.sb("o_acc", (128, 64), F32)
    o_strip = p.sb("o_strip", (128, IN8), BF16)
    l_col = p.sb("l_col", (128, 12), F32)
    env["ps_stat"] = ps_c[:, 0:2]
    env["ps_tra"] = ps_a[0:1, 0:128]
    env["ps_trb"] = ps_b[0:1, 0:128]
    env["ps_bca"] = PS[:, 0:128]
    env["ps_bcb"] = PS[:, 128:256]

    with nc.Block() as blk:
        @blk.gpsimd
        def _(g):
            if consts:
                p.emit_consts(g)
            i = g.memset(mask_hi[:, :], 0.0)
            p.serial(g, i)
            i = g.affine_select(mask_hi[:, :], mask_hi[:, :], [[-1, 128]], ALU.is_ge,
                                NEG, base=0, channel_multiplier=1)
            p.serial(g, i)
            i = g.dma_start(wq_sb.ap().rearrange("p (c n) -> p c n", c=NCH),
                            wq.ap().transpose([1, 0, 2]))
            n = p.inc(i, "ldg", 16)
            g.wait_ge(p.sem("ldg"), n)

    for it in range(SQ // 128):
        t0 = it * 128
        with nc.Block() as blk:
            @blk.sync
            def _(s, t0=t0):
                i = s.dma_start(xt.ap().rearrange("p (c n) -> p c n", c=NCH),
                                x.ap()[:, :, t0:t0 + 128].transpose([1, 0, 2]))
                p.inc(i, "ldx", 16)
                p.wait(s, "ldx")
        emit_ln(p, env, lambda c: xt[:, c * 128:(c + 1) * 128],
                lambda c: zb[:, c * 128:(c + 1) * 128], 128)
        with nc.Block() as blk:
            @blk.tensor
            def _(t, it=it, t0=t0):
                mm0 = p.cnt.get("qev", 0)
                for cg in range(8):
                    pp = (ps_a, ps_b)[cg % 2]
                    if cg >= 2:
                        t.wait_ge(p.sem("qev"), mm0 + cg - 1)
                    for ic in range(NCH):
                        i = t.matmul(pp[:, 0:128],
                                     wq_sb[:, ic * 3 * IN8 + cg * 128: ic * 3 * IN8 + cg * 128 + 128],
                                     zb[:, ic * 128:(ic + 1) * 128],
                                     start=(ic == 0), stop=(ic == NCH - 1))
                    p.inc(i, "qmm")
                pp = ps_a
                t.wait_ge(p.sem("qev"), mm0 + 8 - 1)
                for ic in range(NCH):
                    i = t.matmul(pp[:, 0:512],
                                 zb[:, ic * 128:(ic + 1) * 128],
                                 wq_sb[:, ic * 3 * IN8 + 2 * IN8: ic * 3 * IN8 + 3 * IN8],
                                 start=(ic == 0), stop=(ic == NCH - 1))
                p.inc(i, "qmm")
            @blk.vector
            def _(v, it=it, t0=t0):
                mm0 = p.cnt.get("qmm", 0) - 9
                for cg in range(8):
                    pp = (ps_a, ps_b)[cg % 2]
                    v.wait_ge(p.sem("qmm"), mm0 + cg + 1)
                    dstb = q64 if cg < 4 else k64
                    hh = (cg % 4) * 2
                    v.tensor_copy(dstb[0:64, hh * SQ + t0: hh * SQ + t0 + 128], pp[0:64, 0:128])
                    i = v.tensor_copy(dstb[0:64, (hh + 1) * SQ + t0: (hh + 1) * SQ + t0 + 128],
                                      pp[64:128, 0:128])
                    p.inc(i, "qev")
                v.wait_ge(p.sem("qmm"), mm0 + 9)
                i = v.tensor_copy(v_sb[:, it * IN8:(it + 1) * IN8], ps_a[:, 0:512])
                p.inc(i, "qev")

    for qt in range(NQT):
        kend = (qt + 1) * 128
        ktiles = []
        k0 = 0
        while k0 < kend:
            k1 = min(k0 + 512, kend)
            ktiles.append((k0, k1))
            k0 = k1
        NKT = len(ktiles)
        for h in range(H8):
            c_smm = p.cnt.get("smm", 0)
            c_sex = p.cnt.get("sex", 0)
            c_ptr = p.cnt.get("ptr", 0)
            c_ptv = p.cnt.get("ptv", 0)
            c_pvm = p.cnt.get("pvm", 0)
            c_oup = p.cnt.get("oup", 0)
            c_sal = p.cnt.get("sal", 0)
            with nc.Block() as blk:
                @blk.tensor
                def _(t, qt=qt, h=h, ktiles=ktiles, NKT=NKT,
                      c_sex=c_sex, c_ptv=c_ptv, c_oup=c_oup):
                    for kt in range(NKT + 1):
                        if kt < NKT:
                            k0, k1 = ktiles[kt]
                            if kt >= 2:
                                t.wait_ge(p.sem("sex"), c_sex + kt - 1)
                            i = t.matmul(PSr[kt % 2][:, 0:k1 - k0],
                                         q64[0:64, h * SQ + qt * 128: h * SQ + (qt + 1) * 128],
                                         k64[0:64, h * SQ + k0: h * SQ + k1],
                                         start=True, stop=True)
                            p.inc(i, "smm")
                        if kt >= 1:
                            kk = kt - 1
                            k0, k1 = ktiles[kk]
                            t.wait_ge(p.sem("sex"), c_sex + kk + 1)
                            if kk >= 1:
                                t.wait_ge(p.sem("ptv"), c_ptv + kk)
                            nch_k = (k1 - k0) // 128
                            for j in range(nch_k):
                                i = t.transpose(ps_t[:, j * 128:(j + 1) * 128],
                                                p_sb[:, j * 128:(j + 1) * 128], p.identb[:, :])
                            p.inc(i, "ptr")
                            t.wait_ge(p.sem("ptv"), c_ptv + kk + 1)
                            if kk >= 1:
                                t.wait_ge(p.sem("oup"), c_oup + kk)
                            for j in range(nch_k):
                                vt = (k0 // 128) + j
                                i = t.matmul(ps_c[:, 0:64], pt_sb[:, j * 128:(j + 1) * 128],
                                             v_sb[:, vt * IN8 + h * 64: vt * IN8 + (h + 1) * 64],
                                             start=(j == 0), stop=(j == nch_k - 1))
                            p.inc(i, "pvm")
                @blk.vector
                def _(v, qt=qt, h=h, ktiles=ktiles, NKT=NKT,
                      c_smm=c_smm, c_ptr=c_ptr, c_pvm=c_pvm, c_sal=c_sal):
                    for kt in range(NKT + 1):
                        if kt < NKT:
                            k0, k1 = ktiles[kt]
                            v.wait_ge(p.sem("smm"), c_smm + kt + 1)
                            if k1 == kend:
                                i = v.tensor_add(PSr[kt % 2][:, k1 - k0 - 128:k1 - k0],
                                                 PSr[kt % 2][:, k1 - k0 - 128:k1 - k0],
                                                 mask_hi[:, :])
                                p.serial(v, i)
                            i = v.reduce_max(l_col[:, 0:1], PSr[kt % 2][:, 0:k1 - k0],
                                             AX.X, negate=True)
                            p.serial(v, i)
                            if kt == 0:
                                i = v.tensor_copy(l_col[:, 1:2], l_col[:, 0:1])
                            else:
                                v.wait_ge(p.sem("sal"), c_sal + kt)
                                i = v.tensor_copy(l_col[:, 3:4], l_col[:, 1:2])
                                p.serial(v, i)
                                i = v.tensor_tensor(l_col[:, 1:2], l_col[:, 1:2], l_col[:, 0:1], ALU.min)
                            p.serial(v, i)
                            if kt >= 2:
                                v.wait_ge(p.sem("sal"), c_sal + kt - 1)
                            bc = 2 + 6 * (kt % 2)
                            i = v.tensor_scalar(l_col[:, bc:bc + 1], l_col[:, 1:2],
                                                p.ccol[0.125][:, 0:1], None, ALU.mult)
                            p.inc(i, "sbs")
                        if kt >= 1:
                            kk = kt - 1
                            k0, k1 = ktiles[kk]
                            nch_k = (k1 - k0) // 128
                            if kk >= 1:
                                v.wait_ge(p.sem("pvm"), c_pvm + kk)
                            v.wait_ge(p.sem("ptr"), c_ptr + kk + 1)
                            i = v.tensor_copy(pt_sb[:, 0:k1 - k0], ps_t[:, 0:k1 - k0])
                            p.inc(i, "ptv")
                            v.wait_ge(p.sem("pvm"), c_pvm + kk + 1)
                            v.wait_ge(p.sem("sal"), c_sal + kk + 1)
                            if kk == 0:
                                v.tensor_copy(l_col[:, 5:6], l_col[:, 6:7])
                                i = v.tensor_copy(o_acc[:, 0:64], ps_c[:, 0:64])
                                p.serial(v, i)
                            else:
                                v.scalar_tensor_tensor(l_col[:, 5:6], l_col[:, 5:6],
                                                       l_col[:, 4:5], l_col[:, 6:7],
                                                       ALU.mult, ALU.add)
                                i = v.scalar_tensor_tensor(o_acc[:, 0:64], o_acc[:, 0:64],
                                                           l_col[:, 4:5], ps_c[:, 0:64],
                                                           ALU.mult, ALU.add)
                                p.serial(v, i)
                            i = v.engine_nop()
                            p.inc(i, "oup")
                    i = v.reciprocal(l_col[:, 7:8], l_col[:, 5:6])
                    p.serial(v, i)
                    i = v.tensor_scalar(o_strip[:, h * 64:(h + 1) * 64], o_acc[:, 0:64],
                                        l_col[:, 7:8], None, ALU.mult)
                    p.inc(i, "ofin")
                @blk.scalar
                def _(sc, ktiles=ktiles, NKT=NKT, c_ptr=c_ptr, c_oup=c_oup):
                    sb0 = p.cnt.get("sbs", 0) - NKT
                    for kt in range(NKT):
                        k0, k1 = ktiles[kt]
                        sc.wait_ge(p.sem("sbs"), sb0 + kt + 1)
                        if kt >= 1:
                            sc.wait_ge(p.sem("ptr"), c_ptr + kt)
                            sc.wait_ge(p.sem("oup"), c_oup + kt)
                        bc = 2 + 6 * (kt % 2)
                        i = sc.activation(p_sb[:, 0:k1 - k0], PSr[kt % 2][:, 0:k1 - k0],
                                          AF.Exp, bias=l_col[:, bc:bc + 1],
                                          scale=p.ccol[0.125][:, 0:1],
                                          accum_out=l_col[:, 6:7])
                        p.inc(i, "sex")
                        if kt == 0:
                            i = sc.copy(l_col[:, 4:5], p.ones_col[:, 0:1])
                        else:
                            i = sc.activation(l_col[:, 4:5], l_col[:, 3:4], AF.Exp,
                                              bias=l_col[:, bc:bc + 1],
                                              scale=p.ccol[-0.125][:, 0:1])
                        p.inc(i, "sal")
        with nc.Block() as blk:
            if oscat is None:
                @blk.sync
                def _(s, qt=qt):
                    i = s.dma_start(O.ap()[qt, :, :], o_strip[:, :])
                    p.inc(i, "sto", 16)
                    p.wait(s, "sto")
            else:
                @blk.gpsimd
                def _(g, qt=qt):
                    g.wait_ge(p.sem("xld"), p.cnt["xld"])
                    i = g.indirect_dma_start(
                        oscat[0].ap(),
                        bass.IndirectOffsetOnAxis(ap=oscat[1][:, qt:qt + 1], axis=0),
                        o_strip[:, :], None)
                    p.inc(i, "sto", 16)
                    p.wait(g, "sto")


def build_mfront(name="G2"):
    NI2 = 16
    p = Prog(name)
    x1h = p.inp("x1h", (NCH, 128, TOK + 3))
    h2h = p.inp("h2h", (NCH, 128, TOK + 3))
    win = p.inp("win", (NCH, 128, 4096))
    convw = p.inp("convw", (128, 64))
    convb = p.inp("convb", (128, 16))
    xp = p.inp("xp", (NI2, 128, 96))
    dtw = p.inp("dtw", (64, 2048))
    dtb = p.inp("dtb", (128, 16))
    x2o = p.outp("x2", (NCH, 128, TOK))
    uo = p.outp("u", (NI2, 128, TOK))
    dlo = p.outp("dl", (NI2, 128, TOK))
    zzo = p.outp("zz", (NI2, 128, TOK))
    bco = p.outp("bc", (32, TOK))
    p.setup_consts(vals=(LN_EPS, -1.0))
    PS = p.ps("PS", (128, 1024))
    ps_a = p.ps("ps_a", (128, 512))
    ps_b = p.ps("ps_b", (128, 512))
    ps_c = p.ps("ps_c", (128, 512))
    pv = {"PS": PS, "ps_a": ps_a, "ps_b": ps_b, "ps_c": ps_c}
    emit_mfront(p, pv, x1h, h2h, win, convw, convb, xp, dtw, dtb,
                x2o, uo, dlo, zzo, bco, consts=True)
    return p


def emit_mfront(p, pv, x1h, h2h, win, convw, convb, xp, dtw, dtb,
                x2o, uo, dlo, zzo, bco, consts=False):
    NI2 = 16
    nc = p.nc
    PS, ps_a, ps_b, ps_c = pv["PS"], pv["ps_a"], pv["ps_b"], pv["ps_c"]
    env = make_ln_env(p, 512, alloc_psum=False)
    m_bf = p.sb("m_bf", (128, NCH * (TOK + 3)), BF16)
    w_sb = p.sb("w_sb", (128, NCH * 4096), BF16)
    xpw = p.sb("xpw", (128, NI2 * 96), BF16)
    dtw_sb = p.sb("dtw_sb", (64, 2048), BF16)
    cw_sb = p.sb("cw_sb", (128, 64), F32)
    cb_sb = p.sb("cb_sb", (128, 16), F32)
    db_sb = p.sb("db_sb", (128, 16), F32)
    x1_sb = p.sb("x1_sb", (128, NCH * 256), F32)
    h2_sb = p.sb("h2_sb", (128, NCH * 256), F32)
    x2_sb = x1_sb
    xz = p.sb("xz", (128, NI2 * 515), F32)
    cacc = p.sb("cacc", (128, 512), F32)
    u_sb = p.sb("u_sb", (128, 512), F32)
    ub = p.sb("ub", (128, NI2 * 512), BF16)
    zz_sb = p.sb("zz_sb", (128, 512), F32)
    xd_sb = p.sb("xd_sb", (96, 512), F32)
    xdb = p.sb("xdb", (96, 512), BF16)
    sp1 = p.sb("sp1", (128, 512), F32)
    sp2 = p.sb("sp2", (128, 512), F32)
    sp3 = p.sb("sp3", (128, 512), F32)
    dl_sb = p.sb("dl_sb", (128, 512), F32)
    env["ps_stat"] = ps_c[:, 0:8]
    env["ps_tra"] = ps_a[0:1, 0:512]
    env["ps_trb"] = ps_b[0:1, 0:512]
    env["ps_bca"] = PS[:, 0:512]
    env["ps_bcb"] = PS[:, 512:1024]

    with nc.Block() as blk:
        @blk.gpsimd
        def _(g):
            if consts:
                p.emit_consts(g)
            i = g.dma_start(w_sb.ap().rearrange("p (c n) -> p c n", c=NCH),
                            win.ap().transpose([1, 0, 2]))
            p.inc(i, "ldg", 16)
            i = g.dma_start(xpw.ap().rearrange("p (c n) -> p c n", c=NI2),
                            xp.ap().transpose([1, 0, 2]))
            p.inc(i, "ldg", 16)
            i = g.dma_start(dtw_sb.ap(), dtw.ap())
            p.inc(i, "ldg", 16)
            g.wait_ge(p.sem("ldg"), p.cnt["ldg"])
        @blk.sync
        def _(s):
            for ap_out, ap_in in ((cw_sb.ap(), convw.ap()), (cb_sb.ap(), convb.ap()),
                                  (db_sb.ap(), dtb.ap())):
                i = s.dma_start(ap_out, ap_in)
                p.inc(i, "ld", 16)
            p.wait(s, "ld")

    tiles1 = [(0, 128)] + [(3 + i * 256, 256) for i in range(TOK // 256)]
    for (c0, ww) in tiles1:
        with nc.Block() as blk:
            @blk.sync
            def _(s, c0=c0, ww=ww):
                i = s.dma_start(x1_sb.ap()[:, 0:NCH * ww].rearrange("p (c n) -> p c n", c=NCH),
                                x1h.ap()[:, :, c0:c0 + ww].transpose([1, 0, 2]))
                p.inc(i, "ldx", 16)
                i = s.dma_start(h2_sb.ap()[:, 0:NCH * ww].rearrange("p (c n) -> p c n", c=NCH),
                                h2h.ap()[:, :, c0:c0 + ww].transpose([1, 0, 2]))
                p.inc(i, "ldx", 16)
                p.wait(s, "ldx")
        with nc.Block() as blk:
            @blk.vector
            def _(v, ww=ww):
                for c in range(NCH):
                    i = v.tensor_add(x2_sb[:, c * ww:(c + 1) * ww],
                                     x1_sb[:, c * ww:(c + 1) * ww],
                                     h2_sb[:, c * ww:(c + 1) * ww])
        emit_ln(p, env, (lambda c, ww=ww: x2_sb[:, c * ww:(c + 1) * ww]),
                (lambda c, c0=c0, ww=ww: m_bf[:, c * (TOK + 3) + c0: c * (TOK + 3) + c0 + ww]),
                ww)
        if c0 >= 3:
            with nc.Block() as blk:
                @blk.sync
                def _(s, c0=c0, ww=ww):
                    i = s.dma_start(x2o.ap()[:, :, c0 - 3:c0 - 3 + ww].transpose([1, 0, 2]),
                                    x2_sb.ap()[:, 0:NCH * ww].rearrange("p (c n) -> p c n", c=NCH))
                    p.inc(i, "stx", 16)
                    p.wait(s, "stx")

    c_cev0 = p.cnt.get("cev", 0)
    c_cmm0 = p.cnt.get("cmm", 0)
    with nc.Block() as blk:
        @blk.tensor
        def _(t):
            for oc in range(NI2):
                pp = (ps_a, ps_b)[oc % 2]
                if oc >= 2:
                    t.wait_ge(p.sem("cev"), c_cev0 + oc - 1)
                for ic in range(NCH):
                    i = t.matmul(pp[:, 0:3],
                                 w_sb[:, ic * 4096 + oc * 128: ic * 4096 + oc * 128 + 128],
                                 m_bf[:, ic * (TOK + 3): ic * (TOK + 3) + 3],
                                 start=(ic == 0), stop=(ic == NCH - 1))
                p.inc(i, "cmm")
        @blk.vector
        def _(v):
            for oc in range(NI2):
                pp = (ps_a, ps_b)[oc % 2]
                v.wait_ge(p.sem("cmm"), c_cmm0 + oc + 1)
                i = v.tensor_copy(xz[:, oc * 515: oc * 515 + 3], pp[:, 0:3])
                p.inc(i, "cev")

    NT = TOK // 512
    for it in range(NT):
        t0 = it * 512
        c_xmm = p.cnt.get("xmm", 0)
        c_xev = p.cnt.get("xev", 0)
        with nc.Block() as blk:
            @blk.tensor
            def _(t, t0=t0, c_xev=c_xev):
                for oc in range(32):
                    pp = (ps_a, ps_b)[oc % 2]
                    if oc >= 2:
                        t.wait_ge(p.sem("xev"), c_xev + oc - 1)
                    for ic in range(NCH):
                        i = t.matmul(pp[:, 0:512],
                                     w_sb[:, ic * 4096 + oc * 128: ic * 4096 + oc * 128 + 128],
                                     m_bf[:, ic * (TOK + 3) + t0 + 3: ic * (TOK + 3) + t0 + 3 + 512],
                                     start=(ic == 0), stop=(ic == NCH - 1))
                    p.inc(i, "xmm")
            @blk.vector
            def _(v, it=it, t0=t0, c_xmm=c_xmm):
                for oc in range(NI2):
                    pp = (ps_a, ps_b)[oc % 2]
                    v.wait_ge(p.sem("xmm"), c_xmm + oc + 1)
                    i = v.tensor_copy(xz[:, oc * 515 + 3: oc * 515 + 515], pp[:, 0:512])
                    p.inc(i, "xev")
            @blk.scalar
            def _(sc, it=it, t0=t0, c_xmm=c_xmm, c_stz=p.cnt.get("stz", 0)):
                for oc in range(NI2, 32):
                    pp = (ps_a, ps_b)[oc % 2]
                    sc.wait_ge(p.sem("xmm"), c_xmm + oc + 1)
                    if oc >= NI2 + 1:
                        sc.wait_ge(p.sem("stz"), c_stz + 16 * (oc - NI2))
                    i = sc.activation(zz_sb[:, 0:512], pp[:, 0:512], AF.Silu)
                    p.inc(i, "xev")
            @blk.sync
            def _(s, it=it, t0=t0, c_xev=c_xev):
                for oc in range(NI2, 32):
                    s.wait_ge(p.sem("xev"), c_xev + oc + 1)
                    i = s.dma_start(zzo.ap()[oc - NI2, :, t0:t0 + 512], zz_sb[:, 0:512])
                    p.inc(i, "stz", 16)
                p.wait(s, "stz")
        c_cv = p.cnt.get("cv", 0)
        c_slu = p.cnt.get("slu", 0)
        c_ubc = p.cnt.get("ubc", 0)
        c_stu = p.cnt.get("stu", 0)
        with nc.Block() as blk:
            @blk.vector
            def _(v, c_slu=c_slu):
                for c in range(NI2 + 1):
                    if c < NI2:
                        base = c * 515
                        if c >= 1:
                            v.wait_ge(p.sem("slu"), c_slu + c)
                        i = v.tensor_scalar(cacc[:, 0:512], xz[:, base:base + 512],
                                            cw_sb[:, c * 4:c * 4 + 1], None, ALU.mult)
                        p.serial(v, i)
                        for tau in range(1, 4):
                            i = v.scalar_tensor_tensor(cacc[:, 0:512], xz[:, base + tau:base + tau + 512],
                                                       cw_sb[:, c * 4 + tau:c * 4 + tau + 1],
                                                       cacc[:, 0:512], ALU.mult, ALU.add)
                            if tau < 3:
                                p.serial(v, i)
                        i = v.tensor_copy(xz[:, base:base + 3], xz[:, base + 512:base + 515])
                        p.inc(i, "cv")
                    if c >= 1:
                        v.wait_ge(p.sem("slu"), c_slu + c)
                        i = v.tensor_copy(ub[:, (c - 1) * 512:c * 512], u_sb[:, 0:512])
                        p.inc(i, "ubc")
            @blk.scalar
            def _(sc, c_cv=c_cv, c_ubc=c_ubc, c_stu=c_stu):
                for c in range(NI2):
                    sc.wait_ge(p.sem("cv"), c_cv + c + 1)
                    if c >= 1:
                        sc.wait_ge(p.sem("ubc"), c_ubc + c)
                        sc.wait_ge(p.sem("stu"), c_stu + 16 * c)
                    i = sc.activation(u_sb[:, 0:512], cacc[:, 0:512], AF.Silu,
                                      bias=cb_sb[:, c:c + 1])
                    p.inc(i, "slu")
            @blk.sync
            def _(s, it=it, t0=t0, c_slu=c_slu):
                for c in range(NI2):
                    s.wait_ge(p.sem("slu"), c_slu + c + 1)
                    i = s.dma_start(uo.ap()[c, :, t0:t0 + 512], u_sb[:, 0:512])
                    p.inc(i, "stu", 16)
                p.wait(s, "stu")
        with nc.Block() as blk:
            @blk.tensor
            def _(t):
                for ic in range(NI2):
                    i = t.matmul(PS[0:96, 512:1024], xpw[:, ic * 96:(ic + 1) * 96],
                                 ub[:, ic * 512:(ic + 1) * 512],
                                 start=(ic == 0), stop=(ic == NI2 - 1))
                p.inc(i, "xdm")
            @blk.vector
            def _(v):
                p.wait(v, "xdm")
                v.tensor_copy(xd_sb[:, 0:512], PS[0:96, 512:1024])
                i = v.tensor_copy(xdb[:, 0:512], PS[0:96, 512:1024])
                p.inc(i, "xde")
            @blk.sync
            def _(s, t0=t0):
                p.wait(s, "xde")
                i = s.dma_start(bco.ap()[:, t0:t0 + 512], xd_sb[64:96, 0:512])
                p.inc(i, "stb", 16)
                p.wait(s, "stb")
        c_dmm = p.cnt.get("dmm", 0)
        c_dev = p.cnt.get("dev", 0)
        with nc.Block() as blk:
            @blk.tensor
            def _(t, c_dev=c_dev):
                for oc in range(NI2):
                    pp = (ps_a, ps_b)[oc % 2]
                    if oc >= 2:
                        t.wait_ge(p.sem("dev"), c_dev + oc - 1)
                    i = t.matmul(pp[:, 0:512], dtw_sb[:, oc * 128:(oc + 1) * 128],
                                 xdb[0:64, 0:512], start=True, stop=True)
                    p.inc(i, "dmm")
            @blk.vector
            def _(v, c_dmm=c_dmm, c_dln=p.cnt.get("dln", 0), c_std=p.cnt.get("std", 0)):
                for oc in range(NI2):
                    pp = (ps_a, ps_b)[oc % 2]
                    v.wait_ge(p.sem("dmm"), c_dmm + oc + 1)
                    i = v.tensor_scalar(sp1[:, 0:512], pp[:, 0:512], db_sb[:, oc:oc + 1],
                                        None, ALU.add)
                    p.inc(i, "dab")
                    v.wait_ge(p.sem("dab"), p.cnt["dab"])
                    i = v.tensor_scalar_max(sp3[:, 0:512], sp1[:, 0:512], 0.0)
                    p.serial(v, i)
                    v.wait_ge(p.sem("dln"), c_dln + oc + 1)
                    if oc >= 1:
                        v.wait_ge(p.sem("std"), c_std + 16 * oc)
                    i = v.tensor_add(dl_sb[:, 0:512], sp3[:, 0:512], sp2[:, 0:512])
                    p.inc(i, "dev")
            @blk.scalar
            def _(sc, c_dmm=c_dmm, c_dev=c_dev):
                da0 = p.cnt.get("dab", 0) - NI2
                for oc in range(NI2):
                    sc.wait_ge(p.sem("dab"), da0 + oc + 1)
                    if oc >= 1:
                        sc.wait_ge(p.sem("dev"), c_dev + oc)
                    i = sc.activation(sp2[:, 0:512], sp1[:, 0:512], AF.Abs)
                    p.serial(sc, i, tag="scser")
                    i = sc.activation(sp2[:, 0:512], sp2[:, 0:512], AF.Exp,
                                      scale=p.ccol[-1.0][:, 0:1])
                    p.serial(sc, i, tag="scser")
                    i = sc.activation(sp2[:, 0:512], sp2[:, 0:512], AF.Ln,
                                      bias=p.ones_col[:, 0:1])
                    p.inc(i, "dln")
            @blk.sync
            def _(s, t0=t0, c_dev=c_dev):
                for oc in range(NI2):
                    s.wait_ge(p.sem("dev"), c_dev + oc + 1)
                    i = s.dma_start(dlo.ap()[oc, :, t0:t0 + 512], dl_sb[:, 0:512])
                    p.inc(i, "std", 16)
                p.wait(s, "std")
    return p


def build_add(name="G4"):
    p = Prog(name)
    nc = p.nc
    a = p.inp("a", (NCH, 128, TOK))
    b = p.inp("b", (NCH, 128, TOK))
    o = p.outp("o", (NCH, 128, TOK))
    a_sb = p.sb("a_sb", (128, NCH * 512), F32)
    b_sb = p.sb("b_sb", (128, NCH * 512), F32)
    o_sb = p.sb("o_sb", (128, NCH * 512), F32)
    for it in range(TOK // 512):
        t0 = it * 512
        with nc.Block() as blk:
            @blk.sync
            def _(s, t0=t0):
                i = s.dma_start(a_sb.ap().rearrange("p (c n) -> p c n", c=NCH),
                                a.ap()[:, :, t0:t0 + 512].transpose([1, 0, 2]))
                p.inc(i, "ld", 16)
                i = s.dma_start(b_sb.ap().rearrange("p (c n) -> p c n", c=NCH),
                                b.ap()[:, :, t0:t0 + 512].transpose([1, 0, 2]))
                p.inc(i, "ld", 16)
                p.wait(s, "ld")
        with nc.Block() as blk:
            @blk.vector
            def _(v):
                i = v.tensor_add(o_sb[:, :], a_sb[:, :], b_sb[:, :])
        with nc.Block() as blk:
            @blk.sync
            def _(s, t0=t0):
                i = s.dma_start(o.ap()[:, :, t0:t0 + 512].transpose([1, 0, 2]),
                                o_sb.ap().rearrange("p (c n) -> p c n", c=NCH))
                p.inc(i, "st", 16)
                p.wait(s, "st")
    return p


def build_scan(name="P4"):
    TC = 512
    NTCH = S // TC
    p = Prog(name)
    nc = p.nc
    dl = p.inp("dl", (B, 8, 32, S))
    u = p.inp("u", (B, 8, 32, S))
    zz = p.inp("zz", (B, 8, 32, S))
    bc = p.inp("bc", (B, 32, S))
    alog = p.inp("alog", (128, 32))
    dp = p.inp("dp", (8, 32))
    e8 = p.inp("e8", (8, 128))
    s8 = p.inp("s8", (128, 8))
    y = p.outp("y", (B, 8, 32, S))

    p.setup_consts(vals=(-1.0,))
    GD = 8
    d_sb = p.sb("d_sb", (8, GD * TC), F32)
    u_sb = p.sb("u_sb", (8, GD * TC), F32)
    z_sb = p.sb("z_sb", (8, GD * TC), F32)
    zs_sb = p.sb("zs_sb", (8, GD * TC), F32)
    yst = p.sb("yst", (8, GD * TC), F32)
    bctb = p.sb("bctb", (16, TC), F32)
    bctc = p.sb("bctc", (16, TC), F32)
    bbc = p.sb("bbc", (128, TC), F32)
    cbc = p.sb("cbc", (128, TC), F32)
    a_sb = p.sb("a_sb", (128, 32), F32)
    dp_sb = p.sb("dp_sb", (8, 32), F32)
    e8f = p.sb("e8f", (8, 128), F32)
    ebf = p.sb("ebf", (16, 128), F32)
    s8f = p.sb("s8f", (128, 8), F32)
    dA = p.sb("dA", (128, TC), F32)
    g_sb = p.sb("g_sb", (128, TC), F32)
    t1 = p.sb("t1", (128, TC), F32)
    h_sb = p.sb("h_sb", (128, TC), F32)
    t2 = p.sb("t2", (128, TC), F32)
    t3 = p.sb("t3", (8, TC), F32)
    state = p.sb("state", (128, 32 * B), F32)
    ps_a = p.ps("ps_a", (128, 512))
    ps_b = p.ps("ps_b", (128, 512))
    ps_c = p.ps("ps_c", (128, 512))
    ps_d = p.ps("ps_d", (128, 512))

    with nc.Block() as blk:
        @blk.sync
        def _(s):
            for ap_out, ap_in in ((a_sb.ap(), alog.ap()), (dp_sb.ap(), dp.ap()),
                                  (e8f.ap(), e8.ap()), (s8f.ap(), s8.ap())):
                i = s.dma_start(ap_out, ap_in)
                p.inc(i, "ld", 16)
            p.wait(s, "ld")
        @blk.gpsimd
        def _(g):
            p.emit_consts(g)
            g.memset(state[:, :], 0.0)
    ebi = p.inp("eb", (16, 128))
    with nc.Block() as blk:
        @blk.sync
        def _(s):
            i = s.dma_start(ebf.ap(), ebi.ap())
            p.inc(i, "ld2", 16)
            p.wait(s, "ld2")
    with nc.Block() as blk:
        @blk.scalar
        def _(sc):
            i = sc.activation(a_sb[:, :], a_sb[:, :], AF.Exp)
            p.inc(i, "ae")
        @blk.vector
        def _(v):
            p.wait(v, "ae")
            i = v.tensor_scalar(a_sb[:, :], a_sb[:, :], -1.0, None, ALU.mult)
            p.serial(v, i)

    for b_ in range(B):
        for tc in range(NTCH):
            t0 = tc * TC
            with nc.Block() as blk:
                @blk.sync
                def _(s, b_=b_, t0=t0):
                    i = s.dma_start(bctb.ap(), bc.ap()[b_][0:16, t0:t0 + TC])
                    p.inc(i, "ldb", 16)
                    i = s.dma_start(bctc.ap(), bc.ap()[b_][16:32, t0:t0 + TC])
                    p.inc(i, "ldb", 16)
                    p.wait(s, "ldb")
            with nc.Block() as blk:
                @blk.tensor
                def _(t):
                    t.matmul(ps_c[:, 0:TC], ebf[:, :], bctb[:, 0:TC], start=True, stop=True)
                    i = t.matmul(ps_d[:, 0:TC], ebf[:, :], bctc[:, 0:TC], start=True, stop=True)
                    p.inc(i, "bcm")
                @blk.vector
                def _(v):
                    p.wait(v, "bcm")
                    v.tensor_copy(bbc[:, :], ps_c[:, 0:TC])
                    i = v.tensor_copy(cbc[:, :], ps_d[:, 0:TC])
                    p.inc(i, "bce")
            for grp in range(32 // GD):
                g0 = grp * GD
                with nc.Block() as blk:
                    @blk.sync
                    def _(s, b_=b_, t0=t0, g0=g0):
                        for sb_, dr in ((d_sb, dl), (u_sb, u), (z_sb, zz)):
                            i = s.dma_start(sb_.ap().rearrange("p (k n) -> p k n", k=GD),
                                            dr.ap()[b_][:, g0:g0 + GD, t0:t0 + TC])
                            p.inc(i, "ldt", 16)
                        p.wait(s, "ldt")
                    @blk.scalar
                    def _(sc):
                        p.wait(sc, "ldt")
                        i = sc.activation(zs_sb[:, :], z_sb[:, :], AF.Silu)
                        p.inc(i, "zs")
                for dk in range(GD):
                    dblk = g0 + dk
                    with nc.Block() as blk:
                        @blk.tensor
                        def _(t, dk=dk):
                            i = t.matmul(ps_a[:, 0:TC], e8f[:, :],
                                         d_sb[0:8, dk * TC:(dk + 1) * TC], start=True, stop=True)
                            p.inc(i, "dbm")
                            i = t.matmul(ps_b[:, 0:TC], e8f[:, :],
                                         u_sb[0:8, dk * TC:(dk + 1) * TC], start=True, stop=True)
                            p.inc(i, "ubm")
                        @blk.scalar
                        def _(sc, dblk=dblk):
                            p.wait(sc, "dbm")
                            i = sc.activation(dA[:, :], ps_a[:, 0:TC], AF.Exp,
                                              scale=a_sb[:, dblk:dblk + 1])
                            p.inc(i, "dae")
                        @blk.vector
                        def _(v, b_=b_, dblk=dblk):
                            p.wait(v, "ubm")
                            i = v.tensor_mul(t1[:, :], ps_b[:, 0:TC], bbc[:, :])
                            p.serial(v, i)
                            i = v.tensor_mul(g_sb[:, :], t1[:, :], ps_a[:, 0:TC])
                            p.serial(v, i)
                            p.wait(v, "dae")
                            sl = state[:, b_ * 32 + dblk: b_ * 32 + dblk + 1]
                            i = v.tensor_tensor_scan(h_sb[:, :], dA[:, :], g_sb[:, :],
                                                     sl, ALU.mult, ALU.add)
                            p.serial(v, i)
                            v.tensor_copy(sl, h_sb[:, TC - 1:TC])
                            i = v.tensor_mul(t2[:, :], h_sb[:, :], cbc[:, :])
                            p.inc(i, "ym")
                        @blk.tensor
                        def _(t):
                            p.wait(t, "ym")
                            i = t.matmul(ps_d[0:8, 0:TC], s8f[:, :], t2[:, :],
                                         start=True, stop=True)
                            p.inc(i, "yrm")
                        @blk.vector
                        def _(v, dk=dk, dblk=dblk):
                            p.wait(v, "yrm")
                            i = v.scalar_tensor_tensor(t3[:, :], u_sb[0:8, dk * TC:(dk + 1) * TC],
                                                       dp_sb[:, dblk:dblk + 1], ps_d[0:8, 0:TC],
                                                       ALU.mult, ALU.add)
                            p.serial(v, i)
                            i = v.tensor_mul(yst[0:8, dk * TC:(dk + 1) * TC], t3[:, :],
                                             zs_sb[0:8, dk * TC:(dk + 1) * TC])
                            p.inc(i, "yf")
                with nc.Block() as blk:
                    @blk.sync
                    def _(s, b_=b_, t0=t0, g0=g0):
                        i = s.dma_start(y.ap()[b_][:, g0:g0 + GD, t0:t0 + TC],
                                        yst.ap().rearrange("p (k n) -> p k n", k=GD))
                        p.inc(i, "sty", 16)
                        p.wait(s, "sty")
    return p


I32 = mybir.dt.int32
XPOLL_MAX = 30000000


def setup_xchg(p, nev):
    p.x_nev = nev
    p.x_widx = p.inp("xg_widx", (128, 1), I32)
    p.x_gidx = p.inp("xg_gidx", (128, 1), I32)
    p.x_fidx = p.inp("xg_fidx", (128, 1), I32)
    p.x_magic = p.inp("xg_magic", (128, 2 * nev), I32)
    p.x_mask = p.inp("xg_mask", (128, 1), F32)
    p.xi_sb = p.sb("xg_i_sb", (128, 3), I32)
    p.xm_sb = p.sb("xg_m_sb", (128, 2 * nev), I32)
    p.xk_sb = p.sb("xg_k_sb", (128, 1), F32)
    p.x_flags = [p.nc.dram_tensor("xg_flag%d" % e, [2, 2], I32, kind="Internal",
                                  addr_space="Shared") for e in range(nev)]
    p.x_regs = None


def emit_xchg_load(p, s):
    for k, t in enumerate((p.x_widx, p.x_gidx, p.x_fidx)):
        i = s.dma_start(p.xi_sb[:, k:k + 1], t.ap())
        p.inc(i, "xld", 16)
    i = s.dma_start(p.xm_sb[:, :], p.x_magic.ap())
    p.inc(i, "xld", 16)
    i = s.dma_start(p.xk_sb[:, :], p.x_mask.ap())
    p.inc(i, "xld", 16)


def emit_xsend(p, g, sh, src_ap, rows=128):
    g.wait_ge(p.sem("xld"), p.cnt["xld"])
    i = g.indirect_dma_start(
        sh.ap(), bass.IndirectOffsetOnAxis(ap=p.xi_sb[0:rows, 0:1], axis=0),
        src_ap, None)
    return p.inc(i, "xws", 16)


def emit_xflag(p, g, ev):
    g.wait_ge(p.sem("xws"), p.cnt["xws"])
    i = g.indirect_dma_start(
        p.x_flags[ev].ap(),
        bass.IndirectOffsetOnAxis(ap=p.xi_sb[0:2, 2:3], axis=0),
        p.xm_sb[0:2, 2 * ev:2 * ev + 2], None)
    p.inc(i, "xws", 16)


def emit_xpoll(p, g, ev, done_sem, done_val=1):
    nc = p.nc
    if p.x_regs is None:
        p.x_regs = tuple(g.alloc_register("xr%d" % k) for k in range(5))
    f0, f1, c0, c1, er = p.x_regs
    mr = c1
    g.wait_ge(p.sem("xws"), p.cnt["xws"])
    g.reg_mov(er, XPOLL_MAX)
    with nc.Fori(0, er, engines=[mybir.EngineType.Pool]):
        g.reg_load(f0, p.x_flags[ev].ap()[0:1, 0:1])
        g.reg_load(f1, p.x_flags[ev].ap()[1:2, 0:1])
        g.reg_load(mr, p.xm_sb[0:1, 2 * ev:2 * ev + 1])
        g.reg_alu(f0, f0, mr, ALU.subtract)
        g.reg_alu(f1, f1, mr, ALU.subtract)
        g.reg_alu(c0, f0, 0, ALU.is_equal)
        g.reg_alu(f1, f1, 0, ALU.is_equal)
        g.reg_alu(c0, c0, f1, ALU.mult)
        g.reg_alu(c0, c0, 0, ALU.is_equal)
        g.reg_alu(er, c0, XPOLL_MAX, ALU.mult)
    i = g.sem_inc(p.sem(done_sem), done_val)
    p.cnt[done_sem] = p.cnt.get(done_sem, 0) + done_val


def emit_xrecv(p, g, sh, dst_ap, rows=128):
    i = g.indirect_dma_start(
        dst_ap, None, sh.ap(),
        bass.IndirectOffsetOnAxis(ap=p.xi_sb[0:rows, 1:2], axis=0))
    return p.inc(i, "xws", 16)


def xchg_host_inputs(me, nev, rng, mg=None):
    pa = me ^ 1
    if mg is None:
        mg = rng.integers(1 << 20, 1 << 30, size=nev).astype(np.int32)
    return {
        "xg_widx": (me * 128 + np.arange(128)).astype(np.int32).reshape(128, 1),
        "xg_gidx": (pa * 128 + np.arange(128)).astype(np.int32).reshape(128, 1),
        "xg_fidx": np.full((128, 1), me, np.int32),
        "xg_magic": np.tile(np.repeat(mg, 2), (128, 1)),
        "xg_mask": np.full((128, 1), 0.0 if me == 0 else 1.0, np.float32),
    }


def emit_phase_gate(p, stores):
    nc = p.nc
    with nc.Block() as blk:
        @blk.sync
        def _(s):
            for nm in stores:
                if p.cnt.get(nm, 0):
                    s.wait_ge(p.sem(nm), p.cnt[nm])
            p.semtick(s, "phsy")
    k0 = p.cnt.get("phg", 0)
    ksy = p.cnt.get("phsy", 0)
    with nc.Block() as blk:
        @blk.tensor
        def _(t):
            t.wait_ge(p.sem("phsy"), ksy)
            p.semtick(t, "phg")
            t.wait_ge(p.sem("phg"), k0 + 4)
        @blk.vector
        def _(v):
            v.wait_ge(p.sem("phsy"), ksy)
            p.semtick(v, "phg")
            v.wait_ge(p.sem("phg"), k0 + 4)
        @blk.scalar
        def _(sc):
            sc.wait_ge(p.sem("phsy"), ksy)
            p.semtick(sc, "phg")
            sc.wait_ge(p.sem("phg"), k0 + 4)
        @blk.gpsimd
        def _(g):
            g.wait_ge(p.sem("phsy"), ksy)
            p.semtick(g, "phg")
            g.wait_ge(p.sem("phg"), k0 + 4)
    with nc.Block() as blk:
        @blk.sync
        def _(s):
            s.wait_ge(p.sem("phg"), k0 + 4)


def emit_barrier(p):
    nc = p.nc
    k0 = p.cnt.get("phg", 0)
    with nc.Block() as blk:
        @blk.tensor
        def _(t):
            p.semtick(t, "phg")
            t.wait_ge(p.sem("phg"), k0 + 4)
        @blk.vector
        def _(v):
            p.semtick(v, "phg")
            v.wait_ge(p.sem("phg"), k0 + 4)
        @blk.scalar
        def _(sc):
            p.semtick(sc, "phg")
            sc.wait_ge(p.sem("phg"), k0 + 4)
        @blk.gpsimd
        def _(g):
            p.semtick(g, "phg")
            g.wait_ge(p.sem("phg"), k0 + 4)


def emit_scan2(p, pv, uo, dlo, zzo, bco, acolP, ebc2, dpcolP, y_dr, sh_state, ev,
               no_xchg=False, no_xchg_dbg=("",)):
    nc = p.nc
    TTC = 512
    NTC2 = TOK // TTC
    psb = (pv["sc_ps0"][:, 0:512], pv["sc_ps0"][:, 512:1024])
    psc = (pv["sc_ps1"][:, 0:512], pv["sc_ps1"][:, 512:1024])
    acol = p.sb("sc_acol", (128, 256), F32)
    dpc = p.sb("sc_dpc", (128, 16), F32)
    eb_sb = p.sb("sc_eb", (32, 4096), F32)
    bct = p.sb("sc_bct", (32, TTC), F32)
    state = p.sb("sc_state", (128, 256), F32)
    state2 = p.sb("sc_state2", (128, 256), F32)
    bba = p.sb("sc_bba", (128, 16 * TTC), F32)
    cca = p.sb("sc_cca", (128, 16 * TTC), F32)
    dl_g = [p.sb("sc_dl%d" % k, (128, TTC), F32) for k in range(2)]
    u_g = [p.sb("sc_u%d" % k, (128, TTC), F32) for k in range(2)]
    zz_g = [p.sb("sc_zz%d" % k, (128, TTC), F32) for k in range(2)]
    du_g = [p.sb("sc_du%d" % k, (128, TTC), F32) for k in range(2)]
    dA_t = [p.sb("sc_dA%d" % k, (128, TTC), F32) for k in range(2)]
    gn_t = [p.sb("sc_gn%d" % k, (128, TTC), F32) for k in range(2)]
    h_t = [p.sb("sc_h%d" % k, (128, TTC), F32) for k in range(2)]
    tp_v = p.sb("sc_tpv", (128, TTC), F32)
    tp_g = p.sb("sc_tpg", (128, TTC), F32)
    yacc = p.sb("sc_yacc", (128, TTC), F32)
    t3_sb = p.sb("sc_t3", (128, TTC), F32)
    yb_sb = p.sb("sc_yb", (128, TTC), BF16)

    def C(nm):
        return p.cnt.get(nm, 0)

    with nc.Block() as blk:
        @blk.sync
        def _(s):
            for ap_o, t in ((acol.ap(), acolP), (dpc.ap(), dpcolP), (eb_sb.ap(), ebc2)):
                i = s.dma_start(ap_o, t.ap())
                p.inc(i, "sld", 16)
        @blk.scalar
        def _(sc):
            p.wait(sc, "sld")
            i = sc.activation(acol[:, :], acol[:, :], AF.Exp)
            p.inc(i, "sexp0")
        @blk.vector
        def _(v):
            p.wait(v, "sexp0")
            i = v.tensor_scalar(acol[:, :], acol[:, :], -1.0, None, ALU.mult)
            p.inc(i, "saneg")
            i = v.memset(state[:, :], 0.0)
            p.inc(i, "sst")

    for pp in range(2):
        st = state if pp == 0 else state2
        if pp == 1 and no_xchg:
            with nc.Block() as blk:
                @blk.vector
                def _(v):
                    v.wait_ge(p.sem("sst"), C("sst"))
                    i = v.memset(state2[:, :], 0.0)
                    p.inc(i, "sst")
        elif pp == 1:
            with nc.Block() as blk:
                @blk.gpsimd
                def _(g):
                    g.wait_ge(p.sem("sst"), C("sst"))
                    emit_xsend(p, g, sh_state, state[:, :])
                    emit_xflag(p, g, ev)
                    if "P" not in no_xchg_dbg[0]:
                        emit_xpoll(p, g, ev, "xpl")
                    emit_xrecv(p, g, sh_state, state2[:, :])
                    g.wait_ge(p.sem("xws"), C("xws"))
                    p.semtick(g, "xrc")
                @blk.vector
                def _(v):
                    p.wait(v, "xrc")
                    i = v.tensor_scalar(state2[:, :], state2[:, :],
                                        p.xk_sb[:, 0:1], None, ALU.mult)
                    p.inc(i, "sst")
        for tc in range(NTC2):
            t0 = tc * TTC
            nsel = 32 if pp == 1 else 16
            ndma = 48 if pp == 1 else 32
            b = {nm: C(nm) for nm in
                 ("slb", "slg_0", "slg_1", "sdu_0", "sdu_1", "smmb", "smmc",
                  "sbev", "sbrd", "sda_0", "sda_1", "sgn_0", "sgn_1",
                  "sscn_0", "sscn_1", "syf3", "syb", "syst", "stp", "sya",
                  "sst")}

            def kp(gch, n):
                return gch * 8 + n // 2

            with nc.Block() as blk:
                @blk.sync
                def _(s, pp=pp, tc=tc, t0=t0, b=b, ndma=ndma):
                    s.wait_ge(p.sem("smmb"), b["smmb"])
                    if pp == 1:
                        s.wait_ge(p.sem("smmc"), b["smmc"])
                    i = s.dma_start(bct.ap(), bco.ap()[:, t0:t0 + TTC])
                    p.inc(i, "slb", 16)
                    for gch in range(16):
                        gpar = gch % 2
                        if gch >= 2:
                            s.wait_ge(p.sem("sscn_0"), b["sscn_0"] + (gch - 1) * 8 + 1)
                            s.wait_ge(p.sem("sscn_1"), b["sscn_1"] + (gch - 1) * 8)
                            s.wait_ge(p.sem("sgn_0"), b["sgn_0"] + (gch - 1) * 8 + 1)
                            s.wait_ge(p.sem("sgn_1"), b["sgn_1"] + (gch - 1) * 8)
                            if pp == 1:
                                s.wait_ge(p.sem("syf3"), b["syf3"] + gch - 1)
                                s.wait_ge(p.sem("syb"), b["syb"] + gch - 1)
                        else:
                            s.wait_ge(p.sem("sscn_0"), b["sscn_0"])
                            s.wait_ge(p.sem("sscn_1"), b["sscn_1"])
                            s.wait_ge(p.sem("sgn_0"), b["sgn_0"])
                            s.wait_ge(p.sem("sgn_1"), b["sgn_1"])
                            if pp == 1:
                                s.wait_ge(p.sem("syf3"), b["syf3"])
                                s.wait_ge(p.sem("syb"), b["syb"])
                        i = s.dma_start(dl_g[gpar][:, :], dlo.ap()[gch][:, t0:t0 + TTC])
                        p.inc(i, "slg_%d" % gpar, 16)
                        i = s.dma_start(u_g[gpar][:, :], uo.ap()[gch][:, t0:t0 + TTC])
                        p.inc(i, "slg_%d" % gpar, 16)
                        if pp == 1:
                            i = s.dma_start(zz_g[gpar][:, :], zzo.ap()[gch][:, t0:t0 + TTC])
                            p.inc(i, "slg_%d" % gpar, 16)
                @blk.tensor
                def _(t, pp=pp, nsel=nsel, b=b):
                    p.wait(t, "slb")
                    for k in range(nsel):
                        isC = k >= 16
                        n = k % 16
                        pspair = psc if isC else psb
                        par = n % 2
                        if k >= 2:
                            t.wait_ge(p.sem("sbev"), b["sbev"] + k - 1)
                        i = t.matmul(pspair[par][:, 0:TTC],
                                     eb_sb[:, (2048 if isC else 0) + n * 128:
                                           (2048 if isC else 0) + (n + 1) * 128],
                                     bct[:, 0:TTC], start=True, stop=True)
                        p.inc(i, "smmc" if isC else "smmb")
                @blk.scalar
                def _(sc, pp=pp, tc=tc, t0=t0, b=b, ndma=ndma):
                    sc.wait_ge(p.sem("saneg"), C("saneg"))
                    for gch in range(16):
                        gpar = gch % 2
                        sc.wait_ge(p.sem("slg_%d" % gpar),
                                   b["slg_%d" % gpar] + (gch // 2 + 1) * ndma)
                        for n in range(16):
                            npar = n % 2
                            col = gch * 16 + n
                            k = kp(gch, n)
                            if k >= 2:
                                sc.wait_ge(p.sem("sscn_%d" % npar),
                                           b["sscn_%d" % npar] + k - 1)
                            i = sc.activation(dA_t[npar][:, :], dl_g[gpar][:, :],
                                              AF.Exp, scale=acol[:, col:col + 1])
                            p.inc(i, "sda_%d" % npar)
                        if pp == 1:
                            sc.wait_ge(p.sem("syb"), b["syb"] + gch + 1)
                            i = sc.dma_start(y_dr.ap()[gch][:, t0:t0 + TTC], yb_sb[:, :])
                            p.inc(i, "syst", 16)
                @blk.gpsimd
                def _(g, pp=pp, b=b, ndma=ndma):
                    g.wait_ge(p.sem("sbrd"), b["sbrd"] + 1)
                    for gch in range(16):
                        gpar = gch % 2
                        if pp == 1:
                            g.wait_ge(p.sem("slg_%d" % gpar),
                                      b["slg_%d" % gpar] + (gch // 2 + 1) * ndma)
                        g.wait_ge(p.sem("sdu_%d" % gpar), b["sdu_%d" % gpar] + gch // 2 + 1)
                        for n in range(16):
                            npar = n % 2
                            k = kp(gch, n)
                            if k >= 2:
                                g.wait_ge(p.sem("sscn_%d" % npar),
                                          b["sscn_%d" % npar] + k - 1)
                            i = g.tensor_mul(gn_t[npar][:, :], du_g[gpar][:, :],
                                             bba[:, n * TTC:(n + 1) * TTC])
                            p.inc(i, "sgn_%d" % npar)
                            if pp == 1 and npar == 1:
                                g.wait_ge(p.sem("sscn_1"), b["sscn_1"] + k + 1)
                                if gch * 16 + n - 1 > 0:
                                    g.wait_ge(p.sem("sya"), b["sya"] + gch * 16 + n - 1)
                                i = g.tensor_mul(tp_g[:, :], h_t[1][:, :],
                                                 cca[:, n * TTC:(n + 1) * TTC])
                                p.inc(i, "stp")
                        if pp == 1:
                            g.wait_ge(p.sem("syf3"), b["syf3"] + gch + 1)
                            if b["syst"] + 16 * gch > 0:
                                g.wait_ge(p.sem("syst"), b["syst"] + 16 * gch)
                            i = g.tensor_mul(yb_sb[:, :], t3_sb[:, :], zz_g[gpar][:, :])
                            p.inc(i, "syb")
                @blk.vector
                def _(v, pp=pp, tc=tc, b=b, nsel=nsel, ndma=ndma):
                    v.wait_ge(p.sem("sgn_0"), b["sgn_0"])
                    v.wait_ge(p.sem("sgn_1"), b["sgn_1"])
                    if pp == 1:
                        v.wait_ge(p.sem("stp"), b["stp"])
                    nb = ncc = 0
                    for k in range(nsel):
                        isC = k >= 16
                        n = k % 16
                        pspair = psc if isC else psb
                        par = n % 2
                        if isC:
                            ncc += 1
                            v.wait_ge(p.sem("smmc"), b["smmc"] + ncc)
                            dst = cca
                        else:
                            nb += 1
                            v.wait_ge(p.sem("smmb"), b["smmb"] + nb)
                            dst = bba
                        i = v.tensor_copy(dst[:, n * TTC:(n + 1) * TTC],
                                          pspair[par][:, 0:TTC])
                        p.inc(i, "sbev")
                    p.semtick(v, "sbrd")
                    if pp == 1 and tc == 0:
                        v.wait_ge(p.sem("sst"), b["sst"])
                    for gch in range(16):
                        gpar = gch % 2
                        v.wait_ge(p.sem("slg_%d" % gpar),
                                  b["slg_%d" % gpar] + (gch // 2 + 1) * ndma)
                        gt = b["sgn_0"] + ((gch - 1) * 8 if gch >= 2 else 0)
                        v.wait_ge(p.sem("sgn_0"), gt)
                        gt = b["sgn_1"] + ((gch - 1) * 8 if gch >= 2 else 0)
                        v.wait_ge(p.sem("sgn_1"), gt)
                        i = v.tensor_mul(du_g[gpar][:, :], dl_g[gpar][:, :], u_g[gpar][:, :])
                        p.inc(i, "sdu_%d" % gpar)
                        for n in range(16):
                            npar = n % 2
                            col = gch * 16 + n
                            k = kp(gch, n)
                            v.wait_ge(p.sem("sda_%d" % npar), b["sda_%d" % npar] + k + 1)
                            v.wait_ge(p.sem("sgn_%d" % npar), b["sgn_%d" % npar] + k + 1)
                            i = v.tensor_tensor_scan(h_t[npar][:, :], dA_t[npar][:, :],
                                                     gn_t[npar][:, :],
                                                     st[:, col:col + 1],
                                                     ALU.mult, ALU.add)
                            p.serial(v, i)
                            i = v.tensor_copy(st[:, col:col + 1],
                                              h_t[npar][:, TTC - 1:TTC])
                            p.inc(i, "sscn_%d" % npar)
                            if pp == 0 and tc == NTC2 - 1:
                                p.semtick(v, "sst")
                            if pp == 1:
                                if npar == 0:
                                    i = v.tensor_mul(tp_v[:, :], h_t[0][:, :],
                                                     cca[:, n * TTC:(n + 1) * TTC])
                                    p.serial(v, i)
                                    src_t = tp_v
                                else:
                                    v.wait_ge(p.sem("stp"), b["stp"] + gch * 8 + n // 2 + 1)
                                    src_t = tp_g
                                if n == 0:
                                    i = v.tensor_copy(yacc[:, :], src_t[:, :])
                                else:
                                    i = v.tensor_tensor(yacc[:, :], yacc[:, :],
                                                        src_t[:, :], ALU.add)
                                p.inc(i, "sya")
                        if pp == 1:
                            v.wait_ge(p.sem("syb"), b["syb"] + gch)
                            i = v.scalar_tensor_tensor(t3_sb[:, :], u_g[gpar][:, :],
                                                       dpc[:, gch:gch + 1],
                                                       yacc[:, :], ALU.mult, ALU.add)
                            p.inc(i, "syf3")


def build_g2sg3(name="G2S3", dbg=""):
    NI2 = 16
    p = Prog(name)
    nc = p.nc
    x1h = p.inp("x1h", (NCH, 128, TOK + 3))
    h2h = p.inp("h2h", (NCH, 128, TOK + 3))
    win = p.inp("win", (NCH, 128, 4096))
    convw = p.inp("convw", (128, 64))
    convb = p.inp("convb", (128, 16))
    xp = p.inp("xp", (NI2, 128, 96))
    dtw = p.inp("dtw", (64, 2048))
    dtb = p.inp("dtb", (128, 16))
    acolP = p.inp("acolP", (128, 256))
    ebc2 = p.inp("ebc2", (32, 4096))
    dpcolP = p.inp("dpcolP", (128, 16))
    wout = p.inp("wout", (NI2, 128, D))
    xo = p.outp("xo", (NCH, 128, TOK))
    z = p.outp("z", (NCH, 128, TOK))
    x2o = p.scratch("x2s", (NCH, 128, TOK))
    uo = p.scratch("us", (NI2, 128, TOK))
    dlo = p.scratch("dls", (NI2, 128, TOK))
    zzo = p.scratch("zzs", (NI2, 128, TOK))
    bco = p.scratch("bcs", (32, TOK))
    y_dr = p.scratch("ys", (NI2, 128, TOK), BF16)
    dbgo = p.outp("dbgo", (128, 520))
    p.dbgo = dbgo
    sh_state = nc.dram_tensor("sh_state", [256, 256], F32, kind="Internal",
                              addr_space="Shared")

    p.setup_consts(vals=(LN_EPS, -1.0))
    setup_xchg(p, 1)
    ar = Arena(p, "big_arena", 211072)
    PS = p.ps("PS", (128, 1024))
    ps_a = p.ps("ps_a", (128, 512))
    ps_b = p.ps("ps_b", (128, 512))
    ps_c = p.ps("ps_c", (128, 512))
    ps_D = p.ps("ps_D", (128, 1024))
    pv = {"PS": PS, "ps_a": ps_a, "ps_b": ps_b, "ps_c": ps_c,
          "sc_ps0": PS, "sc_ps1": ps_D}

    with nc.Block() as blk:
        @blk.sync
        def _(s):
            emit_xchg_load(p, s)

    p.set_arena(ar)
    emit_mfront(p, pv, x1h, h2h, win, convw, convb, xp, dtw, dtb,
                x2o, uo, dlo, zzo, bco, consts=True)
    emit_phase_gate(p, ("stx", "stu", "std", "stz", "stb"))
    ar.reset()
    if "S" not in dbg:
        emit_scan2(p, pv, uo, dlo, zzo, bco, acolP, ebc2, dpcolP, y_dr, sh_state, 0,
                   no_xchg=("X" in dbg), no_xchg_dbg=(dbg,))
        emit_phase_gate(p, ("syst",))
    ar.reset()
    if "G" in dbg:
        with nc.Block() as blk:
            @blk.sync
            def _(s):
                i = s.dma_start(xo.ap(), x2o.ap())
                p.inc(i, "dst", 16)
                i = s.dma_start(z.ap(), x2o.ap())
                p.inc(i, "dst", 16)
                p.wait(s, "dst")
        return p
    env3 = make_ln_env(p, 512, alloc_psum=False, pref="g3")
    env3["ps_stat"] = ps_c[:, 0:8]
    env3["ps_tra"] = ps_D[0:1, 0:512]
    env3["ps_trb"] = ps_D[0:1, 512:1024]
    env3["ps_bca"] = PS[:, 0:512]
    env3["ps_bcb"] = PS[:, 512:1024]
    emit_projresln(p, env3, ps_a, x2o, y_dr, wout, xo, z, NI2, True, pref="g3")
    return p


def scan_host_weights(Alog, Dp):
    alogP = np.ascontiguousarray(
        np.asarray(Alog, np.float32).reshape(16, 128, 16).transpose(1, 0, 2)
        .reshape(128, 256))
    ebc2 = np.zeros((32, 4096), np.float32)
    for n_ in range(16):
        ebc2[n_, n_ * 128:(n_ + 1) * 128] = 1.0
        ebc2[16 + n_, 2048 + n_ * 128:2048 + (n_ + 1) * 128] = 1.0
    dpcolP = np.ascontiguousarray(np.asarray(Dp, np.float32).reshape(16, 128).T)
    return {"acolP": alogP, "ebc2": ebc2, "dpcolP": dpcolP}


RA = 3 * 1024 // 2 * 2 + 0


def build_mega(name="MEGA"):
    NI2 = 16
    RA_ = 2 * 512 + 2048
    RB_ = 3 * 256 + 2048
    HA = 2 * 512 + 1
    HB = 3 * 256 + 1
    p = Prog(name)
    nc = p.nc
    x_full = p.inp("x", (NCH, 128, S))
    xg1 = p.inp("xg1", (NCH, 128, TOK))
    wqfl = p.inp("wq", (NCH, 128, 1536))
    g1w = p.inp("g1w", (NCH, 128, D))
    lio = {}
    for st_, nl in (("A", 2), ("B", 6)):
        for l in range(nl):
            for nm, shp in (("wqkv", (NCH, 128, 3 * (512 if st_ == "A" else 1024))),
                            ("wo", ((4 if st_ == "A" else 8), 128, D)),
                            ("lncol", (128, 2 * NCH)), ("qskcol", (64, 2)),
                            ("w1p", (NCH, 128, HGN * 128)), ("b1acol", (128, HGN)),
                            ("b1gcol", (128, HGN)), ("w2p", (HP2, 128, D)),
                            ("b2col", (128, NCH))):
                lio["%s%d_%s" % (st_, l, nm)] = p.inp("%s%d_%s" % (st_, l, nm), shp)
    A_tabs = p.inp("A_tabs", (64, 4 * RA_))
    A_wkill = p.inp("A_wkill", (128, RA_ // 512))
    B_tabs = p.inp("B_tabs", (64, 4 * RB_))
    B_wkill = p.inp("B_wkill", (128, RB_ // 256))
    win = p.inp("win", (NCH, 128, 4096))
    convw = p.inp("convw", (128, 64))
    convb = p.inp("convb", (128, 16))
    xp = p.inp("xp", (NI2, 128, 96))
    dtw = p.inp("dtw", (64, 2048))
    dtb = p.inp("dtb", (128, 16))
    acolP = p.inp("acolP", (128, 256))
    ebc2 = p.inp("ebc2", (32, 4096))
    dpcolP = p.inp("dpcolP", (128, 16))
    wout = p.inp("wout", (NI2, 128, D))
    oscat_i = p.inp("oscat_idx", (128, 32), I32)
    ogat_i = p.inp("ogat_idx", (128, 32), I32)
    out = p.outp("out", (NCH, 128, TOK))
    O_sh = nc.dram_tensor("O_sh", [2 * 4096, 512], BF16, kind="Internal",
                          addr_space="Shared")
    act_dr = p.scratch("act_dr", (NCH, 128, TOK), BF16)
    x1h = p.scratch("x1h", (NCH, 128, TOK + 3))
    hA = [p.scratch("hA%d" % k, (NCH, 128, RA_ + 1)) for k in range(2)]
    hmidA = p.scratch("hmidA", (NCH, 128, RA_))
    hhatA = p.scratch("hhatA", (NCH, 128, RA_), BF16)
    x2s = p.scratch("x2s", (NCH, 128, TOK))
    us = p.scratch("us", (NI2, 128, TOK))
    dls = p.scratch("dls", (NI2, 128, TOK))
    zzs = p.scratch("zzs", (NI2, 128, TOK))
    bcs = p.scratch("bcs", (32, TOK))
    ys = p.scratch("ys", (NI2, 128, TOK), BF16)
    x3_dr = p.scratch("x3_dr", (NCH, 128, TOK))
    hB = [p.scratch("hB%d" % k, (NCH, 128, RB_ + 1)) for k in range(2)]
    hmidB = p.scratch("hmidB", (NCH, 128, RB_))
    hhatB = p.scratch("hhatB", (NCH, 128, RB_), BF16)
    sh_state = nc.dram_tensor("sh_state", [256, 256], F32, kind="Internal",
                              addr_space="Shared")
    sh_E1 = nc.dram_tensor("sh_E1", [256, NCH * (3 + HA)], F32, kind="Internal",
                           addr_space="Shared")
    sh_E2 = nc.dram_tensor("sh_E2", [256, NCH * HA], F32, kind="Internal",
                           addr_space="Shared")
    sh_E3 = nc.dram_tensor("sh_E3", [256, NCH * 3], F32, kind="Internal",
                           addr_space="Shared")
    sh_EB = [nc.dram_tensor("sh_EB%d" % k, [256, NCH * HB], F32, kind="Internal",
                            addr_space="Shared") for k in range(6)]
    NEV = 11
    p.setup_consts(vals=(LN_EPS, 1e-12, -1.0, 0.125, -0.125))
    setup_xchg(p, NEV)
    osc_sb = p.sb("osc_sb", (128, 32), I32)
    oga_sb = p.sb("oga_sb", (128, 32), I32)
    ar = Arena(p, "big_arena", 210624)
    PS = p.ps("PS", (128, 2048))
    ps_a = p.ps("ps_a", (128, 512))
    ps_b = p.ps("ps_b", (128, 512))
    ps_c = p.ps("ps_c", (128, 512))
    ps_t = p.ps("ps_t", (128, 1024), BF16)
    ALLST = ("sto", "oat", "st", "sth", "stf", "stx", "stu", "std", "stz",
             "stb", "syst", "sout", "xws", "hal")

    def gate():
        emit_phase_gate(p, ALLST)
        ar.reset()
        p.new_phase()

    with nc.Block() as blk:
        @blk.sync
        def _(s):
            emit_xchg_load(p, s)
            i = s.dma_start(osc_sb[:, :], oscat_i.ap())
            p.inc(i, "xld", 16)
            i = s.dma_start(oga_sb[:, :], ogat_i.ap())
            p.inc(i, "xld", 16)

    p.set_arena(ar)
    pvf = {"PS": PS[:, 0:512], "PS2": PS[:, 512:1024], "ps_a": ps_a,
           "ps_b": ps_b, "ps_c": ps_c, "ps_t": ps_t}
    emit_flash(p, pvf, x_full, wqfl, None, consts=True, oscat=(O_sh, osc_sb))
    with nc.Block() as blk:
        @blk.gpsimd
        def _(g):
            g.wait_ge(p.sem("sto"), p.cnt["sto"])
            emit_xflag(p, g, 0)
    gate()
    og_sb = p.sb("og_sb", (128, 1024), BF16)
    at_sb = p.sb("at_sb", (128, 1024), BF16)
    with nc.Block() as blk:
        @blk.gpsimd
        def _(g):
            emit_xpoll(p, g, 0, "xpl0")
        @blk.vector
        def _(v):
            v.wait_ge(p.sem("xpl0"), p.cnt.get("xpl0", 0))
            p.semtick(v, "xpl0v")
    for tt in range(16):
        c_og = p.cnt.get("ogl", 0)
        c_otr = p.cnt.get("oatr", 0)
        c_oev = p.cnt.get("oaev", 0)
        c_ost = p.cnt.get("oat", 0)
        with nc.Block() as blk:
            @blk.gpsimd
            def _(g, tt=tt, c_oev=c_oev):
                if c_oev:
                    g.wait_ge(p.sem("oaev"), c_oev)
                for s_ in range(2):
                    i = g.indirect_dma_start(
                        og_sb[:, s_ * 512:(s_ + 1) * 512], None, O_sh.ap(),
                        bass.IndirectOffsetOnAxis(ap=oga_sb[:, 2 * tt + s_:2 * tt + s_ + 1],
                                                  axis=0))
                    p.inc(i, "ogl", 16)
            @blk.tensor
            def _(t, c_og=c_og, c_oev=c_oev):
                t.wait_ge(p.sem("ogl"), c_og + 32)
                if c_oev:
                    t.wait_ge(p.sem("oaev"), c_oev)
                for c8 in range(8):
                    i = t.transpose(ps_t[:, c8 * 128:(c8 + 1) * 128],
                                    og_sb[:, c8 * 128:(c8 + 1) * 128], p.identb[:, :])
                    p.inc(i, "oatr")
            @blk.vector
            def _(v, c_otr=c_otr, c_ost=c_ost):
                v.wait_ge(p.sem("oatr"), c_otr + 8)
                if c_ost:
                    v.wait_ge(p.sem("oat"), c_ost)
                i = v.tensor_copy(at_sb[:, :], ps_t[:, 0:1024])
                p.inc(i, "oaev")
            @blk.sync
            def _(s, tt=tt):
                s.wait_ge(p.sem("oaev"), p.cnt["oaev"])
                i = s.dma_start(act_dr.ap()[:, :, tt * 128:(tt + 1) * 128].transpose([1, 0, 2]),
                                at_sb.ap().rearrange("p (c n) -> p c n", c=NCH))
                p.inc(i, "oat", 16)
    gate()
    env1 = make_ln_env(p, 512, alloc_psum=False, pref="g1")
    env1["ps_stat"] = ps_c[:, 0:8]
    env1["ps_tra"] = PS[0:1, 1024:1536]
    env1["ps_trb"] = PS[0:1, 1536:2048]
    env1["ps_bca"] = PS[:, 0:512]
    env1["ps_bcb"] = PS[:, 512:1024]
    emit_projresln(p, env1, ps_a, xg1, act_dr, g1w,
                   ArView(x1h.ap()[:, :, 3:3 + TOK]),
                   ArView(hA[0].ap()[:, :, HA:HA + TOK]), NCH, True, pref="g1")
    gate()
    exw = p.sb("exw", (128, NCH * (3 + HA)), F32)
    CW1 = 3 + HA
    with nc.Block() as blk:
        @blk.sync
        def _(s):
            p.wait(s, "st")
            i = s.dma_start(exw.ap().rearrange("p (c n) -> p c n", c=NCH)[:, :, 0:3],
                            x1h.ap()[:, :, TOK:TOK + 3].transpose([1, 0, 2]))
            p.inc(i, "hal", 16)
            i = s.dma_start(exw.ap().rearrange("p (c n) -> p c n", c=NCH)[:, :, 3:CW1],
                            hA[0].ap()[:, :, TOK:TOK + HA].transpose([1, 0, 2]))
            p.inc(i, "hal", 16)
        @blk.gpsimd
        def _(g):
            g.wait_ge(p.sem("hal"), p.cnt["hal"])
            emit_xsend(p, g, sh_E1, exw[:, :])
            emit_xflag(p, g, 1)
            emit_xpoll(p, g, 1, "xpl")
            emit_xrecv(p, g, sh_E1, exw[:, :])
            g.wait_ge(p.sem("xws"), p.cnt["xws"])
            p.semtick(g, "xrc")
        @blk.vector
        def _(v):
            p.wait(v, "xrc")
            i = v.tensor_scalar(exw[:, :], exw[:, :], p.xk_sb[:, 0:1], None, ALU.mult)
            p.inc(i, "hmk")
        @blk.sync
        def _(s):
            p.wait(s, "hmk")
            i = s.dma_start(x1h.ap()[:, :, 0:3].transpose([1, 0, 2]),
                            exw.ap().rearrange("p (c n) -> p c n", c=NCH)[:, :, 0:3])
            p.inc(i, "hal", 16)
            i = s.dma_start(hA[0].ap()[:, :, 0:HA].transpose([1, 0, 2]),
                            exw.ap().rearrange("p (c n) -> p c n", c=NCH)[:, :, 3:CW1])
            p.inc(i, "hal", 16)
            p.wait(s, "hal")
    gate()

    def tail_xchg(sh, src_ap_cols, dst_ap_cols, ncols, ev):
        tsb = p.sb("tx%d" % ev, (128, NCH * ncols), F32)
        with nc.Block() as blk:
            @blk.sync
            def _(s):
                for nm in ALLST:
                    if p.cnt.get(nm, 0):
                        s.wait_ge(p.sem(nm), p.cnt[nm])
                i = s.dma_start(tsb.ap().rearrange("p (c n) -> p c n", c=NCH),
                                src_ap_cols.transpose([1, 0, 2]))
                p.inc(i, "hal", 16)
            @blk.gpsimd
            def _(g):
                g.wait_ge(p.sem("hal"), p.cnt["hal"])
                emit_xsend(p, g, sh, tsb[:, :])
                emit_xflag(p, g, ev)
                emit_xpoll(p, g, ev, "xpl")
                emit_xrecv(p, g, sh, tsb[:, :])
                g.wait_ge(p.sem("xws"), p.cnt["xws"])
                p.semtick(g, "xrc")
            @blk.vector
            def _(v):
                p.wait(v, "xrc")
                i = v.tensor_scalar(tsb[:, :], tsb[:, :], p.xk_sb[:, 0:1], None,
                                    ALU.mult)
                p.inc(i, "hmk")
            @blk.sync
            def _(s):
                p.wait(s, "hmk")
                i = s.dma_start(dst_ap_cols.transpose([1, 0, 2]),
                                tsb.ap().rearrange("p (c n) -> p c n", c=NCH))
                p.inc(i, "hal", 16)
                p.wait(s, "hal")

    pvl = {"PS": PS, "ps_a": ps_a, "ps_b": ps_b, "ps_c": ps_c, "ps_t": ps_t}
    for l in range(2):
        io = {k: lio["A%d_%s" % (l, k)] for k in
              ("wqkv", "wo", "lncol", "qskcol", "w1p", "b1acol", "b1gcol",
               "w2p", "b2col")}
        io["tabs"] = A_tabs
        io["wkill"] = A_wkill
        io["h"] = hA[l % 2]
        io["hout"] = ArView(hA[(l + 1) % 2].ap()[:, :, HA:HA + TOK])
        io["hmid_dr"] = hmidA
        io["hhat_dr"] = hhatA
        emit_layer(p, pvl, 512, 8, io)
        gate()
        if l == 0:
            tail_xchg(sh_E2, hA[1].ap()[:, :, TOK:TOK + HA],
                      hA[1].ap()[:, :, 0:HA], HA, 2)
            gate()
    tail_xchg(sh_E3, hA[0].ap()[:, :, TOK + HA - 3:TOK + HA],
              hA[0].ap()[:, :, HA - 3:HA], 3, 3)
    gate()
    pv2 = {"PS": PS[:, 0:1024], "ps_a": ps_a, "ps_b": ps_b, "ps_c": ps_c}
    emit_mfront(p, pv2, x1h, ArView(hA[0].ap()[:, :, HA - 3:HA + TOK]),
                win, convw, convb, xp, dtw, dtb, x2s, us, dls, zzs, bcs)
    gate()
    pv3 = {"sc_ps0": PS[:, 0:1024], "sc_ps1": PS[:, 1024:2048]}
    emit_scan2(p, pv3, us, dls, zzs, bcs, acolP, ebc2, dpcolP, ys, sh_state, 4)
    gate()
    env3 = make_ln_env(p, 512, alloc_psum=False, pref="g3")
    env3["ps_stat"] = ps_c[:, 0:8]
    env3["ps_tra"] = PS[0:1, 1024:1536]
    env3["ps_trb"] = PS[0:1, 1536:2048]
    env3["ps_bca"] = PS[:, 0:512]
    env3["ps_bcb"] = PS[:, 512:1024]
    emit_projresln(p, env3, ps_a, x2s, ys, wout, x3_dr,
                   ArView(hB[0].ap()[:, :, HB:HB + TOK]), NI2, True, pref="g3")
    gate()
    tail_xchg(sh_EB[0], hB[0].ap()[:, :, TOK:TOK + HB],
              hB[0].ap()[:, :, 0:HB], HB, 5)
    gate()
    for l in range(6):
        io = {k: lio["B%d_%s" % (l, k)] for k in
              ("wqkv", "wo", "lncol", "qskcol", "w1p", "b1acol", "b1gcol",
               "w2p", "b2col")}
        io["tabs"] = B_tabs
        io["wkill"] = B_wkill
        io["h"] = hB[l % 2]
        io["hout"] = ArView(hB[(l + 1) % 2].ap()[:, :, HB:HB + TOK])
        io["hmid_dr"] = hmidB
        io["hhat_dr"] = hhatB
        emit_layer(p, pvl, 256, 16, io)
        gate()
        if l < 5:
            tail_xchg(sh_EB[l + 1], hB[(l + 1) % 2].ap()[:, :, TOK:TOK + HB],
                      hB[(l + 1) % 2].ap()[:, :, 0:HB], HB, 6 + l)
            gate()
    hfin = hB[0]
    ga = p.sb("g4a", (128, NCH * 512), F32)
    gb = p.sb("g4b", (128, NCH * 512), F32)
    go = p.sb("g4o", (128, NCH * 512), F32)
    for it in range(TOK // 512):
        t0 = it * 512
        c_g4 = p.cnt.get("g4e", 0)
        with nc.Block() as blk:
            @blk.sync
            def _(s, t0=t0, c_g4=c_g4):
                if c_g4:
                    s.wait_ge(p.sem("g4e"), c_g4)
                i = s.dma_start(ga.ap().rearrange("p (c n) -> p c n", c=NCH),
                                x3_dr.ap()[:, :, t0:t0 + 512].transpose([1, 0, 2]))
                p.inc(i, "g4l", 16)
                i = s.dma_start(gb.ap().rearrange("p (c n) -> p c n", c=NCH),
                                hfin.ap()[:, :, HB + t0:HB + t0 + 512].transpose([1, 0, 2]))
                p.inc(i, "g4l", 16)
            @blk.vector
            def _(v, c_g4=c_g4):
                v.wait_ge(p.sem("g4l"), p.cnt["g4l"])
                if p.cnt.get("sout", 0):
                    v.wait_ge(p.sem("sout"), p.cnt["sout"])
                i = v.tensor_add(go[:, :], ga[:, :], gb[:, :])
                p.inc(i, "g4e")
            @blk.sync
            def _(s, t0=t0):
                s.wait_ge(p.sem("g4e"), p.cnt["g4e"])
                i = s.dma_start(out.ap()[:, :, t0:t0 + 512].transpose([1, 0, 2]),
                                go.ap().rearrange("p (c n) -> p c n", c=NCH))
                p.inc(i, "sout", 16)
    with nc.Block() as blk:
        @blk.sync
        def _(s):
            s.wait_ge(p.sem("sout"), p.cnt["sout"])
        @blk.gpsimd
        def _(g):
            g.wait_ge(p.sem("sout"), p.cnt["sout"])
    return p


_PROGS = {}

def _prog(key, fn):
    if key not in _PROGS:
        _PROGS[key] = fn()
    return _PROGS[key]


def _run(p, in_maps, n=8):
    return run_bass_kernel_spmd(p.nc, in_maps, core_ids=list(range(n))).results


def _assemble_tok(results, key, nch=NCH):
    full = np.zeros((B, S, nch * 128), np.float32)
    for b_ in range(B):
        for hf in range(2):
            full[b_, hf * TOK:(hf + 1) * TOK] = unfm(results[b_ * 2 + hf][key].astype(np.float32))
    return full


def kernel(L0_attn_x, L0_attn_W_qkv, L0_attn_W_o, L0_ffn_wqkv, L0_ffn_q_scale,
           L0_ffn_k_scale, L0_ffn_wo, L0_ffn_ln_g, L0_ffn_ln_b, L0_ffn_w1,
           L0_ffn_b1, L0_ffn_w2, L0_ffn_b2, L1_attn_in_proj_w, L1_attn_conv_w,
           L1_attn_conv_b, L1_attn_x_proj_w, L1_attn_dt_proj_w, L1_attn_dt_proj_b,
           L1_attn_A_log, L1_attn_Dp, L1_attn_out_proj_w, L1_ffn_Wqkv,
           L1_ffn_q_scale, L1_ffn_k_scale, L1_ffn_Wo, L1_ffn_ln_g, L1_ffn_ln_b,
           L1_ffn_W1, L1_ffn_b1, L1_ffn_W2, L1_ffn_b2):
    x = np.asarray(L0_attn_x, np.float32)
    pm = _prog("MEGA", build_mega)
    Wq = np.asarray(L0_attn_W_qkv, np.float32)
    wo_ch = wchunks(np.asarray(L0_attn_W_o, np.float32))
    wdA = [layer_weights(512, 8, np.asarray(L0_ffn_wqkv[l], np.float32),
                         np.asarray(L0_ffn_q_scale[l], np.float32),
                         np.asarray(L0_ffn_k_scale[l], np.float32),
                         np.asarray(L0_ffn_wo[l], np.float32),
                         np.asarray(L0_ffn_ln_g[l], np.float32),
                         np.asarray(L0_ffn_ln_b[l], np.float32),
                         np.asarray(L0_ffn_w1[l], np.float32),
                         np.asarray(L0_ffn_b1[l], np.float32),
                         np.asarray(L0_ffn_w2[l], np.float32),
                         np.asarray(L0_ffn_b2[l], np.float32)) for l in range(2)]
    wdB = [layer_weights(256, 16, np.asarray(L1_ffn_Wqkv[l], np.float32),
                         np.asarray(L1_ffn_q_scale[l], np.float32),
                         np.asarray(L1_ffn_k_scale[l], np.float32),
                         np.asarray(L1_ffn_Wo[l], np.float32),
                         np.asarray(L1_ffn_ln_g[l], np.float32),
                         np.asarray(L1_ffn_ln_b[l], np.float32),
                         np.asarray(L1_ffn_W1[l], np.float32),
                         np.asarray(L1_ffn_b1[l], np.float32),
                         np.asarray(L1_ffn_W2[l], np.float32),
                         np.asarray(L1_ffn_b2[l], np.float32)) for l in range(6)]
    win_t = np.ascontiguousarray(np.asarray(L1_attn_in_proj_w, np.float32).T)
    cw = np.asarray(L1_attn_conv_w, np.float32)[:, 0, :]
    convw_col = np.zeros((128, 64), np.float32)
    for c in range(16):
        for tau in range(4):
            convw_col[:, c * 4 + tau] = cw[c * 128:(c + 1) * 128, tau]
    wd2 = {"win": wchunks(win_t), "convw": convw_col,
           "convb": np.ascontiguousarray(np.asarray(L1_attn_conv_b, np.float32).reshape(16, 128).T),
           "xp": wchunks(np.ascontiguousarray(np.asarray(L1_attn_x_proj_w, np.float32).T)),
           "dtw": np.ascontiguousarray(np.asarray(L1_attn_dt_proj_w, np.float32).T),
           "dtb": np.ascontiguousarray(np.asarray(L1_attn_dt_proj_b, np.float32).reshape(16, 128).T)}
    wd2.update(scan_host_weights(L1_attn_A_log, L1_attn_Dp))
    wd2["wout"] = wchunks(np.ascontiguousarray(np.asarray(L1_attn_out_proj_w, np.float32).T))
    rng = np.random.default_rng()
    mg11 = rng.integers(1 << 20, 1 << 30, size=11).astype(np.int32)
    tabsA = [layer_tables(512, hf * TOK) for hf in range(2)]
    tabsB = [layer_tables(256, hf * TOK) for hf in range(2)]
    ins = []
    for b_ in range(B):
        for hf in range(2):
            d_ = {}
            d_["x"] = fm(x[b_])
            d_["xg1"] = fm(x[b_, hf * TOK:(hf + 1) * TOK])
            hg = hf
            wsl = np.concatenate([Wq[:, hg * 512:(hg + 1) * 512],
                                  Wq[:, 1024 + hg * 512:1024 + (hg + 1) * 512],
                                  Wq[:, 2048 + hg * 512:2048 + (hg + 1) * 512]], 1)
            d_["wq"] = wchunks(wsl)
            d_["g1w"] = np.ascontiguousarray(np.concatenate(
                [wo_ch[hg * 4:hg * 4 + 4], wo_ch[(1 - hg) * 4:(1 - hg) * 4 + 4]], 0))
            for st_, wds, W_ in (("A", wdA, 512), ("B", wdB, 256)):
                for l, wd in enumerate(wds):
                    for k, v in wd.items():
                        d_["%s%d_%s" % (st_, l, k)] = v
            HALO_A, HALO_B = 2, 3
            wkA = np.zeros((128, (2 * 512 + 2048) // 512), np.float32)
            wkB = np.zeros((128, (3 * 256 + 2048) // 256), np.float32)
            if hf == 0:
                wkA[:, HALO_A] = NEG
                wkB[:, HALO_B] = NEG
            d_["A_tabs"] = tabsA[hf]
            d_["A_wkill"] = wkA
            d_["B_tabs"] = tabsB[hf]
            d_["B_wkill"] = wkB
            d_.update(wd2)
            d_.update(xchg_host_inputs(hf, 11, rng, mg11))
            me, pa = hf, hf ^ 1
            osc = np.zeros((128, 32), np.int32)
            oga = np.zeros((128, 32), np.int32)
            for qt in range(32):
                osc[:, qt] = me * 4096 + qt * 128 + np.arange(128)
            for tt in range(16):
                for s_ in range(2):
                    slot = me if s_ == 0 else pa
                    oga[:, 2 * tt + s_] = slot * 4096 + hf * 2048 + tt * 128 + np.arange(128)
            d_["oscat_idx"] = osc
            d_["ogat_idx"] = oga
            ins.append(d_)
    r = _run(pm, ins)
    return _assemble_tok(r, "out").astype(np.float32)


def kernel_multi(L0_attn_x, L0_attn_W_qkv, L0_attn_W_o, L0_ffn_wqkv, L0_ffn_q_scale,
           L0_ffn_k_scale, L0_ffn_wo, L0_ffn_ln_g, L0_ffn_ln_b, L0_ffn_w1,
           L0_ffn_b1, L0_ffn_w2, L0_ffn_b2, L1_attn_in_proj_w, L1_attn_conv_w,
           L1_attn_conv_b, L1_attn_x_proj_w, L1_attn_dt_proj_w, L1_attn_dt_proj_b,
           L1_attn_A_log, L1_attn_Dp, L1_attn_out_proj_w, L1_ffn_Wqkv,
           L1_ffn_q_scale, L1_ffn_k_scale, L1_ffn_Wo, L1_ffn_ln_g, L1_ffn_ln_b,
           L1_ffn_W1, L1_ffn_b1, L1_ffn_W2, L1_ffn_b2):
    x = np.asarray(L0_attn_x, np.float32)

    pf = _prog("flash", build_flash)
    Wq = np.asarray(L0_attn_W_qkv, np.float32)
    ins = []
    for b_ in range(B):
        for hg in range(2):
            wsl = np.concatenate([Wq[:, hg * 512:(hg + 1) * 512],
                                  Wq[:, 1024 + hg * 512:1024 + (hg + 1) * 512],
                                  Wq[:, 2048 + hg * 512:2048 + (hg + 1) * 512]], 1)
            ins.append({"x": fm(x[b_]), "wq": wchunks(wsl)})
    rf = _run(pf, ins)
    O_full = np.zeros((B, S, D), np.float32)
    for b_ in range(B):
        for hg in range(2):
            O_full[b_, :, hg * 512:(hg + 1) * 512] = \
                rf[b_ * 2 + hg]["O"].astype(np.float32).reshape(S, 512)

    pg1 = _prog("G1", lambda: build_proj_res_ln("G1", NCH, False))
    wo_ch = wchunks(np.asarray(L0_attn_W_o, np.float32))
    ins = []
    for b_ in range(B):
        for hf in range(2):
            sl = slice(hf * TOK, (hf + 1) * TOK)
            ins.append({"x": fm(x[b_, sl]), "act": fm(O_full[b_, sl]), "w": wo_ch})
    r1 = _run(pg1, ins)
    x1 = _assemble_tok(r1, "xo")
    h = _assemble_tok(r1, "z")

    pa = _prog("layerA", lambda: build_layer("layerA", 512, 8))
    for l in range(2):
        wd = layer_weights(512, 8, np.asarray(L0_ffn_wqkv[l], np.float32),
                           np.asarray(L0_ffn_q_scale[l], np.float32),
                           np.asarray(L0_ffn_k_scale[l], np.float32),
                           np.asarray(L0_ffn_wo[l], np.float32),
                           np.asarray(L0_ffn_ln_g[l], np.float32),
                           np.asarray(L0_ffn_ln_b[l], np.float32),
                           np.asarray(L0_ffn_w1[l], np.float32),
                           np.asarray(L0_ffn_b1[l], np.float32),
                           np.asarray(L0_ffn_w2[l], np.float32),
                           np.asarray(L0_ffn_b2[l], np.float32))
        ins = []
        for b_ in range(B):
            for hf in range(2):
                d_ = dict(wd)
                d_.update(layer_core_inputs(512, h[b_], hf * TOK, hf == 0))
                ins.append(d_)
        rl = _run(pa, ins)
        h = _assemble_tok(rl, "hout")

    pgs = _prog("G2S3", build_g2sg3)
    win_t = np.ascontiguousarray(np.asarray(L1_attn_in_proj_w, np.float32).T)
    cw = np.asarray(L1_attn_conv_w, np.float32)[:, 0, :]
    convw_col = np.zeros((128, 64), np.float32)
    for c in range(16):
        for tau in range(4):
            convw_col[:, c * 4 + tau] = cw[c * 128:(c + 1) * 128, tau]
    wd2 = {"win": wchunks(win_t), "convw": convw_col,
           "convb": np.ascontiguousarray(np.asarray(L1_attn_conv_b, np.float32).reshape(16, 128).T),
           "xp": wchunks(np.ascontiguousarray(np.asarray(L1_attn_x_proj_w, np.float32).T)),
           "dtw": np.ascontiguousarray(np.asarray(L1_attn_dt_proj_w, np.float32).T),
           "dtb": np.ascontiguousarray(np.asarray(L1_attn_dt_proj_b, np.float32).reshape(16, 128).T)}
    wd2.update(scan_host_weights(L1_attn_A_log, L1_attn_Dp))
    wd2["wout"] = wchunks(np.ascontiguousarray(np.asarray(L1_attn_out_proj_w, np.float32).T))
    rng = np.random.default_rng()
    mg1 = rng.integers(1 << 20, 1 << 30, size=1).astype(np.int32)
    ins = []
    for b_ in range(B):
        for hf in range(2):
            lo = hf * TOK - 3
            x1h = np.zeros((TOK + 3, D), np.float32)
            h2h = np.zeros((TOK + 3, D), np.float32)
            x1h[max(0, -lo):] = x1[b_, max(lo, 0): lo + TOK + 3]
            h2h[max(0, -lo):] = h[b_, max(lo, 0): lo + TOK + 3]
            d_ = dict(wd2); d_["x1h"] = fm(x1h); d_["h2h"] = fm(h2h)
            d_.update(xchg_host_inputs(hf, 1, rng, mg1))
            ins.append(d_)
    r3 = _run(pgs, ins)
    x3 = _assemble_tok(r3, "xo")
    h = _assemble_tok(r3, "z")

    pb = _prog("layerB", lambda: build_layer("layerB", 256, 16))
    for l in range(6):
        wd = layer_weights(256, 16, np.asarray(L1_ffn_Wqkv[l], np.float32),
                           np.asarray(L1_ffn_q_scale[l], np.float32),
                           np.asarray(L1_ffn_k_scale[l], np.float32),
                           np.asarray(L1_ffn_Wo[l], np.float32),
                           np.asarray(L1_ffn_ln_g[l], np.float32),
                           np.asarray(L1_ffn_ln_b[l], np.float32),
                           np.asarray(L1_ffn_W1[l], np.float32),
                           np.asarray(L1_ffn_b1[l], np.float32),
                           np.asarray(L1_ffn_W2[l], np.float32),
                           np.asarray(L1_ffn_b2[l], np.float32))
        ins = []
        for b_ in range(B):
            for hf in range(2):
                d_ = dict(wd)
                d_.update(layer_core_inputs(256, h[b_], hf * TOK, hf == 0))
                ins.append(d_)
        rl = _run(pb, ins)
        h = _assemble_tok(rl, "hout")

    pg4 = _prog("G4", build_add)
    ins = []
    for b_ in range(B):
        for hf in range(2):
            sl = slice(hf * TOK, (hf + 1) * TOK)
            ins.append({"a": fm(x3[b_, sl]), "b": fm(h[b_, sl])})
    r5 = _run(pg4, ins)
    out = _assemble_tok(r5, "o")
    return out.astype(np.float32)
